# version 01
# baseline (speedup 1.0000x reference)
import numpy as np
import ml_dtypes
import concourse.bass as bass
import concourse.mybir as mybir
from concourse.alu_op_type import AluOpType
from concourse.bass_utils import run_bass_kernel_spmd

F32 = mybir.dt.float32
BF16 = mybir.dt.bfloat16
F16 = mybir.dt.float16
U8 = mybir.dt.uint8
AF = mybir.ActivationFunctionType
AX = mybir.AxisListType
OP = AluOpType
BF = ml_dtypes.bfloat16

NCORES = 8
T = 2048
D = 1024
CC = D // 128
TC = T // 512
NT = T // 128
EPS_LN = 1e-5
NROT = 4


class Bld:

    def __init__(self, nc):
        self.nc = nc
        self.sems = {}
        self.cnt = {}
        self.rot = {"S": -1, "G": -1, "C": -1}
        self.cur = {}
        self.scr_off = 0
        self.scr_stack = []
        self.scratch_addr = None
        self.scratch_bytes = 0
        self.uid = 0

    def sem(self, name, handle):
        self.sems[name] = handle
        self.cnt[name] = 0

    def inc(self, inst, name, k=1):
        inst.then_inc(self.sems[name], k)
        self.cnt[name] += k
        return inst

    def wait(self, eng, name, tgt=None):
        eng.wait_ge(self.sems[name], self.cnt[name] if tgt is None else tgt)

    def gbegin(self, q, eng=None):
        self.rot[q] = (self.rot[q] + 1) % NROT
        s = f"d{q}{self.rot[q]}"
        self.cur[q] = s
        if eng is not None and self.cnt[s] > 0:
            eng.wait_ge(self.sems[s], self.cnt[s])
        return s

    def dinc(self, inst, q):
        s = self.cur[q]
        inst.then_inc(self.sems[s], 16)
        self.cnt[s] += 16
        return inst

    def gend(self, q):
        s = self.cur[q]
        return (s, self.cnt[s])

    def dma(self, eng, out, in_, q):
        self.gbegin(q, eng)
        self.dinc(eng.dma_start(out=out, in_=in_), q)
        return self.gend(q)

    def wtok(self, eng, tok):
        eng.wait_ge(self.sems[tok[0]], tok[1])

    def wait_allq(self, eng, q):
        for i in range(NROT):
            s = f"d{q}{i}"
            eng.wait_ge(self.sems[s], self.cnt[s])

    def push(self):
        self.scr_stack.append(self.scr_off)

    def pop(self):
        self.scr_off = self.scr_stack.pop()

    def scr(self, name, shape, dtype, align=32):
        per_part = int(np.prod(shape[1:])) * mybir.dt.size(dtype)
        off = (self.scr_off + align - 1) // align * align
        assert off + per_part <= self.scratch_bytes, (
            f"scratch overflow: {name} needs {per_part} at {off}, cap {self.scratch_bytes}")
        self.scr_off = off + per_part
        self.uid += 1
        self.offs = getattr(self, "offs", {})
        self.offs[name] = (off, per_part)
        return self.nc.alloc_sbuf_tensor_at(
            f"scr_{name}_{self.uid}", list(shape), dtype,
            offset=self.scratch_addr + off)

    def scr_at(self, name, shape, dtype, at):
        per_part = int(np.prod(shape[1:])) * mybir.dt.size(dtype)
        off, sz = self.offs[at]
        assert per_part <= sz, (name, per_part, at, sz)
        self.uid += 1
        return self.nc.alloc_sbuf_tensor_at(
            f"scr_{name}_{self.uid}", list(shape), dtype,
            offset=self.scratch_addr + off)


def _prep_inputs(I):
    C = np.ascontiguousarray
    f32 = np.float32
    shared = {
        "w_qkv_t": C(np.asarray(I["L0_attn_W_qkv"]).T),
        "w_out_t": C(np.asarray(I["L0_attn_W_out"]).T),
        "w_gate": C(np.asarray(I["L0_ffn_w_gate"])),
        "w_up": C(np.asarray(I["L0_ffn_w_up"])),
        "w_down": C(np.asarray(I["L0_ffn_w_down"])),
        "clip_x": C(np.reshape(np.asarray(I["L0_ffn_clip_x"]), (1, 1))),
        "clip_down": C(np.reshape(np.asarray(I["L0_ffn_clip_down"]), (1, 1))),
        "wq_t": C(np.asarray(I["L1_attn_Wq"]).T), "wk_t": C(np.asarray(I["L1_attn_Wk"]).T),
        "wv_t": C(np.asarray(I["L1_attn_Wv"]).T), "wo_t": C(np.asarray(I["L1_attn_Wo"]).T),
        "bq": C(np.reshape(np.asarray(I["L1_attn_bq"]), (CC, 128)).T),
        "bk": C(np.reshape(np.asarray(I["L1_attn_bk"]), (CC, 128)).T),
        "bv": C(np.reshape(np.asarray(I["L1_attn_bv"]), (CC, 128)).T),
        "bo": C(np.reshape(np.asarray(I["L1_attn_bo"]), (CC, 128)).T),
        "ln_g": C(np.reshape(np.asarray(I["L1_attn_ln_gamma"]), (CC, 128)).T),
        "ln_b": C(np.reshape(np.asarray(I["L1_attn_ln_beta"]), (CC, 128)).T),
        "w_q1": C(np.asarray(I["L1_ffn_w_q"])),
        "w_k1": C(np.asarray(I["L1_ffn_w_k"])),
        "w_g1": C(np.asarray(I["L1_ffn_w_gate"])),
        "ve_r": C(np.transpose(np.asarray(I["L1_ffn_value_experts"]),
                               (0, 2, 1, 3)).reshape(8, 1024, 512)),
        "oe_flat": C(np.reshape(np.asarray(I["L1_ffn_output_experts"]), (4096, 1024))),
        "ident": np.eye(128, dtype=f32),
    }
    tri = np.zeros((128, 4 * 512), dtype=f32)
    jj = np.arange(128)[:, None]
    qq = np.arange(512)[None, :]
    for bb in range(4):
        tri[:, 512 * bb:512 * (bb + 1)] = (jj + 128 * bb <= qq)
    shared["tri"] = tri.astype(BF)
    x = np.asarray(I["L0_attn_x"])
    in_maps = []
    for i in range(NCORES):
        m = dict(shared)
        m["x_t"] = C(x[i].T)
        in_maps.append(m)
    return in_maps


def build_program(stop_after=None, taps=()):
    nc = bass.Bass("TRN2", target_bir_lowering=False, debug=False)

    din = {}
    for name, shape in [
        ("x_t", [D, T]), ("w_qkv_t", [D, 3 * D]), ("w_out_t", [D, D]),
        ("w_gate", [2816, D]), ("w_up", [2816, D]), ("w_down", [D, 2816]),
        ("clip_x", [1, 1]), ("clip_down", [1, 1]),
        ("wq_t", [D, D]), ("wk_t", [D, D]), ("wv_t", [D, D]), ("wo_t", [D, D]),
        ("bq", [128, CC]), ("bk", [128, CC]), ("bv", [128, CC]), ("bo", [128, CC]),
        ("ln_g", [128, CC]), ("ln_b", [128, CC]),
        ("w_q1", [D, 512]), ("w_k1", [D, 512]), ("w_g1", [D, 64]),
        ("ve_r", [8, D, 512]), ("oe_flat", [4096, D]), ("ident", [128, 128]),
    ]:
        din[name] = nc.dram_tensor(name, shape, F32, kind="ExternalInput")
    din["tri"] = nc.dram_tensor("tri", [128, 4 * 512], BF16, kind="ExternalInput")
    dout = nc.dram_tensor("out_t", [D, T], F32, kind="ExternalOutput")

    dsc = {
        "qwg": nc.dram_tensor("qwg", [2816, D], BF16),
        "qwu": nc.dram_tensor("qwu", [2816, D], BF16),
        "qwd": nc.dram_tensor("qwd", [D, 2816], BF16),
        "hsp": nc.dram_tensor("hsp", [2816, T], F32),
        "vec": nc.dram_tensor("vec", [128, 16], F32),
        "vec1": nc.dram_tensor("vec1", [128, 1], F32),
        "rowb": nc.dram_tensor("rowb", [64, T], BF16),
        "ccin": nc.dram_tensor("ccin", [1, 1], F32),
        "statsd": nc.dram_tensor("statsd", [128, 32 * 512], F32),
        "minvd": nc.dram_tensor("minvd", [16, T], F32),
        "ved": nc.dram_tensor("ved", [8, D, 512], BF16),
        "oed": nc.dram_tensor("oed", [4096, D], BF16),
        "lst": nc.dram_tensor("lst", [128, 32 * 512], F32),
        "ld16": nc.dram_tensor("ld16", [16, T], F32),
        "ld8": nc.dram_tensor("ld8", [8, T], F32),
        "minvd1": nc.dram_tensor("minvd1", [8, T], F32),
        "sd": nc.dram_tensor("sd", [64, T], BF16),
        "ccout": nc.dram_tensor("ccout", [NCORES, 1], F32),
    }
    tap_outs = {}
    for tname, tshape in taps:
        tap_outs[tname] = nc.dram_tensor("tap_" + tname, tshape, F32, kind="ExternalOutput")

    from contextlib import ExitStack
    ctx = ExitStack()
    with ctx:
        X = ctx.enter_context(nc.sbuf_tensor("X", [128, CC * T], F32))
        ident = ctx.enter_context(nc.sbuf_tensor("identity_sb", [128, 128], F32))
        tri = ctx.enter_context(nc.sbuf_tensor("tri_sb", [128, 4 * 512], BF16))
        onecol_bf = ctx.enter_context(nc.sbuf_tensor("onecol_bf", [128, 1], BF16))
        consts = ctx.enter_context(nc.sbuf_tensor("consts", [128, 8], F32))
        biases = ctx.enter_context(nc.sbuf_tensor("biases", [128, 6 * CC], F32))
        stat = ctx.enter_context(nc.sbuf_tensor("stat", [128, 256], F32))
        ones_bf = ctx.enter_context(nc.sbuf_tensor("ones_bf", [128, 128], BF16))
        wsc = ctx.enter_context(nc.sbuf_tensor("wsc", [128, 52], F32))
        stat2col = ctx.enter_context(nc.sbuf_tensor("stat2col", [128, 2], F32))
        clip_sb = ctx.enter_context(nc.sbuf_tensor("clip_sb", [128, 2], F32))
        SCR_BYTES = 137 * 1024
        scratch = ctx.enter_context(nc.sbuf_tensor("scratch", [128, SCR_BYTES], U8))
        PBt = ctx.enter_context(nc.psum_tensor("PBt", [128, 4096], F32))
        pb = [PBt[:, 512 * i:512 * (i + 1)] for i in range(8)]

        sem_names = (["sT", "sV", "sS", "cc"]
                     + [f"d{q}{i}" for q in "SGC" for i in range(NROT)])
        sems = {n: ctx.enter_context(nc.semaphore(n)) for n in sem_names}

        block = ctx.enter_context(nc.Block())
        b = Bld(nc)
        b.scratch_bytes = SCR_BYTES
        b.scratch_addr = nc.lookup_mloc(scratch).addr
        for n in sem_names:
            b.sem(n, sems[n])

        st = _State(nc, block, b, din, dout, dsc, X, ident, tri, onecol_bf,
                    consts, biases, stat, ones_bf, pb, tap_outs)
        st.PBt = PBt
        st.wsc, st.stat2col, st.clip_sb = wsc, stat2col, clip_sb

        phases = [
            ("init", st.ph_init),
            ("wquant", st.ph_wquant),
            ("l1stash", st.ph_l1stash),
            ("l0f1", lambda: st.ph_l0ffn(0)),
            ("l0att", st.ph_l0att),
            ("l0f2", lambda: st.ph_l0ffn(1)),
            ("l1f1", lambda: st.ph_l1ffn(0)),
            ("l1att", st.ph_l1att),
            ("l1f2", lambda: st.ph_l1ffn(1)),
            ("store", st.ph_store),
        ]
        for name, fn in phases:
            fn()
            if stop_after is not None and name == stop_after:
                break
        st.ph_store_taps()
    return nc


class _State:
    def __init__(self, nc, block, b, din, dout, dsc, X, ident, tri, onecol_bf,
                 consts, biases, stat, ones_bf, pb, tap_outs):
        self.nc, self.block, self.b = nc, block, b
        self.din, self.dout, self.dsc = din, dout, dsc
        self.X, self.ident, self.tri = X, ident, tri
        self.onecol_bf, self.consts = onecol_bf, consts
        self.biases, self.stat, self.ones_bf, self.pb = biases, stat, ones_bf, pb
        self.taps = tap_outs
        self.stored = False

    def tap_now(self, name, ap, wait=("sV", "sS", "sT")):
        if name not in self.taps:
            return
        b, block = self.b, self.block
        dram = self.taps[name]

        @block.gpsimd
        def _(g):
            for w in wait:
                b.wait(g, w)
            b.wait_allq(g, "G")
            tok = b.dma(g, dram.ap(), ap, "G")
            b.wtok(g, tok)

    def maybe_tap(self, name, ap):
        if name not in self.taps:
            return
        b, block = self.b, self.block
        dram = self.taps[name]

        @block.gpsimd
        def _(g):
            b.wait(g, "sV"); b.wait(g, "sS"); b.wait(g, "sT")
            tok = b.dma(g, dram.ap(), ap, "G")
            b.wtok(g, tok)

    def emit_xbf_copies(self, xbf, x2bf):
        b, block = self.b, self.block

        @block.vector
        def _(v):
            b.wait_allq(v, "S")
            b.wait_allq(v, "G")
            b.wait(v, "sT")
            v.drain()
            b.inc(v.tensor_copy(out=xbf[:], in_=self.X[:]), "sV", 1)

        @block.scalar
        def _(s):
            b.wait_allq(s, "S")
            b.wait_allq(s, "G")
            b.wait(s, "sV")
            b.wait(s, "sT")
            b.inc(s.activation(out=x2bf[:], in_=self.X[:], func=AF.Square,
                               bias=self.consts[:, 0:1], scale=1.0), "sS", 1)

    def emit_ln_stats(self, xbf, x2bf, rb, cb):
        b, block, pb = self.b, self.block, self.pb
        D_inv = 1.0 / D
        epscol = self.consts[:, 2:3]
        vc0 = b.cnt["sV"]
        sc0 = b.cnt["sS"]

        @block.tensor
        def _(t):
            t.wait_ge(b.sems["sV"], vc0)
            t.wait_ge(b.sems["sS"], sc0)
            for tc in range(TC):
                if tc >= 2:
                    t.wait_ge(b.sems["sV"], vc0 + 2 * (tc - 2) + 1)
                for cc in range(CC):
                    csl = slice(T * cc + 512 * tc, T * cc + 512 * tc + 512)
                    t.matmul(pb[2 * (tc % 2)][:, :], self.ones_bf[:], xbf[:, csl],
                             start=(cc == 0), stop=(cc == CC - 1))
                    inst = t.matmul(pb[2 * (tc % 2) + 1][:, :], self.ones_bf[:],
                                    x2bf[:, csl], start=(cc == 0), stop=(cc == CC - 1))
                b.inc(inst, "sT", 1)
        tmm = b.cnt["sT"]

        for tc in range(TC):
            sl = slice(512 * tc, 512 * (tc + 1))
            p_sx = pb[2 * (tc % 2)]
            p_sxx = pb[2 * (tc % 2) + 1]
            p_tmp = pb[4 + (tc % 2)]

            @block.vector
            def _(v, tc=tc, sl=sl, p_sx=p_sx, p_sxx=p_sxx):
                v.wait_ge(b.sems["sT"], tmm - TC + tc + 1)
                v.drain()
                v.tensor_scalar(out=cb[:, sl], in0=p_sx[:, :], scalar1=D_inv,
                                scalar2=None, op0=OP.mult)
                v.drain()
                v.tensor_tensor(out=rb[:, sl], in0=cb[:, sl], in1=cb[:, sl], op=OP.mult)
                v.drain()
                b.inc(v.scalar_tensor_tensor(out=rb[:, sl], in0=p_sxx[:, :],
                                             scalar=D_inv, in1=rb[:, sl],
                                             op0=OP.mult, op1=OP.subtract), "sV", 1)

            @block.scalar
            def _(s, tc=tc, sl=sl, p_tmp=p_tmp):
                s.wait_ge(b.sems["sV"], vc0 + 2 * tc + 1)
                if tc >= 2:
                    s.wait_ge(b.sems["sV"], vc0 + 2 * (tc - 2) + 2)
                b.inc(s.activation(out=p_tmp[:, :], in_=rb[:, sl], func=AF.Sqrt,
                                   bias=epscol, scale=1.0), "sS", 1)

            @block.vector
            def _(v, tc=tc, sl=sl, p_tmp=p_tmp):
                v.wait_ge(b.sems["sS"], sc0 + tc + 1)
                v.drain()
                v.reciprocal(out=rb[:, sl], in_=p_tmp[:, :])
                v.drain()
                b.inc(v.tensor_tensor(out=cb[:, sl], in0=cb[:, sl], in1=rb[:, sl],
                                      op=OP.mult), "sV", 1)

    def emit_flat_allreduce_max(self, col_ap, st):
        b, block, dsc = self.b, self.block, self.dsc
        vc = b.cnt["sV"]

        @block.gpsimd
        def _(g):
            g.wait_ge(b.sems["sV"], vc)
            tok = b.dma(g, dsc["vec1"].ap(), col_ap, "G")
            b.wtok(g, tok)
            tok = b.dma(g, st[0:1, 65:193],
                        dsc["vec1"].ap().rearrange("a b -> b a"), "G")
            b.wtok(g, tok)
        g1 = b.cnt["dG" + str(b.rot["G"])], b.rot["G"]

        @block.vector
        def _(v):
            b.wait_allq(v, "G")
            v.drain()
            b.inc(v.reduce_max(out=st[0:1, 64:65], in_=st[0:1, 65:193],
                               axis=AX.X), "sV", 1)
        vc2 = b.cnt["sV"]

        @block.gpsimd
        def _(g):
            g.wait_ge(b.sems["sV"], vc2)
            tok = b.dma(g, dsc["ccin"].ap(), st[0:1, 64:65], "G")
            b.wtok(g, tok)
            b.inc(g.collective_compute(
                "AllGather", OP.bypass, replica_groups=[list(range(NCORES))],
                ins=[dsc["ccin"].ap()], outs=[dsc["ccout"].ap()]), "cc", 1)
            b.wait(g, "cc")
            tok = b.dma(g, st[0:1, 65:65 + NCORES],
                        dsc["ccout"].ap().rearrange("a b -> b a"), "G")
            b.wtok(g, tok)

        @block.vector
        def _(v):
            b.wait_allq(v, "G")
            v.drain()
            b.inc(v.reduce_max(out=st[0:1, 64:65], in_=st[0:1, 65:65 + NCORES],
                               axis=AX.X), "sV", 1)

    def emit_act_quant_scalars(self, gmax_cell, clip_cell, inv_col, fac_col, fac_scale):
        b, block = self.b, self.block
        st = self.stat

        @block.scalar
        def _(sc):
            b.wait(sc, "sV")
            b.inc(sc.activation(out=st[0:1, 70:71], in_=clip_cell, func=AF.Sigmoid,
                                bias=self.consts[0:1, 0:1], scale=1.0), "sS", 1)

        @block.vector
        def _(v):
            b.wait(v, "sS")
            v.drain()
            v.tensor_tensor(out=st[0:1, 71:72], in0=gmax_cell, in1=st[0:1, 70:71],
                            op=OP.mult)
            v.drain()
            v.tensor_scalar(out=st[0:1, 72:73], in0=st[0:1, 71:72], scalar1=1.0 / 127.0,
                            scalar2=None, op0=OP.mult)
            v.drain()
            v.reciprocal(out=st[0:1, 73:74], in_=st[0:1, 72:73])
            v.drain()
            b.inc(v.tensor_scalar(out=st[0:1, 74:75], in0=st[0:1, 72:73],
                                  scalar1=fac_scale, scalar2=None, op0=OP.mult), "sV", 1)
        vc = b.cnt["sV"]

        @block.gpsimd
        def _(g):
            g.wait_ge(b.sems["sV"], vc)
            tok = b.dma(g, self.dsc["vec"].ap()[0:1, 0:2], st[0:1, 73:75], "G")
            b.wtok(g, tok)
            tok = b.dma(g, self.stat2col[:, 0:2],
                        self.dsc["vec"].ap()[0:1, 0:2].partition_broadcast(128), "G")
            b.wtok(g, tok)

        @block.vector
        def _(v):
            b.wait_allq(v, "G")
            v.drain()
            v.tensor_copy(out=inv_col, in_=self.stat2col[:, 0:1])
            v.drain()
            b.inc(v.tensor_copy(out=fac_col, in_=self.stat2col[:, 1:2]), "sV", 1)

    def ph_init(self):
        b, block, din = self.b, self.block, self.din

        @block.sync
        def _(q):
            b.gbegin("S", q)
            b.dinc(q.dma_start(out=self.X[:].rearrange("p (c t) -> p c t", c=CC),
                               in_=din["x_t"].ap().rearrange("(c p) t -> p c t", p=128)), "S")
            b.dinc(q.dma_start(out=self.ident[:], in_=din["ident"].ap()), "S")
            b.dinc(q.dma_start(out=self.tri[:], in_=din["tri"].ap()), "S")
            for i, nm in enumerate(["bq", "bk", "bv", "bo", "ln_g", "ln_b"]):
                b.dinc(q.dma_start(out=self.biases[:, i * CC:(i + 1) * CC],
                                   in_=din[nm].ap()), "S")
            b.dinc(q.dma_start(out=self.clip_sb[0:1, 0:1], in_=din["clip_x"].ap()), "S")
            b.dinc(q.dma_start(out=self.clip_sb[0:1, 1:2], in_=din["clip_down"].ap()), "S")

        @block.vector
        def _(v):
            v.memset(self.onecol_bf[:], 1.0)
            v.memset(self.ones_bf[:], 1.0)
            v.memset(self.consts[:], 0.0)
            v.drain()
            v.memset(self.consts[:, 1:2], 1.0)
            v.drain()
            v.memset(self.consts[:, 2:3], EPS_LN)
            v.drain()
            v.memset(self.consts[:, 3:4], 128.5)
            v.drain()
            v.memset(self.consts[:, 4:5], -10.0)
            v.drain()
            b.inc(v.memset(self.consts[:, 5:6], -50.0), "sV", 1)

    def ph_wquant(self):
        b, block, din, dsc = self.b, self.block, self.din, self.dsc
        b.push()
        jobs = [("w_gate", "qwg", 0, 22, 1024), ("w_up", "qwu", 22, 44, 1024),
                ("w_down", "qwd", 44, 52, 2816)]
        wt = [b.scr(f"wt{i}", [128, 2816], F32) for i in range(2)]
        u8t = b.scr("wu8", [128, 2816], U8)
        qbt = [b.scr(f"wqb{i}", [128, 2816], BF16) for i in range(2)]
        ld_t, v_t, st_t = {}, {}, {}
        it = 0
        for (src, dst, c0, c1, W) in jobs:
            for r in range(c1 - c0):
                i = it; it += 1
                bi = i % 2

                @block.sync
                def _(q, src=src, r=r, W=W, bi=bi, i=i):
                    if i >= 2:
                        q.wait_ge(b.sems["sV"], v_t[i - 2])
                    ld_t[i] = b.dma(q, wt[bi][:, 0:W],
                                    din[src].ap()[128 * r:128 * (r + 1), :], "S")

                @block.vector
                def _(v, c0=c0, r=r, W=W, bi=bi, i=i):
                    b.wtok(v, ld_t[i])
                    if i >= 2:
                        b.wtok(v, st_t[i - 2])
                    v.drain()
                    v.reduce_max(out=self.wsc[:, c0 + r:c0 + r + 1], in_=wt[bi][:, 0:W],
                                 axis=AX.X, apply_absolute_value=True)
                    v.drain()
                    v.reciprocal(out=self.stat[:, 2:3], in_=self.wsc[:, c0 + r:c0 + r + 1])
                    v.drain()
                    v.tensor_scalar(out=self.stat[:, 3:4], in0=self.stat[:, 2:3],
                                    scalar1=127.0, scalar2=None, op0=OP.mult)
                    v.drain()
                    v.tensor_scalar(out=u8t[:, 0:W], in0=wt[bi][:, 0:W],
                                    scalar1=self.stat[:, 3:4], scalar2=128.0,
                                    op0=OP.mult, op1=OP.add)
                    v.drain()
                    b.inc(v.tensor_scalar(out=qbt[bi][:, 0:W], in0=u8t[:, 0:W],
                                          scalar1=128.0, scalar2=None, op0=OP.subtract),
                          "sV", 1)
                    v_t[i] = b.cnt["sV"]

                @block.gpsimd
                def _(g, dst=dst, r=r, W=W, bi=bi, i=i):
                    g.wait_ge(b.sems["sV"], v_t[i])
                    st_t[i] = b.dma(g, dsc[dst].ap()[128 * r:128 * (r + 1), :],
                                    qbt[bi][:, 0:W], "G")
        self.wq_store_done = [st_t[it - 1 - k] for k in range(min(NROT, it))]
        self.tap_now("qwg", dsc["qwg"].ap())
        b.pop()

    def ph_l0ffn(self, call):
        b, block, din, dsc, pb = self.b, self.block, self.din, self.dsc, self.pb
        X, st = self.X, self.stat
        b.push()
        rb = b.scr("rb", [128, T], F32)
        cb = b.scr("cb", [128, T], F32)
        b.push()
        xbf = b.scr("xbf", [128, CC * T], BF16)
        x2bf = b.scr("x2bf", [128, CC * T], BF16)
        self.emit_xbf_copies(xbf, x2bf)
        self.emit_ln_stats(xbf, x2bf, rb, cb)
        b.pop()
        xq = b.scr("xq", [128, CC * T], BF16)
        xtmp = b.scr("xtmp", [128, T], F32)
        u8t = b.scr("u8t", [128, T], U8)

        @block.vector
        def _(v):
            v.drain()
            v.memset(st[:, 0:1], 0.0)
            for cc in range(CC):
                v.drain()
                v.tensor_tensor(out=xtmp[:], in0=X[:, T * cc:T * (cc + 1)], in1=rb[:],
                                op=OP.mult)
                v.drain()
                v.tensor_tensor(out=xtmp[:], in0=xtmp[:], in1=cb[:], op=OP.subtract)
                v.drain()
                v.reduce_max(out=st[:, 1:2], in_=xtmp[:], axis=AX.X,
                             apply_absolute_value=True)
                v.drain()
                inst = v.tensor_tensor(out=st[:, 0:1], in0=st[:, 0:1], in1=st[:, 1:2],
                                       op=OP.max)
            b.inc(inst, "sV", 1)
        self.emit_flat_allreduce_max(st[:, 0:1], st)
        self.emit_act_quant_scalars(st[0:1, 64:65], self.clip_sb[0:1, 0:1],
                                    st[:, 5:6], st[:, 6:7], 1.0 / 127.0)

        @block.vector
        def _(v):
            b.wait(v, "sV")
            v.drain()
            v.tensor_scalar(out=st[:, 8:30], in0=self.wsc[:, 0:22],
                            scalar1=st[:, 6:7], scalar2=None, op0=OP.mult)
            v.drain()
            b.inc(v.tensor_scalar(out=st[:, 30:52], in0=self.wsc[:, 22:44],
                                  scalar1=st[:, 6:7], scalar2=None, op0=OP.mult), "sV", 1)

        @block.vector
        def _(v):
            for cc in range(CC):
                v.drain()
                v.tensor_tensor(out=xtmp[:], in0=X[:, T * cc:T * (cc + 1)], in1=rb[:],
                                op=OP.mult)
                v.drain()
                v.tensor_tensor(out=xtmp[:], in0=xtmp[:], in1=cb[:], op=OP.subtract)
                v.drain()
                v.tensor_scalar(out=u8t[:], in0=xtmp[:], scalar1=st[:, 5:6],
                                scalar2=128.0, op0=OP.mult, op1=OP.add)
                v.drain()
                inst = v.tensor_scalar(out=xq[:, T * cc:T * (cc + 1)], in0=u8t[:],
                                       scalar1=128.0, scalar2=None, op0=OP.subtract)
            b.inc(inst, "sV", 1)
        xq_ready = b.cnt["sV"]
        self.tap_now("xq", xq[:], wait=("sV",))
        self.tap_now("rb", rb[:], wait=("sV",))
        self.tap_now("cb", cb[:], wait=("sV",))
        self.tap_now("gmaxrow", st[0:1, 64:80], wait=("sV",))

        gblk = [b.scr(f"gblk{c}", [128, 1408], BF16) for c in range(CC)]
        ublk = [b.scr(f"ublk{c}", [128, 1408], BF16) for c in range(CC)]
        sg4 = b.scr("sg4", [128, 4 * 512], BF16)
        htl = [b.scr(f"htl{i}", [128, T], F32) for i in range(2)]
        ICG, ICN = 2, 11
        half_mm_t = {}
        half_ev_t = {}
        h_st_t = {}
        wld_t = {}

        @block.vector
        def _(v):
            v.drain()
            b.inc(v.memset(st[:, 2:3], 0.0), "sV", 1)

        for icg in range(ICG):
            @block.sync
            def _(q, icg=icg):
                if icg >= 1:
                    q.wait_ge(b.sems["sT"], half_mm_t[(icg - 1, ICN - 1, 1)])
                else:
                    b.wait(q, "sT")
                for tk in self.wq_store_done:
                    b.wtok(q, tk)
                r0 = icg * 1408
                b.gbegin("S", q)
                for cc in range(CC):
                    b.dinc(q.dma_start_transpose(
                        gblk[cc][:], dsc["qwg"].ap()[r0:r0 + 1408,
                                                     128 * cc:128 * (cc + 1)]), "S")
                    b.dinc(q.dma_start_transpose(
                        ublk[cc][:], dsc["qwu"].ap()[r0:r0 + 1408,
                                                     128 * cc:128 * (cc + 1)]), "S")
                wld_t[icg] = b.gend("S")

            for ic in range(ICN):
                icga = icg * ICN + ic
                for half in range(2):
                    bG = (pb[2 * half], pb[2 * half + 1])
                    bU = (pb[4 + 2 * half], pb[5 + 2 * half])

                    @block.tensor
                    def _(t, icg=icg, ic=ic, half=half, bG=bG, bU=bU, icga=icga):
                        b.wtok(t, wld_t[icg])
                        t.wait_ge(b.sems["sV"], xq_ready)
                        prev = ((icg, ic - 1, half) if ic > 0
                                else ((icg - 1, ICN - 1, half) if icg > 0 else None))
                        if prev is not None and prev in half_ev_t:
                            t.wait_ge(b.sems["sV"], half_ev_t[prev])
                        for wblk, banks in ((gblk, bG), (ublk, bU)):
                            for cc in range(CC):
                                lhsT = wblk[cc][:, 128 * ic:128 * (ic + 1)]
                                for j in range(2):
                                    tc = 2 * half + j
                                    inst = t.matmul(
                                        banks[j][:, :], lhsT,
                                        xq[:, T * cc + 512 * tc:T * cc + 512 * tc + 512],
                                        start=(cc == 0), stop=(cc == CC - 1))
                        b.inc(inst, "sT", 1)
                        half_mm_t[(icg, ic, half)] = b.cnt["sT"]

                    @block.scalar
                    def _(sc, icg=icg, ic=ic, half=half, bG=bG, icga=icga):
                        sc.wait_ge(b.sems["sT"], half_mm_t[(icg, ic, half)])
                        for j in range(2):
                            tc = 2 * half + j
                            inst = sc.activation(out=sg4[:, 512 * tc:512 * (tc + 1)],
                                                 in_=bG[j][:, :], func=AF.Silu,
                                                 scale=st[:, 8 + icga:9 + icga])
                        b.inc(inst, "sS", 1)

                    @block.vector
                    def _(v, icg=icg, ic=ic, half=half, bU=bU, icga=icga):
                        b.wait(v, "sS")
                        if icga >= 2 and half == 0:
                            b.wtok(v, h_st_t[icga - 2])
                        for j in range(2):
                            tc = 2 * half + j
                            v.drain()
                            v.scalar_tensor_tensor(
                                out=htl[icga % 2][:, 512 * tc:512 * (tc + 1)],
                                in0=bU[j][:, :], scalar=st[:, 30 + icga:31 + icga],
                                in1=sg4[:, 512 * tc:512 * (tc + 1)],
                                op0=OP.mult, op1=OP.mult)
                            v.drain()
                            v.reduce_max(out=st[:, 1:2],
                                         in_=htl[icga % 2][:, 512 * tc:512 * (tc + 1)],
                                         axis=AX.X, apply_absolute_value=True)
                            v.drain()
                            inst = v.tensor_tensor(out=st[:, 2:3], in0=st[:, 2:3],
                                                   in1=st[:, 1:2], op=OP.max)
                        b.inc(inst, "sV", 1)
                        half_ev_t[(icg, ic, half)] = b.cnt["sV"]

                @block.gpsimd
                def _(g, icg=icg, ic=ic, icga=icga):
                    g.wait_ge(b.sems["sV"], half_ev_t[(icg, ic, 1)])
                    h_st_t[icga] = b.dma(
                        g, dsc["hsp"].ap()[128 * icga:128 * (icga + 1), :],
                        htl[icga % 2][:], "G")

        @block.gpsimd
        def _(g):
            if "hrows" in self.taps:
                for tk in h_st_t.values():
                    b.wtok(g, tk)
        self.tap_now("hrows", dsc["hsp"].ap()[0:256, :], wait=())

        @block.vector
        def _(v):
            v.drain()
            b.inc(v.tensor_copy(out=st[:, 0:1], in_=st[:, 2:3]), "sV", 1)
        self.emit_flat_allreduce_max(st[:, 0:1], st)
        self.emit_act_quant_scalars(st[0:1, 64:65], self.clip_sb[0:1, 1:2],
                                    st[:, 5:6], st[:, 6:7], 0.5 / 127.0)

        @block.vector
        def _(v):
            b.wait(v, "sV")
            v.drain()
            b.inc(v.tensor_scalar(out=st[:, 8:16], in0=self.wsc[:, 44:52],
                                  scalar1=st[:, 6:7], scalar2=None, op0=OP.mult), "sV", 1)
        dsc_ready = b.cnt["sV"]
        b.pop()

        b.push()
        ICN2 = 11
        hq = b.scr("hq", [128, ICN2 * T], BF16)
        wdb = [b.scr(f"wdb{i}", [128, 1024], BF16) for i in range(ICN2)]
        ht2 = [b.scr(f"ht2{i}", [128, T], F32) for i in range(2)]
        u8b = b.scr("u8b", [128, T], U8)
        hq_q_t = {}
        hl_t = {}
        dn_mm_t = {}
        dn_ev_t = {}
        for rnd in range(2):
            for ic in range(ICN2):
                icga = rnd * ICN2 + ic

                @block.gpsimd
                def _(g, rnd=rnd, ic=ic, icga=icga):
                    b.wtok(g, h_st_t[icga])
                    if icga >= 2:
                        g.wait_ge(b.sems["sV"], hq_q_t[icga - 2])
                    if icga < 2:
                        g.wait_ge(b.sems["sT"], half_mm_t[(ICG - 1, ICN - 1, 1)])
                    if rnd == 1 and ic == 0:
                        g.wait_ge(b.sems["sT"], dn_mm_t[(0, 3)])
                    hl_t[icga] = b.dma(
                        g, ht2[icga % 2][:],
                        dsc["hsp"].ap()[128 * icga:128 * (icga + 1), :], "G")

                @block.vector
                def _(v, ic=ic, icga=icga):
                    b.wtok(v, hl_t[icga])
                    v.wait_ge(b.sems["sV"], dsc_ready)
                    v.drain()
                    v.tensor_scalar(out=u8b[:], in0=ht2[icga % 2][:], scalar1=st[:, 5:6],
                                    scalar2=128.0, op0=OP.mult, op1=OP.add)
                    v.drain()
                    b.inc(v.tensor_scalar(out=hq[:, T * ic:T * (ic + 1)], in0=u8b[:],
                                          scalar1=128.0, scalar2=None, op0=OP.subtract),
                          "sV", 1)
                    hq_q_t[icga] = b.cnt["sV"]

            @block.sync
            def _(q, rnd=rnd):
                if rnd == 1:
                    q.wait_ge(b.sems["sT"], dn_mm_t[(0, 3)])
                else:
                    q.wait_ge(b.sems["sT"], half_mm_t[(ICG - 1, ICN - 1, 1)])
                b.gbegin("S", q)
                for ic in range(ICN2):
                    icga = rnd * ICN2 + ic
                    b.dinc(q.dma_start_transpose(
                        wdb[ic][:, :],
                        dsc["qwd"].ap()[:, 128 * icga:128 * (icga + 1)]), "S")
            wdl = b.gend("S")

            for sw in range(4):
                hcg, tcg = sw // 2, sw % 2

                @block.tensor
                def _(t, rnd=rnd, sw=sw, hcg=hcg, tcg=tcg, wdl=wdl):
                    b.wtok(t, wdl)
                    t.wait_ge(b.sems["sV"], hq_q_t[rnd * ICN2 + ICN2 - 1])
                    prev = (rnd, sw - 1) if sw > 0 else ((rnd - 1, 3) if rnd > 0 else None)
                    if prev is not None:
                        t.wait_ge(b.sems["sV"], dn_ev_t[prev])
                    for ic in range(ICN2):
                        for hj in range(4):
                            hc = 4 * hcg + hj
                            for tj in range(2):
                                tc = 2 * tcg + tj
                                inst = t.matmul(
                                    pb[2 * hj + tj][:, :],
                                    wdb[ic][:, 128 * hc:128 * (hc + 1)],
                                    hq[:, T * ic + 512 * tc:T * ic + 512 * tc + 512],
                                    start=(ic == 0), stop=(ic == ICN2 - 1))
                    b.inc(inst, "sT", 1)
                    dn_mm_t[(rnd, sw)] = b.cnt["sT"]

                @block.vector
                def _(v, rnd=rnd, sw=sw, hcg=hcg, tcg=tcg):
                    v.wait_ge(b.sems["sT"], dn_mm_t[(rnd, sw)])
                    for hj in range(4):
                        hc = 4 * hcg + hj
                        for tj in range(2):
                            tc = 2 * tcg + tj
                            v.drain()
                            inst = v.scalar_tensor_tensor(
                                out=X[:, T * hc + 512 * tc:T * hc + 512 * tc + 512],
                                in0=pb[2 * hj + tj][:, :], scalar=st[:, 8 + hc:9 + hc],
                                in1=X[:, T * hc + 512 * tc:T * hc + 512 * tc + 512],
                                op0=OP.mult, op1=OP.add)
                    b.inc(inst, "sV", 1)
                    dn_ev_t[(rnd, sw)] = b.cnt["sV"]
        b.pop()

    def emit_ln_bf16_apply(self, xbf, rb, cb, rbbf, cbbf):
        b, block = self.b, self.block

        @block.vector
        def _(v):
            b.wait(v, "sS")
            v.drain()
            v.tensor_copy(out=rbbf[:], in_=rb[:])
            v.drain()
            v.tensor_copy(out=cbbf[:], in_=cb[:])
            v.drain()
            for cc in range(CC):
                sl = slice(T * cc, T * (cc + 1))
                v.tensor_tensor(out=xbf[:, sl], in0=xbf[:, sl], in1=rbbf[:], op=OP.mult)
                v.drain()
                inst = v.tensor_tensor(out=xbf[:, sl], in0=xbf[:, sl], in1=cbbf[:],
                                       op=OP.subtract)
                v.drain()
            b.inc(inst, "sV", 1)

    def ph_l0att(self):
        b, block, din, dsc, pb, PBt = self.b, self.block, self.din, self.dsc, self.pb, self.PBt
        X, st = self.X, self.stat
        NB = 32
        b.push()
        xbf = b.scr("xbf2", [128, CC * T], BF16)
        b.push()
        x2bf = b.scr("x2bf2", [128, CC * T], BF16)
        rb = b.scr("rb2", [128, T], F32)
        cb = b.scr("cb2", [128, T], F32)
        rbbf = b.scr("rbbf", [128, T], BF16)
        cbbf = b.scr("cbbf", [128, T], BF16)
        self.emit_xbf_copies(xbf, x2bf)
        self.emit_ln_stats(xbf, x2bf, rb, cb)
        self.emit_ln_bf16_apply(xbf, rb, cb, rbbf, cbbf)
        b.pop()
        xn = xbf
        xn_ready = b.cnt["sV"]

        acc = b.scr("acc", [128, 8 * 66], F32)
        accbf = b.scr("accbf", [128, 8 * 66], BF16)
        sst = b.scr("sst", [128, 512], F32)
        qw = [b.scr_at(f"qw{i}", [128, 128], BF16, "acc") if i < 8 else
              b.scr_at(f"qw{i}", [128, 128], BF16, "accbf") for i in range(0)]
        ktm = b.scr("ktm", [128, NT * 1024], BF16)
        vtm = b.scr("vtm", [128, NT * 1024], BF16)
        kv_done = {}

        b.push()
        kvw = [b.scr(f"kvw{c}", [128, 1024], BF16) for c in range(CC)]
        for which, dst in (("k", ktm), ("v", vtm)):
            col0 = 1024 if which == "k" else 2048
            wtoks = []

            @block.gpsimd
            def _(g, which=which, col0=col0, wtoks=wtoks):
                b.wait(g, "sT")
                if which == "k":
                    b.wait(g, "sV")
                for c in range(CC):
                    wtoks.append(b.dma(
                        g, kvw[c][:],
                        din["w_qkv_t"].ap()[128 * c:128 * (c + 1), col0:col0 + 1024], "G"))

            mm_t = {}
            ev_t = {}
            for tt in range(NT):
                bk0 = pb[4 + 2 * (tt % 2)]
                bk1 = pb[5 + 2 * (tt % 2)]

                @block.tensor
                def _(t, which=which, tt=tt, bk0=bk0, bk1=bk1, wtoks=wtoks):
                    if tt == 0:
                        t.wait_ge(b.sems["sV"], xn_ready)
                        for tk in wtoks:
                            b.wtok(t, tk)
                    if tt >= 2:
                        t.wait_ge(b.sems["sS"], ev_t[tt - 2])
                    for c in range(CC):
                        lhsT = xn[:, T * c + 128 * tt:T * c + 128 * (tt + 1)]
                        t.matmul(bk0[:, :], lhsT, kvw[c][:, 0:512],
                                 start=(c == 0), stop=(c == CC - 1))
                        inst = t.matmul(bk1[:, :], lhsT, kvw[c][:, 512:1024],
                                        start=(c == 0), stop=(c == CC - 1))
                    b.inc(inst, "sT", 1)
                    mm_t[tt] = b.cnt["sT"]

                @block.scalar
                def _(sc, which=which, tt=tt, bk0=bk0, bk1=bk1):
                    sc.wait_ge(b.sems["sT"], mm_t[tt])
                    if which == "k":
                        o = ktm[:, 1024 * tt:1024 * (tt + 1)]
                        sc.activation(out=o[:, 0:512], in_=bk0[:, :], func=AF.Exp,
                                      bias=self.consts[:, 0:1], scale=1.0)
                        inst = sc.activation(out=o[:, 512:1024], in_=bk1[:, :], func=AF.Exp,
                                             bias=self.consts[:, 0:1], scale=1.0)
                    else:
                        o = vtm[:, 1024 * tt:1024 * (tt + 1)]
                        sc.copy(out=o[:, 0:512], in_=bk0[:, :])
                        inst = sc.copy(out=o[:, 512:1024], in_=bk1[:, :])
                    b.inc(inst, "sS", 1)
                    ev_t[tt] = b.cnt["sS"]
            kv_done[which] = (mm_t[NT - 1], b.cnt["sS"])
        b.pop()

        b.push()
        expq = b.scr("expq", [128, CC * T], BF16)
        qwbase = self.nc.alloc_sbuf_tensor_at(
            "scr_qwall", [128, 16 * 128], BF16,
            offset=b.scratch_addr + b.offs["acc"][0])
        qw = [qwbase[:, 128 * i:128 * (i + 1)] for i in range(16)]
        q_mm_t = {}
        q_ev_t = {}
        qw_tok = {}
        for hp in range(CC):
            @block.gpsimd
            def _(g, hp=hp):
                if hp >= 2:
                    g.wait_ge(b.sems["sT"], q_mm_t[hp - 2])
                else:
                    b.wait(g, "sT")
                toks = []
                for c in range(CC):
                    toks.append(b.dma(
                        g, qw[8 * (hp % 2) + c][:],
                        din["w_qkv_t"].ap()[128 * c:128 * (c + 1),
                                            128 * hp:128 * (hp + 1)], "G"))
                qw_tok[hp] = toks

            bks = [pb[4 * (hp % 2) + j] for j in range(4)]

            @block.tensor
            def _(t, hp=hp, bks=bks):
                for tk in qw_tok[hp]:
                    b.wtok(t, tk)
                if hp >= 2:
                    t.wait_ge(b.sems["sS"], q_ev_t[hp - 2])
                for c in range(CC):
                    lhsT = qw[8 * (hp % 2) + c][:]
                    for tc in range(TC):
                        inst = t.matmul(bks[tc][:, :], lhsT,
                                        xn[:, T * c + 512 * tc:T * c + 512 * (tc + 1)],
                                        start=(c == 0), stop=(c == CC - 1))
                b.inc(inst, "sT", 1)
                q_mm_t[hp] = b.cnt["sT"]

            @block.scalar
            def _(sc, hp=hp, bks=bks):
                sc.wait_ge(b.sems["sT"], q_mm_t[hp])
                for tc in range(TC):
                    inst = sc.activation(out=expq[:, T * hp + 512 * tc:T * hp + 512 * (tc + 1)],
                                         in_=bks[tc][:, :], func=AF.Exp,
                                         bias=self.consts[:, 0:1], scale=1.0)
                b.inc(inst, "sS", 1)
                q_ev_t[hp] = b.cnt["sS"]

        attnstore = xn

        @block.vector
        def _(v):
            v.wait_ge(b.sems["sS"], q_ev_t[CC - 1])
            v.wait_ge(b.sems["sS"], kv_done["v"][1])
            v.drain()
            v.memset(acc[:], 0.0)
            v.drain()
            v.memset(accbf[:], 0.0)
            v.drain()
            v.memset(acc.rearrange("p (h z) -> p h z", h=8)[:, :, 65:66], 1.0)
            v.drain()
            b.inc(v.memset(accbf.rearrange("p (h z) -> p h z", h=8)[:, :, 65:66], 1.0),
                  "sV", 1)
        acc_ready = b.cnt["sV"]

        att_mm_t = {}
        ctx_mm_t = {}
        v1_t = {}
        v2_t = {}
        sst_dma_t = {}
        for n in range(NB):
            r = n % 2
            pctx = PBt[:, 1024 * r:1024 * r + 512]
            patt = PBt[:, 2048 + 1024 * r:2048 + 1024 * r + 512]
            psts = PBt[:, 2560 + 1024 * r:2560 + 1024 * r + 512]
            psum_k = PBt[:, 1024 * r + 512:1024 * r + 512 + 8]
            tt, half = n // 2, n % 2
            krows = slice(64 * half, 64 * half + 64)

            @block.tensor
            def _(t, n=n, pctx=pctx, patt=patt, psts=psts, psum_k=psum_k,
                  tt=tt, krows=krows):
                if n == 0:
                    t.wait_ge(b.sems["sV"], acc_ready)
                if n >= 1:
                    t.wait_ge(b.sems["sV"], v1_t[n - 1])
                if n >= 2:
                    t.wait_ge(b.sems["sV"], v2_t[n - 2])
                for h in range(16):
                    hp, par = h // 2, h % 2
                    qrows = slice(64 * par, 64 * par + 64)
                    rhs = expq[qrows, T * hp + 64 * n:T * hp + 64 * (n + 1)]
                    acol = accbf[qrows, 66 * hp:66 * hp + 66]
                    rbase = 32 * (h // 4)
                    cbase = 128 * (h % 4)
                    t.matmul(patt[slice(64 * par, 64 * par + 64),
                                  64 * hp:64 * (hp + 1)],
                             acol[:, 0:64], rhs, start=True, stop=True)
                    t.matmul(psts[rbase:rbase + 1, cbase:cbase + 64],
                             acol[:, 64:65], rhs, start=True, stop=True,
                             tile_position=(64 * par, rbase))
                    inst = t.matmul(psts[rbase:rbase + 1, cbase + 64:cbase + 128],
                                    acol[:, 65:66], rhs, start=True, stop=True,
                                    tile_position=(64 * par, rbase))
                b.inc(inst, "sT", 1)
                att_mm_t[n] = b.cnt["sT"]
                for h in range(16):
                    hp, par = h // 2, h % 2
                    kl = ktm[krows, 1024 * tt + 64 * h:1024 * tt + 64 * (h + 1)]
                    vl = vtm[krows, 1024 * tt + 64 * h:1024 * tt + 64 * (h + 1)]
                    ol = pctx[slice(64 * par, 64 * par + 64), 64 * hp:64 * (hp + 1)]
                    t.matmul(ol, kl, vl, start=True, stop=True)
                    inst = t.matmul(psum_k[slice(64 * par, 64 * par + 64), hp:hp + 1],
                                    kl, self.onecol_bf[krows, 0:1],
                                    start=True, stop=True)
                b.inc(inst, "sT", 1)
                ctx_mm_t[n] = b.cnt["sT"]

            @block.vector
            def _(v, n=n, pctx=pctx, psum_k=psum_k):
                v.wait_ge(b.sems["sT"], ctx_mm_t[n])
                v.drain()
                acc3 = acc.rearrange("p (h z) -> p h z", h=8)
                v.tensor_tensor(out=acc3[:, :, 0:64], in0=acc3[:, :, 0:64],
                                in1=pctx.rearrange("p (h z) -> p h z", h=8), op=OP.add)
                v.drain()
                v.tensor_tensor(out=acc3[:, :, 64:65], in0=acc3[:, :, 64:65],
                                in1=psum_k.rearrange("p (h z) -> p h z", h=8), op=OP.add)
                v.drain()
                inst = v.tensor_copy(
                    out=accbf.rearrange("p (h z) -> p h z", h=8)[:, :, 0:65],
                    in_=acc3[:, :, 0:65])
                v.drain()
                b.inc(inst, "sV", 1)
                v1_t[n] = b.cnt["sV"]

            @block.vector
            def _(v, n=n, patt=patt, psts=psts):
                if n >= 1:
                    b.wtok(v, sst_dma_t[n - 1])
                v.drain()
                a3 = attnstore.rearrange("p (hp tcol) -> p hp tcol", hp=8)
                v.tensor_copy(out=a3[:, :, 64 * n:64 * (n + 1)],
                              in_=patt.rearrange("p (hp z) -> p hp z", hp=8))
                v.drain()
                inst = v.tensor_copy(out=sst[:], in_=psts[:, :])
                v.drain()
                b.inc(inst, "sV", 1)
                v2_t[n] = b.cnt["sV"]

            @block.gpsimd
            def _(g, n=n):
                g.wait_ge(b.sems["sV"], v2_t[n])
                sst_dma_t[n] = b.dma(
                    g, dsc["statsd"].ap()[:, 512 * n:512 * (n + 1)], sst[:], "G")

        s2 = b.scr_at("s2", [128, 512], F32, "sst")
        minvt = b.scr_at("minvt", [128, 256], F32, "accbf")

        @block.gpsimd
        def _(g):
            for n in range(NB):
                b.wtok(g, sst_dma_t[n])
            src = dsc["statsd"].ap().rearrange("(rg s) c -> rg s c", rg=4)[:, 0:1, :] \
                .squeeze(1)
            src = src.rearrange("rg (n c) -> rg n c", n=NB)
            self._l0_sst_rd = b.dma(g, s2[:], src, "G")

        @block.vector
        def _(v):
            b.wtok(v, self._l0_sst_rd)
            v.drain()
            s3 = s2[:].rearrange("p (hs w t) -> p hs w t", hs=4, w=2)
            m3 = minvt[:].rearrange("p (hs t) -> p hs t", hs=4)
            v.scalar_tensor_tensor(out=m3, in0=s3[:, :, 1, :], scalar=8e-6,
                                   in1=s3[:, :, 0, :], op0=OP.mult, op1=OP.max)
            v.drain()
            b.inc(v.reciprocal(out=minvt[:], in_=minvt[:]), "sV", 1)
        minv_ready = b.cnt["sV"]

        @block.gpsimd
        def _(g):
            g.wait_ge(b.sems["sV"], minv_ready)
            toks = []
            for rg in range(4):
                dst = dsc["minvd"].ap()[4 * rg:4 * rg + 4, :] \
                    .rearrange("hs (n t) -> n hs t", n=NB)
                toks.append(b.dma(
                    g, dst,
                    minvt[32 * rg:32 * (rg + 1), :].rearrange(
                        "n (hs t) -> n hs t", hs=4), "G"))
            self._l0_minv_wr = toks

        b.pop()
        gsc = [b.scr(f"gsc{i}", [128, T], BF16) for i in range(2)]
        gsc_tok = {}
        scale_t = {}
        for hp in range(CC):
            @block.gpsimd
            def _(g, hp=hp):
                if hp == 0:
                    for tk in self._l0_minv_wr:
                        b.wtok(g, tk)
                    b.wait(g, "sT")
                if hp >= 2:
                    g.wait_ge(b.sems["sV"], scale_t[hp - 2])
                src = dsc["minvd"].ap()[2 * hp:2 * hp + 2, :]
                gsc_tok[hp] = b.dma(
                    g, gsc[hp % 2][:],
                    src.unsqueeze(1).broadcast_to([2, 64, T]), "G")

            @block.vector
            def _(v, hp=hp):
                b.wtok(v, gsc_tok[hp])
                v.drain()
                sl = slice(T * hp, T * (hp + 1))
                b.inc(v.tensor_tensor(out=attnstore[:, sl], in0=attnstore[:, sl],
                                      in1=gsc[hp % 2][:], op=OP.mult), "sV", 1)
                scale_t[hp] = b.cnt["sV"]

        wob = [b.scr(f"wob{i}", [128, 1024], BF16) for i in range(CC)]

        @block.gpsimd
        def _(g):
            b.wait(g, "sT")
            toks = []
            for c in range(CC):
                toks.append(b.dma(g, wob[c][:],
                                  din["w_out_t"].ap()[128 * c:128 * (c + 1), :], "G"))
            self._l0_wo_tok = toks

        wo_mm_t = {}
        wo_ev_t = {}
        for dc in range(CC):
            bks = [pb[4 * (dc % 2) + j] for j in range(4)]

            @block.tensor
            def _(t, dc=dc, bks=bks):
                if dc == 0:
                    for tk in self._l0_wo_tok:
                        b.wtok(t, tk)
                    t.wait_ge(b.sems["sV"], scale_t[CC - 1])
                if dc >= 2:
                    t.wait_ge(b.sems["sV"], wo_ev_t[dc - 2])
                for c in range(CC):
                    lhsT = wob[c][:, 128 * dc:128 * (dc + 1)]
                    for tc in range(TC):
                        inst = t.matmul(bks[tc][:, :], lhsT,
                                        attnstore[:, T * c + 512 * tc:T * c + 512 * (tc + 1)],
                                        start=(c == 0), stop=(c == CC - 1))
                b.inc(inst, "sT", 1)
                wo_mm_t[dc] = b.cnt["sT"]

            @block.vector
            def _(v, dc=dc, bks=bks):
                v.wait_ge(b.sems["sT"], wo_mm_t[dc])
                for tc in range(TC):
                    v.drain()
                    inst = v.scalar_tensor_tensor(
                        out=X[:, T * dc + 512 * tc:T * dc + 512 * (tc + 1)],
                        in0=bks[tc][:, :], scalar=1.0,
                        in1=X[:, T * dc + 512 * tc:T * dc + 512 * (tc + 1)],
                        op0=OP.mult, op1=OP.add)
                b.inc(inst, "sV", 1)
                wo_ev_t[dc] = b.cnt["sV"]
        b.pop()

    def ph_l1stash(self):
        b, block, din, dsc = self.b, self.block, self.din, self.dsc

        @block.gpsimd
        def _(g):
            toks = []
            for e in range(8):
                toks.append(b.dma(g, dsc["ved"].ap()[e], din["ve_r"].ap()[e], "G"))
            for j in range(8):
                toks.append(b.dma(g, dsc["oed"].ap()[512 * j:512 * (j + 1), :],
                                  din["oe_flat"].ap()[512 * j:512 * (j + 1), :], "G"))
            self.l1stash_tok = toks

    def ph_l1ffn(self, call):
        b, block, din, dsc, pb, PBt = self.b, self.block, self.din, self.dsc, self.pb, self.PBt
        X, st = self.X, self.stat
        b.push()
        xbf = b.scr("xbf3", [128, CC * T], BF16)
        b.push()
        x2bf = b.scr("x2bf3", [128, CC * T], BF16)
        rb = b.scr("rb3", [128, T], F32)
        cb = b.scr("cb3", [128, T], F32)
        rbbf = b.scr("rbbf3", [128, T], BF16)
        cbbf = b.scr("cbbf3", [128, T], BF16)
        self.emit_xbf_copies(xbf, x2bf)
        self.emit_ln_stats(xbf, x2bf, rb, cb)
        self.emit_ln_bf16_apply(xbf, rb, cb, rbbf, cbbf)
        b.pop()
        xn = xbf
        xn_ready = b.cnt["sV"]

        gates = b.scr("gates", [128, NT * 64], F32)
        maskbf = b.scr("maskbf", [128, NT * 64], U8)
        qkT = b.scr("qkT", [128, 8 * T], BF16)
        values = b.scr("valuesl1", [128, NT * 520], BF16)

        b.push()
        wg1 = b.scr("wg1", [128, 8 * 64], BF16)
        wq1 = b.scr("wq1", [128, 8 * 512], BF16)
        wk1 = b.scr("wk1", [128, 8 * 512], BF16)

        @block.gpsimd
        def _(g):
            b.wait(g, "sT")
            b.wait(g, "sV")
            toks = []
            for c in range(CC):
                toks.append(b.dma(g, wg1[:, 64 * c:64 * (c + 1)],
                                  din["w_g1"].ap()[128 * c:128 * (c + 1), :], "G"))
                toks.append(b.dma(g, wq1[:, 512 * c:512 * (c + 1)],
                                  din["w_q1"].ap()[128 * c:128 * (c + 1), :], "G"))
                toks.append(b.dma(g, wk1[:, 512 * c:512 * (c + 1)],
                                  din["w_k1"].ap()[128 * c:128 * (c + 1), :], "G"))
            self._l1_w_tok = toks

        g_mm_t = {}
        g_ev_t = {}
        for tt in range(NT):
            bkg = pb[6 + (tt % 2)]

            @block.tensor
            def _(t, tt=tt, bkg=bkg):
                if tt == 0:
                    t.wait_ge(b.sems["sV"], xn_ready)
                    for tk in self._l1_w_tok:
                        b.wtok(t, tk)
                if tt >= 2:
                    t.wait_ge(b.sems["sS"], g_ev_t[tt - 2])
                for c in range(CC):
                    inst = t.matmul(bkg[:, 0:64],
                                    xn[:, T * c + 128 * tt:T * c + 128 * (tt + 1)],
                                    wg1[:, 64 * c:64 * (c + 1)],
                                    start=(c == 0), stop=(c == CC - 1))
                b.inc(inst, "sT", 1)
                g_mm_t[tt] = b.cnt["sT"]

            @block.scalar
            def _(sc, tt=tt, bkg=bkg):
                sc.wait_ge(b.sems["sT"], g_mm_t[tt])
                b.inc(sc.activation(out=gates[:, 64 * tt:64 * (tt + 1)], in_=bkg[:, 0:64],
                                    func=AF.Sigmoid, bias=self.consts[:, 0:1],
                                    scale=1.0), "sS", 1)
                g_ev_t[tt] = b.cnt["sS"]
        gates_done = b.cnt["sS"]

        @block.vector
        def _(v):
            v.wait_ge(b.sems["sS"], gates_done)
            v.drain()
            g4 = gates[:].rearrange("p (tt h e) -> p tt h e", tt=NT, h=8)
            v.reduce_max(out=st[:, 16:16 + 128].rearrange("p (tt h) -> p tt h", tt=NT),
                         in_=g4, axis=AX.X)
            v.drain()
            mxb = st[:, 16:16 + 128].rearrange("p (tt h) -> p tt h", tt=NT) \
                .unsqueeze(3).broadcast_to([128, NT, 8, 8])
            b.inc(v.tensor_tensor(out=maskbf[:].rearrange("p (tt h e) -> p tt h e",
                                                          tt=NT, h=8),
                                  in0=g4, in1=mxb, op=OP.is_ge), "sV", 1)
        mask_ready = b.cnt["sV"]

        qk_mm_t = {}
        qk_ev_t = {}
        for i, (which, mc) in enumerate([(w, m) for w in ("q", "k") for m in range(4)]):
            wsb = wq1 if which == "q" else wk1
            qoff = (0 if which == "q" else 4 * T)
            bks = [pb[4 * (i % 2) + j] for j in range(4)]

            @block.tensor
            def _(t, which=which, mc=mc, i=i, wsb=wsb, bks=bks):
                if i < 2:
                    t.wait_ge(b.sems["sS"], gates_done)
                if i >= 2:
                    t.wait_ge(b.sems["sV"], qk_ev_t[i - 2])
                for c in range(CC):
                    lhsT = wsb[:, 512 * c + 128 * mc:512 * c + 128 * (mc + 1)]
                    for tc in range(TC):
                        inst = t.matmul(bks[tc][:, :], lhsT,
                                        xn[:, T * c + 512 * tc:T * c + 512 * (tc + 1)],
                                        start=(c == 0), stop=(c == CC - 1))
                b.inc(inst, "sT", 1)
                qk_mm_t[i] = b.cnt["sT"]

            @block.vector
            def _(v, which=which, mc=mc, i=i, qoff=qoff, bks=bks):
                v.wait_ge(b.sems["sT"], qk_mm_t[i])
                scale = 0.125 if which == "q" else 1.0
                for tc in range(TC):
                    v.drain()
                    inst = v.tensor_scalar(
                        out=qkT[:, qoff + T * mc + 512 * tc:qoff + T * mc + 512 * (tc + 1)],
                        in0=bks[tc][:, :], scalar1=scale, scalar2=None, op0=OP.mult)
                b.inc(inst, "sV", 1)
                qk_ev_t[i] = b.cnt["sV"]
        qk_done = b.cnt["sV"]
        b.pop()

        b.push()
        vew = [b.scr(f"vew{j}", [128, 512], BF16) for j in range(32)]
        val_mm_t = {}
        val_ev_t = {}
        ve_tok = {}
        for eg in range(2):
            @block.gpsimd
            def _(g, eg=eg):
                for tk in getattr(self, "l1stash_tok", []):
                    b.wtok(g, tk)
                if eg == 0:
                    b.wait(g, "sT")
                    b.wait(g, "sV")
                if eg == 1:
                    t_last = val_mm_t[(NT - 1, 0)]
                    g.wait_ge(b.sems["sT"], t_last)
                for j in range(4):
                    e = 4 * eg + j
                    for c in range(CC):
                        ve_tok[(eg, j, c)] = b.dma(
                            g, vew[8 * j + c][:],
                            dsc["ved"].ap()[e, 128 * c:128 * (c + 1), :], "G")

            for tt in range(NT):
                bks = [pb[4 * (tt % 2) + j] for j in range(4)]

                @block.tensor
                def _(t, tt=tt, eg=eg, bks=bks):
                    if tt == 0:
                        if eg == 0:
                            t.wait_ge(b.sems["sV"], qk_done)
                        for k in list(ve_tok):
                            if k[0] == eg:
                                b.wtok(t, ve_tok[k])
                    prev = (tt - 2, eg) if tt >= 2 else ((NT - 2 + tt, eg - 1) if eg > 0 else None)
                    if prev in val_ev_t:
                        t.wait_ge(b.sems["sV"], val_ev_t[prev])
                    for c in range(CC):
                        lhsT = xn[:, T * c + 128 * tt:T * c + 128 * (tt + 1)]
                        for j in range(4):
                            inst = t.matmul(bks[j][:, :], lhsT, vew[8 * j + c][:],
                                            start=(c == 0), stop=(c == CC - 1))
                    b.inc(inst, "sT", 1)
                    val_mm_t[(tt, eg)] = b.cnt["sT"]

                @block.vector
                def _(v, tt=tt, eg=eg, bks=bks):
                    v.wait_ge(b.sems["sT"], val_mm_t[(tt, eg)])
                    if tt == 0 and eg == 0:
                        v.wait_ge(b.sems["sV"], mask_ready)
                    v3 = values[:, 520 * tt:520 * (tt + 1)].rearrange(
                        "p (h z) -> p h z", h=8)
                    for j in range(4):
                        e = 4 * eg + j
                        mask_ap = maskbf[:].rearrange(
                            "p (tt h e) -> p tt h e", tt=NT, h=8) \
                            [:, tt:tt + 1, :, e:e + 1].squeeze(1) \
                            .broadcast_to([128, 8, 64])
                        v.drain()
                        v.copy_predicated(out=v3[:, :, 0:64], mask=mask_ap,
                                          data=bks[j][:, :].rearrange(
                                              "p (h z) -> p h z", h=8))
                    v.drain()
                    if eg == 1:
                        gsap = st[:, 16 + 8 * tt:16 + 8 * (tt + 1)] \
                            .unsqueeze(2).broadcast_to([128, 8, 64])
                        inst = v.tensor_tensor(out=v3[:, :, 0:64], in0=v3[:, :, 0:64],
                                               in1=gsap, op=OP.mult)
                        v.drain()
                    else:
                        inst = v.nop()
                    b.inc(inst, "sV", 1)
                    val_ev_t[(tt, eg)] = b.cnt["sV"]

        @block.vector
        def _(v):
            v.drain()
            b.inc(v.memset(values[:].rearrange("p (tt h z) -> p tt h z", tt=NT, h=8)
                           [:, :, :, 64:65], 1.0), "sV", 1)
        values_done = b.cnt["sV"]
        b.pop()

        ptt = [b.scr(f"ptt{i}", [128, 512], BF16) for i in range(4)]
        araw = b.scr("araw", [128, 4 * T], BF16)
        lbuf = b.scr("lbuf", [128, 512], F32)
        s_mm_t = {}
        p_ev_t = {}
        msk_t = {}
        a_mm_t = {}
        a_ev_t = {}
        l_dma_t = {}
        QT = qkT
        pa = PBt[:, 0:512]
        pl = PBt[:, 512:1024]
        for hp in range(4):
            for qc in range(TC):
                jt_list = list(range(4 * qc + 4))
                nji = len(jt_list)
                for ji, jt in enumerate(jt_list):
                    psS = [PBt[:, 1024 + 512 * (2 * (ji % 2) + par):
                               1536 + 512 * (2 * (ji % 2) + par)] for par in range(2)]

                    @block.tensor
                    def _(t, hp=hp, qc=qc, jt=jt, ji=ji, nji=nji, psS=psS):
                        if (hp, qc) == (0, 0) and ji == 0:
                            t.wait_ge(b.sems["sV"], values_done)
                        if ji >= 2:
                            t.wait_ge(b.sems["sS"], p_ev_t[(hp, qc, ji - 2)])
                        if ji == 0:
                            prev = ((hp, qc - 1) if qc > 0 else
                                    ((hp - 1, TC - 1) if hp > 0 else None))
                            if prev is not None:
                                t.wait_ge(b.sems["sV"], a_ev_t[prev])
                        for par in range(2):
                            rows = slice(64 * par, 64 * par + 64)
                            inst = t.matmul(
                                psS[par][:, :],
                                QT[rows, 4 * T + T * hp + 128 * jt:
                                   4 * T + T * hp + 128 * (jt + 1)],
                                QT[rows, T * hp + 512 * qc:T * hp + 512 * (qc + 1)],
                                start=True, stop=True)
                        b.inc(inst, "sT", 1)
                        s_mm_t[(hp, qc, ji)] = b.cnt["sT"]

                    @block.scalar
                    def _(sc, hp=hp, qc=qc, jt=jt, ji=ji, psS=psS):
                        sc.wait_ge(b.sems["sT"], s_mm_t[(hp, qc, ji)])
                        if ji >= 2:
                            sc.wait_ge(b.sems["sT"], a_mm_t[(hp, qc, ji - 2)])
                        for par in range(2):
                            inst = sc.activation(out=ptt[par + 2 * (ji % 2)][:],
                                                 in_=psS[par][:, :], func=AF.Exp,
                                                 bias=self.consts[:, 4:5], scale=1.0)
                        b.inc(inst, "sS", 1)
                        p_ev_t[(hp, qc, ji)] = b.cnt["sS"]

                    if jt >= 4 * qc:
                        @block.vector
                        def _(v, hp=hp, qc=qc, jt=jt, ji=ji):
                            v.wait_ge(b.sems["sS"], p_ev_t[(hp, qc, ji)])
                            bidx = jt - 4 * qc
                            v.drain()
                            for par in range(2):
                                inst = v.tensor_tensor(
                                    out=ptt[par + 2 * (ji % 2)][:],
                                    in0=ptt[par + 2 * (ji % 2)][:],
                                    in1=self.tri[:, 512 * bidx:512 * (bidx + 1)],
                                    op=OP.mult)
                                v.drain()
                            b.inc(inst, "sV", 1)
                            msk_t[(hp, qc, ji)] = b.cnt["sV"]

                    @block.tensor
                    def _(t, hp=hp, qc=qc, jt=jt, ji=ji, nji=nji):
                        t.wait_ge(b.sems["sS"], p_ev_t[(hp, qc, ji)])
                        if (hp, qc, ji) in msk_t:
                            t.wait_ge(b.sems["sV"], msk_t[(hp, qc, ji)])
                        for par in range(2):
                            h = 2 * hp + par
                            vl = values[:, 520 * jt + 65 * h:520 * jt + 65 * h + 64]
                            ol = values[:, 520 * jt + 65 * h + 64:520 * jt + 65 * h + 65]
                            pt = ptt[par + 2 * (ji % 2)][:]
                            t.matmul(pa[slice(64 * par, 64 * par + 64), :], vl, pt,
                                     start=(ji == 0), stop=(ji == nji - 1))
                            inst = t.matmul(pl[32 * par:32 * par + 1, :], ol, pt,
                                            start=(ji == 0), stop=(ji == nji - 1),
                                            tile_position=(0, 32 * par))
                        b.inc(inst, "sT", 1)
                        a_mm_t[(hp, qc, ji)] = b.cnt["sT"]

                @block.vector
                def _(v, hp=hp, qc=qc, nji=nji):
                    v.wait_ge(b.sems["sT"], a_mm_t[(hp, qc, nji - 1)])
                    if (hp, qc) != (0, 0):
                        prev = ((hp, qc - 1) if qc > 0 else (hp - 1, TC - 1))
                        b.wtok(v, l_dma_t[prev])
                    v.drain()
                    v.tensor_copy(out=araw[:, T * hp + 512 * qc:T * hp + 512 * (qc + 1)],
                                  in_=pa[:, :])
                    v.drain()
                    inst = v.tensor_copy(out=lbuf[:], in_=pl[:, :])
                    v.drain()
                    b.inc(inst, "sV", 1)
                    a_ev_t[(hp, qc)] = b.cnt["sV"]

                @block.gpsimd
                def _(g, hp=hp, qc=qc):
                    g.wait_ge(b.sems["sV"], a_ev_t[(hp, qc)])
                    l_dma_t[(hp, qc)] = b.dma(
                        g, dsc["lst"].ap()[:, 512 * (4 * hp + qc):512 * (4 * hp + qc + 1)],
                        lbuf[:], "G")

        lsb2 = b.scr("lsb2", [128, 128], F32)

        @block.gpsimd
        def _(g):
            for k in l_dma_t.values():
                b.wtok(g, k)
            src = dsc["lst"].ap()[:, 0:16 * 512] \
                .rearrange("(ra rb) c -> ra rb c", ra=4)[0:2, 0:1, :].squeeze(1)
            src = src.rearrange("par (hp qc q) -> par hp qc q", hp=4, qc=4)
            dst = dsc["ld8"].ap().rearrange("(hp par) (qc q) -> par hp qc q",
                                            hp=4, qc=4)
            tk = b.dma(g, dst, src, "G")
            b.wtok(g, tk)
            tk = b.dma(g, lsb2[:],
                       dsc["ld8"].ap().rearrange("h (qa qb) -> h qa qb", qa=16), "G")
            self._l1_l_rd = tk

        @block.vector
        def _(v):
            b.wtok(v, self._l1_l_rd)
            v.drain()
            v.reciprocal(out=lsb2[:], in_=lsb2[:])
            v.drain()
            b.inc(v.tensor_scalar(out=lsb2[:], in0=lsb2[:], scalar1=0.5,
                                  scalar2=None, op0=OP.mult), "sV", 1)
        minv_ready = b.cnt["sV"]

        @block.gpsimd
        def _(g):
            g.wait_ge(b.sems["sV"], minv_ready)
            self._l1_minv_wr = b.dma(
                g, dsc["minvd1"].ap().rearrange("h (qa qb) -> h qa qb", qa=16),
                lsb2[:], "G")

        gmT = b.scr("gmT", [64, T], F32)
        minvB = b.scr("minvB", [64, T], F32)
        gmtm = gates
        gm_tr_t = {}

        @block.vector
        def _(v):
            b.wait(v, "sV")
            v.drain()
            b.inc(v.tensor_tensor(out=gmtm[:], in0=gates[:], in1=maskbf[:],
                                  op=OP.mult), "sV", 1)
        gmtm_ready = b.cnt["sV"]

        for tt in range(NT):
            ptr = PBt[:, 3072 + 512 * (tt % 2):3072 + 512 * (tt % 2) + 128]

            @block.tensor
            def _(t, tt=tt, ptr=ptr):
                t.wait_ge(b.sems["sV"], gmtm_ready)
                if tt >= 2:
                    t.wait_ge(b.sems["sV"], gm_tr_t[tt - 2])
                b.inc(t.transpose(ptr[0:64, 0:128], gmtm[:, 64 * tt:64 * (tt + 1)],
                                  self.ident[:]), "sT", 1)
                gm_tr_t[(tt, "mm")] = b.cnt["sT"]

            @block.vector
            def _(v, tt=tt, ptr=ptr):
                v.wait_ge(b.sems["sT"], gm_tr_t[(tt, "mm")])
                v.drain()
                b.inc(v.tensor_copy(out=gmT[0:64, 128 * tt:128 * (tt + 1)],
                                    in_=ptr[0:64, :]), "sV", 1)
                gm_tr_t[tt] = b.cnt["sV"]

        @block.gpsimd
        def _(g):
            b.wtok(g, self._l1_minv_wr)
            self._l1_minvB = b.dma(
                g, minvB[0:64, :],
                dsc["minvd1"].ap().unsqueeze(1).broadcast_to([8, 8, T]), "G")

        @block.vector
        def _(v):
            b.wait(v, "sV")
            b.wtok(v, self._l1_minvB)
            v.drain()
            b.inc(v.tensor_tensor(out=gmT[0:64, :], in0=gmT[0:64, :],
                                  in1=minvB[0:64, :], op=OP.mult), "sV", 1)
        sT_ready = b.cnt["sV"]

        @block.gpsimd
        def _(g):
            g.wait_ge(b.sems["sV"], sT_ready)
            self._l1_sd_wr = b.dma(g, dsc["sd"].ap(), gmT[0:64, :], "G")

        oew = [b.scr(f"oew{i}", [128, 1024], BF16) for i in range(2)]
        sBt = [b.scr(f"sBt{i}", [128, 512], BF16) for i in range(2)]
        Gt = [b.scr(f"Gt{i}", [128, 512], BF16) for i in range(2)]
        oe_tok = {}
        sB_tok = {}
        g_bld_t = {}
        oe_mm_t = {}
        oe_ev_t = {}
        for qc in range(TC):
            for ch in range(32):
                e, hp = ch // 4, ch % 4
                ci = 32 * qc + ch
                rot = ci % 2

                @block.gpsimd
                def _(g, qc=qc, ch=ch, e=e, hp=hp, ci=ci, rot=rot):
                    if ci == 0:
                        b.wtok(g, self._l1_sd_wr)
                    if ci >= 2:
                        g.wait_ge(b.sems["sT"], oe_mm_t[ci - 2])
                    oe_tok[ci] = b.dma(
                        g, oew[rot][:],
                        dsc["oed"].ap()[512 * e + 128 * hp:512 * e + 128 * (hp + 1), :],
                        "G")
                    src = dsc["sd"].ap().rearrange("(h e2) t -> h e2 t", e2=8) \
                        [2 * hp:2 * hp + 2, e, 512 * qc:512 * (qc + 1)] \
                        .unsqueeze(1).broadcast_to([2, 64, 512])
                    sB_tok[ci] = b.dma(g, sBt[rot][:], src, "G")

                @block.vector
                def _(v, qc=qc, hp=hp, ci=ci, rot=rot):
                    b.wtok(v, sB_tok[ci])
                    if ci >= 2:
                        v.wait_ge(b.sems["sT"], oe_mm_t[ci - 2])
                    v.drain()
                    b.inc(v.tensor_tensor(
                        out=Gt[rot][:],
                        in0=araw[:, T * hp + 512 * qc:T * hp + 512 * (qc + 1)],
                        in1=sBt[rot][:], op=OP.mult), "sV", 1)
                    g_bld_t[ci] = b.cnt["sV"]

                @block.tensor
                def _(t, qc=qc, ch=ch, ci=ci, rot=rot):
                    b.wtok(t, oe_tok[ci])
                    t.wait_ge(b.sems["sV"], g_bld_t[ci])
                    if ch == 0 and qc > 0:
                        t.wait_ge(b.sems["sV"], oe_ev_t[qc - 1])
                    for dc in range(CC):
                        inst = t.matmul(pb[dc][:, :],
                                        oew[rot][:, 128 * dc:128 * (dc + 1)],
                                        Gt[rot][:], start=(ch == 0), stop=(ch == 31))
                    b.inc(inst, "sT", 1)
                    oe_mm_t[ci] = b.cnt["sT"]

            @block.vector
            def _(v, qc=qc):
                v.wait_ge(b.sems["sT"], oe_mm_t[32 * qc + 31])
                for dc in range(CC):
                    v.drain()
                    inst = v.scalar_tensor_tensor(
                        out=X[:, T * dc + 512 * qc:T * dc + 512 * (qc + 1)],
                        in0=pb[dc][:, :], scalar=1.0,
                        in1=X[:, T * dc + 512 * qc:T * dc + 512 * (qc + 1)],
                        op0=OP.mult, op1=OP.add)
                b.inc(inst, "sV", 1)
                oe_ev_t[qc] = b.cnt["sV"]
        b.pop()

    def ph_l1att(self):
        b, block, din, dsc, pb, PBt = self.b, self.block, self.din, self.dsc, self.pb, self.PBt
        X, st = self.X, self.stat
        b.push()
        yt = b.scr("yt", [128, CC * T], F16)
        b.push()
        xbf = b.scr("xbf4", [128, CC * T], BF16)
        b.push()
        x2bf = b.scr("x2bf4", [128, CC * T], BF16)
        rb = b.scr("rb4", [128, T], F32)
        cb = b.scr("cb4", [128, T], F32)
        rbbf = b.scr("rbbf4", [128, T], BF16)
        cbbf = b.scr("cbbf4", [128, T], BF16)
        self.emit_xbf_copies(xbf, x2bf)
        self.emit_ln_stats(xbf, x2bf, rb, cb)
        self.emit_ln_bf16_apply(xbf, rb, cb, rbbf, cbbf)
        b.pop()
        xn = xbf
        xn_ready = b.cnt["sV"]

        araw = b.scr("araw2", [128, 8 * T], BF16)
        lbuf = b.scr("lbuf2", [128, 512], F32)
        b.push()
        qp = [b.scr(f"qp{i}", [128, T], BF16) for i in range(2)]
        kp = [b.scr(f"kp{i}", [128, T], BF16) for i in range(2)]
        vp = [b.scr(f"vp{i}", [128, NT * 128], BF16) for i in range(2)]
        ptt = [b.scr(f"pttA{i}", [128, 1024], BF16) for i in range(2)]
        qkvw = [b.scr(f"qkvw{i}", [128, 3 * 128], BF16) for i in range(8)]

        qkv_tok = {}
        qkv_mm_t = {}
        qkv_ev_t = {}
        s_mm_t = {}
        p_ev_t = {}
        a_mm_t = {}
        a_ev_t = {}
        l_dma_t = {}
        pa = PBt[:, 0:512]
        pl = PBt[:, 512:1024]
        for hp in range(CC):
            rot = hp % 2

            @block.gpsimd
            def _(g, hp=hp):
                if hp == 0:
                    b.wait(g, "sT")
                    b.wait(g, "sV")
                if hp >= 1:
                    g.wait_ge(b.sems["sT"], self._la_vmm_last[hp - 1])
                toks = []
                for c in range(CC):
                    for wi, wn in enumerate(("wq_t", "wk_t", "wv_t")):
                        toks.append(b.dma(
                            g, qkvw[c][:, 128 * wi:128 * (wi + 1)],
                            din[wn].ap()[128 * c:128 * (c + 1),
                                         128 * hp:128 * (hp + 1)], "G"))
                qkv_tok[hp] = toks

            qk_half_t = {}
            for which, wslice, bcol, scale2 in (("q", slice(0, 128), 0, 0.125),
                                                ("k", slice(128, 256), 1, None)):
                dstp = qp if which == "q" else kp
                for qh in range(2):
                    bks = [pb[6], pb[7]]

                    @block.tensor
                    def _(t, hp=hp, rot=rot, which=which, wslice=wslice, qh=qh, bks=bks):
                        if hp == 0 and which == "q" and qh == 0:
                            t.wait_ge(b.sems["sV"], xn_ready)
                        for tk in qkv_tok[hp]:
                            b.wtok(t, tk)
                        prev = qk_half_t.get((hp, which, qh - 1))
                        if prev is None and which == "k":
                            prev = qk_half_t.get((hp, "q", 1))
                        if prev is not None:
                            t.wait_ge(b.sems["sV"], prev)
                        elif hp >= 1:
                            t.wait_ge(b.sems["sV"], qkv_ev_t[(hp - 1, "v", 3)])
                        if hp >= 2:
                            t.wait_ge(b.sems["sT"], s_mm_t[(hp - 2, TC - 1, NT - 1)])
                        for c in range(CC):
                            lhsT = qkvw[c][:, wslice]
                            for j in range(2):
                                tc = 2 * qh + j
                                inst = t.matmul(bks[j][:, :], lhsT,
                                                xn[:, T * c + 512 * tc:T * c + 512 * (tc + 1)],
                                                start=(c == 0), stop=(c == CC - 1))
                        b.inc(inst, "sT", 1)
                        qkv_mm_t[(hp, which, qh)] = b.cnt["sT"]

                    @block.vector
                    def _(v, hp=hp, rot=rot, which=which, bcol=bcol, scale2=scale2,
                          qh=qh, bks=bks, dstp=dstp):
                        v.wait_ge(b.sems["sT"], qkv_mm_t[(hp, which, qh)])
                        bb = self.biases[:, bcol * CC + hp:bcol * CC + hp + 1]
                        for j in range(2):
                            tc = 2 * qh + j
                            v.drain()
                            if scale2 is None:
                                inst = v.tensor_scalar(
                                    out=dstp[rot][:, 512 * tc:512 * (tc + 1)],
                                    in0=bks[j][:, :], scalar1=bb, scalar2=None,
                                    op0=OP.add)
                            else:
                                inst = v.tensor_scalar(
                                    out=dstp[rot][:, 512 * tc:512 * (tc + 1)],
                                    in0=bks[j][:, :], scalar1=bb, scalar2=scale2,
                                    op0=OP.add, op1=OP.mult)
                        b.inc(inst, "sV", 1)
                        qk_half_t[(hp, which, qh)] = b.cnt["sV"]
                        qkv_ev_t[(hp, which)] = b.cnt["sV"]

            vv_mm = {}
            for tg in range(4):
                bkv = [pb[6 + (tg % 2)]]

                @block.tensor
                def _(t, hp=hp, rot=rot, tg=tg, bkv=bkv):
                    if tg == 0:
                        t.wait_ge(b.sems["sV"], qk_half_t[(hp, "k", 1)])
                    if tg == 1:
                        pass
                    if tg >= 2:
                        t.wait_ge(b.sems["sV"], qkv_ev_t[(hp, "v", tg - 2)])
                    if hp >= 2 and tg < 2:
                        t.wait_ge(b.sems["sT"], a_mm_t[(hp - 2, TC - 1, NT - 1)])
                    for tj in range(4):
                        tt = 4 * tg + tj
                        for c in range(CC):
                            inst = t.matmul(
                                bkv[0][:, 128 * tj:128 * (tj + 1)],
                                xn[:, T * c + 128 * tt:T * c + 128 * (tt + 1)],
                                qkvw[c][:, 256:384],
                                start=(c == 0), stop=(c == CC - 1))
                    b.inc(inst, "sT", 1)
                    vv_mm[tg] = b.cnt["sT"]

                @block.vector
                def _(v, hp=hp, rot=rot, tg=tg, bkv=bkv):
                    v.wait_ge(b.sems["sT"], vv_mm[tg])
                    if hp >= 2 and tg == 0:
                        t_l = a_mm_t[(hp - 2, TC - 1, NT - 1)]
                        v.wait_ge(b.sems["sT"], t_l)
                    v.drain()
                    inst = v.tensor_copy(out=vp[rot][:, 512 * tg:512 * (tg + 1)],
                                         in_=bkv[0][:, :])
                    v.drain()
                    b.inc(inst, "sV", 1)
                    qkv_ev_t[(hp, "v", tg)] = b.cnt["sV"]
            if not hasattr(self, "_la_vmm_last"):
                self._la_vmm_last = {}
            self._la_vmm_last[hp] = vv_mm[3]
            vready = b.cnt["sV"]

            for qc in range(TC):
                for jt in range(NT):
                    psS = PBt[:, 1024 + 1024 * (jt % 2):2048 + 1024 * (jt % 2)]

                    @block.tensor
                    def _(t, hp=hp, rot=rot, qc=qc, jt=jt, vready=vready, psS=psS):
                        if qc == 0 and jt == 0:
                            t.wait_ge(b.sems["sV"], vready)
                        if jt >= 2:
                            t.wait_ge(b.sems["sS"], p_ev_t[(hp, qc, jt - 2)])
                        if jt == 0:
                            prev = ((hp, qc - 1) if qc > 0 else
                                    ((hp - 1, TC - 1) if hp >= 1 else None))
                            if prev is not None and prev in a_ev_t:
                                t.wait_ge(b.sems["sV"], a_ev_t[prev])
                        for par in range(2):
                            rows = slice(64 * par, 64 * par + 64)
                            inst = t.matmul(
                                psS[:, 512 * par:512 * (par + 1)],
                                kp[rot][rows, 128 * jt:128 * (jt + 1)],
                                qp[rot][rows, 512 * qc:512 * (qc + 1)],
                                start=True, stop=True)
                        b.inc(inst, "sT", 1)
                        s_mm_t[(hp, qc, jt)] = b.cnt["sT"]

                    @block.scalar
                    def _(sc, hp=hp, qc=qc, jt=jt, psS=psS):
                        sc.wait_ge(b.sems["sT"], s_mm_t[(hp, qc, jt)])
                        if jt >= 2:
                            sc.wait_ge(b.sems["sT"], a_mm_t[(hp, qc, jt - 2)])
                        b.inc(sc.activation(out=ptt[jt % 2][:], in_=psS[:, :],
                                            func=AF.Exp, bias=self.consts[:, 5:6],
                                            scale=1.0), "sS", 1)
                        p_ev_t[(hp, qc, jt)] = b.cnt["sS"]

                    @block.tensor
                    def _(t, hp=hp, rot=rot, qc=qc, jt=jt, psS=psS):
                        t.wait_ge(b.sems["sS"], p_ev_t[(hp, qc, jt)])
                        for par in range(2):
                            pt = ptt[jt % 2][:, 512 * par:512 * (par + 1)]
                            vl = vp[rot][:, 128 * jt:128 * jt + 64] if False else \
                                vp[rot].rearrange("p (tt h) -> p tt h", tt=NT) \
                                [:, jt:jt + 1, :].squeeze(1)
                            t.matmul(pa[slice(64 * par, 64 * par + 64), :],
                                     vl[:, 64 * par:64 * (par + 1)] if False else
                                     vp[rot][:, 128 * jt + 64 * par:128 * jt + 64 * (par + 1)]
                                     if False else vp[rot][:, 128 * jt + 64 * 0:128 * jt + 64],
                                     pt, start=(jt == 0), stop=(jt == NT - 1))
                            inst = t.matmul(pl[32 * par:32 * par + 1, :],
                                            self.onecol_bf[:, 0:1], pt,
                                            start=(jt == 0), stop=(jt == NT - 1),
                                            tile_position=(0, 32 * par))
                        b.inc(inst, "sT", 1)
                        a_mm_t[(hp, qc, jt)] = b.cnt["sT"]

                @block.vector
                def _(v, hp=hp, qc=qc):
                    v.wait_ge(b.sems["sT"], a_mm_t[(hp, qc, NT - 1)])
                    if (hp, qc) != (0, 0):
                        prev = ((hp, qc - 1) if qc > 0 else (hp - 1, TC - 1))
                        b.wtok(v, l_dma_t[prev])
                    v.drain()
                    v.tensor_copy(out=araw[:, T * hp + 512 * qc:T * hp + 512 * (qc + 1)],
                                  in_=pa[:, :])
                    v.drain()
                    inst = v.tensor_copy(out=lbuf[:], in_=pl[:, :])
                    v.drain()
                    b.inc(inst, "sV", 1)
                    a_ev_t[(hp, qc)] = b.cnt["sV"]

                @block.gpsimd
                def _(g, hp=hp, qc=qc):
                    g.wait_ge(b.sems["sV"], a_ev_t[(hp, qc)])
                    l_dma_t[(hp, qc)] = b.dma(
                        g, dsc["lst"].ap()[:, 512 * (4 * hp + qc):512 * (4 * hp + qc + 1)],
                        lbuf[:], "G")
        b.pop()

        lsb2 = b.scr("lsb2A", [128, 256], F32)

        @block.gpsimd
        def _(g):
            for k in l_dma_t.values():
                b.wtok(g, k)
            src = dsc["lst"].ap().rearrange("(ra rb) c -> ra rb c", ra=4)[0:2, 0:1, :] \
                .squeeze(1)
            src = src.rearrange("par (hp qc q) -> par hp qc q", hp=8, qc=4)
            dst = dsc["ld16"].ap().rearrange("(hp par) (qc q) -> par hp qc q",
                                             hp=8, qc=4)
            tk = b.dma(g, dst, src, "G")
            b.wtok(g, tk)
            tk = b.dma(g, lsb2[:],
                       dsc["ld16"].ap().rearrange("h (qa qb) -> h qa qb", qa=8), "G")
            self._la_l_rd = tk

        @block.vector
        def _(v):
            b.wtok(v, self._la_l_rd)
            v.drain()
            b.inc(v.reciprocal(out=lsb2[:], in_=lsb2[:]), "sV", 1)
        minv_ready = b.cnt["sV"]

        @block.gpsimd
        def _(g):
            g.wait_ge(b.sems["sV"], minv_ready)
            self._la_minv_wr = b.dma(
                g, dsc["minvd"].ap().rearrange("h (qa qb) -> h qa qb", qa=8),
                lsb2[:], "G")

        gsc = [b.scr(f"gscA{i}", [128, T], BF16) for i in range(2)]
        gsc_tok = {}
        scale_t = {}
        for hp in range(CC):
            @block.gpsimd
            def _(g, hp=hp):
                if hp == 0:
                    b.wtok(g, self._la_minv_wr)
                if hp >= 2:
                    g.wait_ge(b.sems["sV"], scale_t[hp - 2])
                src = dsc["minvd"].ap()[2 * hp:2 * hp + 2, :]
                gsc_tok[hp] = b.dma(g, gsc[hp % 2][:],
                                    src.unsqueeze(1).broadcast_to([2, 64, T]), "G")

            @block.vector
            def _(v, hp=hp):
                b.wtok(v, gsc_tok[hp])
                v.drain()
                sl = slice(T * hp, T * (hp + 1))
                b.inc(v.tensor_tensor(out=araw[:, sl], in0=araw[:, sl],
                                      in1=gsc[hp % 2][:], op=OP.mult), "sV", 1)
                scale_t[hp] = b.cnt["sV"]

        wob = [b.scr(f"wobA{i}", [128, 1024], BF16) for i in range(CC)]
        bvb8 = b.scr("bvb8", [128, CC], BF16)

        @block.gpsimd
        def _(g):
            b.wait(g, "sT")
            toks = []
            for c in range(CC):
                toks.append(b.dma(g, wob[c][:],
                                  din["wo_t"].ap()[128 * c:128 * (c + 1), :], "G"))
            self._la_wo_tok = toks

        @block.vector
        def _(v):
            b.wait(v, "sV")
            v.drain()
            b.inc(v.tensor_copy(out=bvb8[:], in_=self.biases[:, 2 * CC:3 * CC]), "sV", 1)
        bv_ready = b.cnt["sV"]

        cv_t = {}

        @block.tensor
        def _(t):
            for tk in self._la_wo_tok:
                b.wtok(t, tk)
            t.wait_ge(b.sems["sV"], bv_ready)
            pc = PBt[:, 2048:2176]
            for dc in range(CC):
                for c in range(CC):
                    inst = t.matmul(pc[:, dc:dc + 1],
                                    wob[c][:, 128 * dc:128 * (dc + 1)],
                                    bvb8[:, c:c + 1],
                                    start=(c == 0), stop=(c == CC - 1))
            b.inc(inst, "sT", 1)
        cv_done = b.cnt["sT"]

        @block.vector
        def _(v):
            v.wait_ge(b.sems["sT"], cv_done)
            v.drain()
            v.tensor_copy(out=st[:, 32:40], in_=PBt[:, 2048:2056])
            v.drain()
            b.inc(v.tensor_tensor(out=st[:, 40:48], in0=st[:, 32:40],
                                  in1=self.biases[:, 3 * CC:4 * CC], op=OP.add), "sV", 1)
        boc_ready = b.cnt["sV"]

        wo_mm_t = {}
        wo_ev_t = {}
        for dc in range(CC):
            bks = [pb[4 * (dc % 2) + j] for j in range(4)]

            @block.tensor
            def _(t, dc=dc, bks=bks):
                if dc == 0:
                    t.wait_ge(b.sems["sV"], scale_t[CC - 1])
                    t.wait_ge(b.sems["sV"], boc_ready)
                if dc >= 2:
                    t.wait_ge(b.sems["sV"], wo_ev_t[dc - 2])
                for c in range(CC):
                    lhsT = wob[c][:, 128 * dc:128 * (dc + 1)]
                    for tc in range(TC):
                        inst = t.matmul(bks[tc][:, :], lhsT,
                                        araw[:, T * c + 512 * tc:T * c + 512 * (tc + 1)],
                                        start=(c == 0), stop=(c == CC - 1))
                b.inc(inst, "sT", 1)
                wo_mm_t[dc] = b.cnt["sT"]

            @block.vector
            def _(v, dc=dc, bks=bks):
                v.wait_ge(b.sems["sT"], wo_mm_t[dc])
                for tc in range(TC):
                    v.drain()
                    inst = v.scalar_tensor_tensor(
                        out=yt[:, T * dc + 512 * tc:T * dc + 512 * (tc + 1)],
                        in0=bks[tc][:, :], scalar=st[:, 40 + dc:41 + dc],
                        in1=xn[:, T * dc + 512 * tc:T * dc + 512 * (tc + 1)],
                        op0=OP.add, op1=OP.add)
                b.inc(inst, "sV", 1)
                wo_ev_t[dc] = b.cnt["sV"]
        y_done = b.cnt["sV"]
        b.pop()

        b.push()
        ybf = b.scr("ybf", [128, CC * T], BF16)
        y2bf = b.scr("y2bf", [128, CC * T], BF16)
        rb2 = b.scr("rb5", [128, T], F32)
        cb2 = b.scr("cb5", [128, T], F32)
        t1 = b.scr("t1", [128, T], F32)

        @block.vector
        def _(v):
            v.wait_ge(b.sems["sV"], y_done)
            v.drain()
            b.inc(v.tensor_copy(out=ybf[:], in_=yt[:]), "sV", 1)

        @block.scalar
        def _(s):
            b.wait(s, "sV")
            b.inc(s.activation(out=y2bf[:], in_=yt[:], func=AF.Square,
                               bias=self.consts[:, 0:1], scale=1.0), "sS", 1)
        self.emit_ln_stats(ybf, y2bf, rb2, cb2)

        @block.vector
        def _(v):
            b.wait(v, "sV")
            v.drain()
            v.reciprocal(out=st[:, 48:56], in_=self.biases[:, 4 * CC:5 * CC])
            v.drain()
            b.inc(v.tensor_tensor(out=st[:, 48:56], in0=st[:, 48:56],
                                  in1=self.biases[:, 5 * CC:6 * CC], op=OP.mult),
                  "sV", 1)

        xup_t = {}
        for dc in range(CC):
            @block.vector
            def _(v, dc=dc):
                b.wait(v, "sV")
                v.drain()
                v.tensor_tensor(out=t1[:], in0=yt[:, T * dc:T * (dc + 1)], in1=rb2[:],
                                op=OP.mult)
                v.drain()
                v.scalar_tensor_tensor(out=t1[:], in0=t1[:],
                                       scalar=st[:, 48 + dc:49 + dc],
                                       in1=cb2[:], op0=OP.add, op1=OP.subtract)
                v.drain()
                inst = v.scalar_tensor_tensor(
                    out=X[:, T * dc:T * (dc + 1)], in0=t1[:],
                    scalar=self.biases[:, 4 * CC + dc:4 * CC + dc + 1],
                    in1=X[:, T * dc:T * (dc + 1)], op0=OP.mult, op1=OP.add)
                v.drain()
                b.inc(inst, "sV", 1)
                xup_t[dc] = b.cnt["sV"]
        b.pop()
        b.pop()
    def ph_store(self):
        b, block = self.b, self.block
        self.stored = True

        @block.sync
        def _(q):
            b.wait(q, "sV")
            b.wait(q, "sT")
            b.wait(q, "sS")
            tok = b.dma(q, self.dout.ap().rearrange("(c p) t -> p c t", p=128),
                        self.X[:].rearrange("p (c t) -> p c t", c=CC), "S")
            b.wtok(q, tok)

    def ph_store_taps(self):
        b, block = self.b, self.block
        if not self.stored:
            self.ph_store()

        @block.sync
        def _(q):
            b.wait_allq(q, "S")

        @block.gpsimd
        def _(g):
            b.wait_allq(g, "G")


def kernel(**inputs):
    in_maps = _prep_inputs(inputs)
    nc = build_program()
    res = run_bass_kernel_spmd(nc, in_maps, core_ids=list(range(NCORES)))
    out = np.stack([np.ascontiguousarray(r["out_t"].T) for r in res.results])
    return out.astype(np.float32)


# version 02
# speedup vs baseline: 1.0553x; 1.0553x over previous
import numpy as np
import ml_dtypes
import concourse.bass as bass
import concourse.mybir as mybir
from concourse.alu_op_type import AluOpType
from concourse.bass_utils import run_bass_kernel_spmd

F32 = mybir.dt.float32
BF16 = mybir.dt.bfloat16
F16 = mybir.dt.float16
U8 = mybir.dt.uint8
AF = mybir.ActivationFunctionType
AX = mybir.AxisListType
OP = AluOpType
BF = ml_dtypes.bfloat16

NCORES = 8
T = 2048
D = 1024
CC = D // 128
TC = T // 512
NT = T // 128
EPS_LN = 1e-5
NROT = 4


class Bld:

    def __init__(self, nc):
        self.nc = nc
        self.sems = {}
        self.cnt = {}
        self.rot = {"S": -1, "G": -1, "C": -1}
        self.cur = {}
        self.scr_off = 0
        self.scr_stack = []
        self.scratch_addr = None
        self.scratch_bytes = 0
        self.uid = 0

    def sem(self, name, handle):
        self.sems[name] = handle
        self.cnt[name] = 0

    def inc(self, inst, name, k=1):
        inst.then_inc(self.sems[name], k)
        self.cnt[name] += k
        return inst

    def wait(self, eng, name, tgt=None):
        eng.wait_ge(self.sems[name], self.cnt[name] if tgt is None else tgt)

    def gbegin(self, q, eng=None):
        self.rot[q] = (self.rot[q] + 1) % NROT
        s = f"d{q}{self.rot[q]}"
        self.cur[q] = s
        if eng is not None and self.cnt[s] > 0:
            eng.wait_ge(self.sems[s], self.cnt[s])
        return s

    def dinc(self, inst, q):
        s = self.cur[q]
        inst.then_inc(self.sems[s], 16)
        self.cnt[s] += 16
        return inst

    def gend(self, q):
        s = self.cur[q]
        return (s, self.cnt[s])

    def dma(self, eng, out, in_, q):
        self.gbegin(q, eng)
        self.dinc(eng.dma_start(out=out, in_=in_), q)
        return self.gend(q)

    def wtok(self, eng, tok):
        eng.wait_ge(self.sems[tok[0]], tok[1])

    def wait_allq(self, eng, q):
        for i in range(NROT):
            s = f"d{q}{i}"
            eng.wait_ge(self.sems[s], self.cnt[s])

    def push(self):
        self.scr_stack.append(self.scr_off)

    def pop(self):
        self.scr_off = self.scr_stack.pop()

    def scr(self, name, shape, dtype, align=32):
        per_part = int(np.prod(shape[1:])) * mybir.dt.size(dtype)
        off = (self.scr_off + align - 1) // align * align
        assert off + per_part <= self.scratch_bytes, (
            f"scratch overflow: {name} needs {per_part} at {off}, cap {self.scratch_bytes}")
        self.scr_off = off + per_part
        self.uid += 1
        self.offs = getattr(self, "offs", {})
        self.offs[name] = (off, per_part)
        return self.nc.alloc_sbuf_tensor_at(
            f"scr_{name}_{self.uid}", list(shape), dtype,
            offset=self.scratch_addr + off)

    def scr_at(self, name, shape, dtype, at, delta=0):
        per_part = int(np.prod(shape[1:])) * mybir.dt.size(dtype)
        off, sz = self.offs[at]
        assert delta + per_part <= sz, (name, per_part, at, sz, delta)
        off = off + delta
        self.uid += 1
        return self.nc.alloc_sbuf_tensor_at(
            f"scr_{name}_{self.uid}", list(shape), dtype,
            offset=self.scratch_addr + off)


def _prep_inputs(I):
    C = np.ascontiguousarray
    f32 = np.float32
    shared = {
        "w_qkv_t": C(np.asarray(I["L0_attn_W_qkv"]).T),
        "w_out_t": C(np.asarray(I["L0_attn_W_out"]).T),
        "w_gate": C(np.asarray(I["L0_ffn_w_gate"])),
        "w_up": C(np.asarray(I["L0_ffn_w_up"])),
        "w_down": C(np.asarray(I["L0_ffn_w_down"])),
        "clip_x": C(np.reshape(np.asarray(I["L0_ffn_clip_x"]), (1, 1))),
        "clip_down": C(np.reshape(np.asarray(I["L0_ffn_clip_down"]), (1, 1))),
        "wq_t": C(np.asarray(I["L1_attn_Wq"]).T), "wk_t": C(np.asarray(I["L1_attn_Wk"]).T),
        "wv_t": C(np.asarray(I["L1_attn_Wv"]).T), "wo_t": C(np.asarray(I["L1_attn_Wo"]).T),
        "bq": C(np.reshape(np.asarray(I["L1_attn_bq"]), (CC, 128)).T),
        "bk": C(np.reshape(np.asarray(I["L1_attn_bk"]), (CC, 128)).T),
        "bv": C(np.reshape(np.asarray(I["L1_attn_bv"]), (CC, 128)).T),
        "bo": C(np.reshape(np.asarray(I["L1_attn_bo"]), (CC, 128)).T),
        "ln_g": C(np.reshape(np.asarray(I["L1_attn_ln_gamma"]), (CC, 128)).T),
        "ln_b": C(np.reshape(np.asarray(I["L1_attn_ln_beta"]), (CC, 128)).T),
        "w_q1": C(np.asarray(I["L1_ffn_w_q"])),
        "w_k1": C(np.asarray(I["L1_ffn_w_k"])),
        "w_g1": C(np.asarray(I["L1_ffn_w_gate"])),
        "ve_r": C(np.transpose(np.asarray(I["L1_ffn_value_experts"]),
                               (0, 2, 1, 3)).reshape(8, 1024, 512)),
        "oe_flat": C(np.reshape(np.asarray(I["L1_ffn_output_experts"]), (4096, 1024))),
        "ident": np.eye(128, dtype=f32),
    }
    tri = np.zeros((128, 4 * 512), dtype=f32)
    jj = np.arange(128)[:, None]
    qq = np.arange(512)[None, :]
    for bb in range(4):
        tri[:, 512 * bb:512 * (bb + 1)] = (jj + 128 * bb <= qq)
    shared["tri"] = tri.astype(BF)
    x = np.asarray(I["L0_attn_x"])
    in_maps = []
    for i in range(NCORES):
        m = dict(shared)
        m["x_t"] = C(x[i].T)
        in_maps.append(m)
    return in_maps


def build_program(stop_after=None, taps=()):
    nc = bass.Bass("TRN2", target_bir_lowering=False, debug=False)

    din = {}
    for name, shape in [
        ("x_t", [D, T]), ("w_qkv_t", [D, 3 * D]), ("w_out_t", [D, D]),
        ("w_gate", [2816, D]), ("w_up", [2816, D]), ("w_down", [D, 2816]),
        ("clip_x", [1, 1]), ("clip_down", [1, 1]),
        ("wq_t", [D, D]), ("wk_t", [D, D]), ("wv_t", [D, D]), ("wo_t", [D, D]),
        ("bq", [128, CC]), ("bk", [128, CC]), ("bv", [128, CC]), ("bo", [128, CC]),
        ("ln_g", [128, CC]), ("ln_b", [128, CC]),
        ("w_q1", [D, 512]), ("w_k1", [D, 512]), ("w_g1", [D, 64]),
        ("ve_r", [8, D, 512]), ("oe_flat", [4096, D]), ("ident", [128, 128]),
    ]:
        din[name] = nc.dram_tensor(name, shape, F32, kind="ExternalInput")
    din["tri"] = nc.dram_tensor("tri", [128, 4 * 512], BF16, kind="ExternalInput")
    dout = nc.dram_tensor("out_t", [D, T], F32, kind="ExternalOutput")

    dsc = {
        "qwg": nc.dram_tensor("qwg", [2816, D], BF16),
        "qwu": nc.dram_tensor("qwu", [2816, D], BF16),
        "qwd": nc.dram_tensor("qwd", [D, 2816], BF16),
        "hsp": nc.dram_tensor("hsp", [2816, T], F32),
        "vec": nc.dram_tensor("vec", [128, 16], F32),
        "vec1": nc.dram_tensor("vec1", [128, 1], F32),
        "rowb": nc.dram_tensor("rowb", [64, T], BF16),
        "ccin": nc.dram_tensor("ccin", [1, 1], F32),
        "statsd": nc.dram_tensor("statsd", [128, 32 * 512], F32),
        "minvd": nc.dram_tensor("minvd", [16, T], F32),
        "ved": nc.dram_tensor("ved", [8, D, 512], BF16),
        "oed": nc.dram_tensor("oed", [4096, D], BF16),
        "lst": nc.dram_tensor("lst", [128, 32 * 512], F32),
        "ld16": nc.dram_tensor("ld16", [16, T], F32),
        "ld8": nc.dram_tensor("ld8", [8, T], F32),
        "minvd1": nc.dram_tensor("minvd1", [8, T], F32),
        "sd": nc.dram_tensor("sd", [64, T], BF16),
        "ccout": nc.dram_tensor("ccout", [NCORES, 1], F32),
    }
    tap_outs = {}
    for tname, tshape in taps:
        tap_outs[tname] = nc.dram_tensor("tap_" + tname, tshape, F32, kind="ExternalOutput")

    from contextlib import ExitStack
    ctx = ExitStack()
    with ctx:
        X = ctx.enter_context(nc.sbuf_tensor("X", [128, CC * T], F32))
        ident = ctx.enter_context(nc.sbuf_tensor("identity_sb", [128, 128], F32))
        tri = ctx.enter_context(nc.sbuf_tensor("tri_sb", [128, 4 * 512], BF16))
        onecol_bf = ctx.enter_context(nc.sbuf_tensor("onecol_bf", [128, 1], BF16))
        consts = ctx.enter_context(nc.sbuf_tensor("consts", [128, 8], F32))
        biases = ctx.enter_context(nc.sbuf_tensor("biases", [128, 6 * CC], F32))
        stat = ctx.enter_context(nc.sbuf_tensor("stat", [128, 256], F32))
        ones_bf = ctx.enter_context(nc.sbuf_tensor("ones_bf", [128, 128], BF16))
        wsc = ctx.enter_context(nc.sbuf_tensor("wsc", [128, 52], F32))
        stat2col = ctx.enter_context(nc.sbuf_tensor("stat2col", [128, 2], F32))
        clip_sb = ctx.enter_context(nc.sbuf_tensor("clip_sb", [128, 2], F32))
        SCR_BYTES = 137 * 1024
        scratch = ctx.enter_context(nc.sbuf_tensor("scratch", [128, SCR_BYTES], U8))
        PBt = ctx.enter_context(nc.psum_tensor("PBt", [128, 4096], F32))
        pb = [PBt[:, 512 * i:512 * (i + 1)] for i in range(8)]

        sem_names = (["sT", "sV", "sS", "cc"]
                     + [f"d{q}{i}" for q in "SGC" for i in range(NROT)])
        sems = {n: ctx.enter_context(nc.semaphore(n)) for n in sem_names}

        block = ctx.enter_context(nc.Block())
        b = Bld(nc)
        b.scratch_bytes = SCR_BYTES
        b.scratch_addr = nc.lookup_mloc(scratch).addr
        for n in sem_names:
            b.sem(n, sems[n])

        st = _State(nc, block, b, din, dout, dsc, X, ident, tri, onecol_bf,
                    consts, biases, stat, ones_bf, pb, tap_outs)
        st.PBt = PBt
        st.wsc, st.stat2col, st.clip_sb = wsc, stat2col, clip_sb

        phases = [
            ("init", st.ph_init),
            ("wquant", st.ph_wquant),
            ("l1stash", st.ph_l1stash),
            ("l0f1", lambda: st.ph_l0ffn(0)),
            ("l0att", st.ph_l0att),
            ("l0f2", lambda: st.ph_l0ffn(1)),
            ("l1f1", lambda: st.ph_l1ffn(0)),
            ("l1att", st.ph_l1att),
            ("l1f2", lambda: st.ph_l1ffn(1)),
            ("store", st.ph_store),
        ]
        for name, fn in phases:
            fn()
            if stop_after is not None and name == stop_after:
                break
        st.ph_store_taps()
    return nc


class _State:
    def __init__(self, nc, block, b, din, dout, dsc, X, ident, tri, onecol_bf,
                 consts, biases, stat, ones_bf, pb, tap_outs):
        self.nc, self.block, self.b = nc, block, b
        self.din, self.dout, self.dsc = din, dout, dsc
        self.X, self.ident, self.tri = X, ident, tri
        self.onecol_bf, self.consts = onecol_bf, consts
        self.biases, self.stat, self.ones_bf, self.pb = biases, stat, ones_bf, pb
        self.taps = tap_outs
        self.stored = False

    def tap_now(self, name, ap, wait=("sV", "sS", "sT")):
        if name not in self.taps:
            return
        b, block = self.b, self.block
        dram = self.taps[name]

        @block.gpsimd
        def _(g):
            for w in wait:
                b.wait(g, w)
            b.wait_allq(g, "G")
            tok = b.dma(g, dram.ap(), ap, "G")
            b.wtok(g, tok)

    def maybe_tap(self, name, ap):
        if name not in self.taps:
            return
        b, block = self.b, self.block
        dram = self.taps[name]

        @block.gpsimd
        def _(g):
            b.wait(g, "sV"); b.wait(g, "sS"); b.wait(g, "sT")
            tok = b.dma(g, dram.ap(), ap, "G")
            b.wtok(g, tok)

    def emit_xbf_copies(self, xbf, x2bf):
        b, block = self.b, self.block

        @block.vector
        def _(v):
            b.wait_allq(v, "S")
            b.wait_allq(v, "G")
            b.wait(v, "sT")
            v.drain()
            b.inc(v.tensor_copy(out=xbf[:], in_=self.X[:]), "sV", 1)

        @block.scalar
        def _(s):
            b.wait_allq(s, "S")
            b.wait_allq(s, "G")
            b.wait(s, "sV")
            b.wait(s, "sT")
            b.inc(s.activation(out=x2bf[:], in_=self.X[:], func=AF.Square,
                               bias=self.consts[:, 0:1], scale=1.0), "sS", 1)

    def emit_ln_stats(self, xbf, x2bf, rb, cb):
        b, block, pb = self.b, self.block, self.pb
        D_inv = 1.0 / D
        epscol = self.consts[:, 2:3]
        vc0 = b.cnt["sV"]
        sc0 = b.cnt["sS"]

        @block.tensor
        def _(t):
            t.wait_ge(b.sems["sV"], vc0)
            t.wait_ge(b.sems["sS"], sc0)
            for tc in range(TC):
                if tc >= 2:
                    t.wait_ge(b.sems["sV"], vc0 + 2 * (tc - 2) + 1)
                for cc in range(CC):
                    csl = slice(T * cc + 512 * tc, T * cc + 512 * tc + 512)
                    t.matmul(pb[2 * (tc % 2)][:, :], self.ones_bf[:], xbf[:, csl],
                             start=(cc == 0), stop=(cc == CC - 1))
                    inst = t.matmul(pb[2 * (tc % 2) + 1][:, :], self.ones_bf[:],
                                    x2bf[:, csl], start=(cc == 0), stop=(cc == CC - 1))
                b.inc(inst, "sT", 1)
        tmm = b.cnt["sT"]

        for tc in range(TC):
            sl = slice(512 * tc, 512 * (tc + 1))
            p_sx = pb[2 * (tc % 2)]
            p_sxx = pb[2 * (tc % 2) + 1]
            p_tmp = pb[4 + (tc % 2)]

            @block.vector
            def _(v, tc=tc, sl=sl, p_sx=p_sx, p_sxx=p_sxx):
                v.wait_ge(b.sems["sT"], tmm - TC + tc + 1)
                v.drain()
                v.tensor_scalar(out=cb[:, sl], in0=p_sx[:, :], scalar1=D_inv,
                                scalar2=None, op0=OP.mult)
                v.drain()
                v.tensor_tensor(out=rb[:, sl], in0=cb[:, sl], in1=cb[:, sl], op=OP.mult)
                v.drain()
                b.inc(v.scalar_tensor_tensor(out=rb[:, sl], in0=p_sxx[:, :],
                                             scalar=D_inv, in1=rb[:, sl],
                                             op0=OP.mult, op1=OP.subtract), "sV", 1)

            @block.scalar
            def _(s, tc=tc, sl=sl, p_tmp=p_tmp):
                s.wait_ge(b.sems["sV"], vc0 + 2 * tc + 1)
                if tc >= 2:
                    s.wait_ge(b.sems["sV"], vc0 + 2 * (tc - 2) + 2)
                b.inc(s.activation(out=p_tmp[:, :], in_=rb[:, sl], func=AF.Sqrt,
                                   bias=epscol, scale=1.0), "sS", 1)

            @block.vector
            def _(v, tc=tc, sl=sl, p_tmp=p_tmp):
                v.wait_ge(b.sems["sS"], sc0 + tc + 1)
                v.drain()
                v.reciprocal(out=rb[:, sl], in_=p_tmp[:, :])
                v.drain()
                b.inc(v.tensor_tensor(out=cb[:, sl], in0=cb[:, sl], in1=rb[:, sl],
                                      op=OP.mult), "sV", 1)

    def emit_flat_allreduce_max(self, col_ap, st):
        b, block, dsc = self.b, self.block, self.dsc
        vc = b.cnt["sV"]

        @block.gpsimd
        def _(g):
            g.wait_ge(b.sems["sV"], vc)
            tok = b.dma(g, dsc["vec1"].ap(), col_ap, "G")
            b.wtok(g, tok)
            tok = b.dma(g, st[0:1, 65:193],
                        dsc["vec1"].ap().rearrange("a b -> b a"), "G")
            b.wtok(g, tok)
        g1 = b.cnt["dG" + str(b.rot["G"])], b.rot["G"]

        @block.vector
        def _(v):
            b.wait_allq(v, "G")
            v.drain()
            b.inc(v.reduce_max(out=st[0:1, 64:65], in_=st[0:1, 65:193],
                               axis=AX.X), "sV", 1)
        vc2 = b.cnt["sV"]

        @block.gpsimd
        def _(g):
            g.wait_ge(b.sems["sV"], vc2)
            tok = b.dma(g, dsc["ccin"].ap(), st[0:1, 64:65], "G")
            b.wtok(g, tok)
            b.inc(g.collective_compute(
                "AllGather", OP.bypass, replica_groups=[list(range(NCORES))],
                ins=[dsc["ccin"].ap()], outs=[dsc["ccout"].ap()]), "cc", 1)
            b.wait(g, "cc")
            tok = b.dma(g, st[0:1, 65:65 + NCORES],
                        dsc["ccout"].ap().rearrange("a b -> b a"), "G")
            b.wtok(g, tok)

        @block.vector
        def _(v):
            b.wait_allq(v, "G")
            v.drain()
            b.inc(v.reduce_max(out=st[0:1, 64:65], in_=st[0:1, 65:65 + NCORES],
                               axis=AX.X), "sV", 1)

    def emit_act_quant_scalars(self, gmax_cell, clip_cell, inv_col, fac_col, fac_scale):
        b, block = self.b, self.block
        st = self.stat

        @block.scalar
        def _(sc):
            b.wait(sc, "sV")
            b.inc(sc.activation(out=st[0:1, 70:71], in_=clip_cell, func=AF.Sigmoid,
                                bias=self.consts[0:1, 0:1], scale=1.0), "sS", 1)

        @block.vector
        def _(v):
            b.wait(v, "sS")
            v.drain()
            v.tensor_tensor(out=st[0:1, 71:72], in0=gmax_cell, in1=st[0:1, 70:71],
                            op=OP.mult)
            v.drain()
            v.tensor_scalar(out=st[0:1, 72:73], in0=st[0:1, 71:72], scalar1=1.0 / 127.0,
                            scalar2=None, op0=OP.mult)
            v.drain()
            v.reciprocal(out=st[0:1, 73:74], in_=st[0:1, 72:73])
            v.drain()
            b.inc(v.tensor_scalar(out=st[0:1, 74:75], in0=st[0:1, 72:73],
                                  scalar1=fac_scale, scalar2=None, op0=OP.mult), "sV", 1)
        vc = b.cnt["sV"]

        @block.gpsimd
        def _(g):
            g.wait_ge(b.sems["sV"], vc)
            tok = b.dma(g, self.dsc["vec"].ap()[0:1, 0:2], st[0:1, 73:75], "G")
            b.wtok(g, tok)
            tok = b.dma(g, self.stat2col[:, 0:2],
                        self.dsc["vec"].ap()[0:1, 0:2].partition_broadcast(128), "G")
            b.wtok(g, tok)

        @block.vector
        def _(v):
            b.wait_allq(v, "G")
            v.drain()
            v.tensor_copy(out=inv_col, in_=self.stat2col[:, 0:1])
            v.drain()
            b.inc(v.tensor_copy(out=fac_col, in_=self.stat2col[:, 1:2]), "sV", 1)

    def ph_init(self):
        b, block, din = self.b, self.block, self.din

        @block.sync
        def _(q):
            b.gbegin("S", q)
            b.dinc(q.dma_start(out=self.X[:].rearrange("p (c t) -> p c t", c=CC),
                               in_=din["x_t"].ap().rearrange("(c p) t -> p c t", p=128)), "S")
            b.dinc(q.dma_start(out=self.ident[:], in_=din["ident"].ap()), "S")
            b.dinc(q.dma_start(out=self.tri[:], in_=din["tri"].ap()), "S")
            for i, nm in enumerate(["bq", "bk", "bv", "bo", "ln_g", "ln_b"]):
                b.dinc(q.dma_start(out=self.biases[:, i * CC:(i + 1) * CC],
                                   in_=din[nm].ap()), "S")
            b.dinc(q.dma_start(out=self.clip_sb[0:1, 0:1], in_=din["clip_x"].ap()), "S")
            b.dinc(q.dma_start(out=self.clip_sb[0:1, 1:2], in_=din["clip_down"].ap()), "S")

        @block.vector
        def _(v):
            v.memset(self.onecol_bf[:], 1.0)
            v.memset(self.ones_bf[:], 1.0)
            v.memset(self.consts[:], 0.0)
            v.drain()
            v.memset(self.consts[:, 1:2], 1.0)
            v.drain()
            v.memset(self.consts[:, 2:3], EPS_LN)
            v.drain()
            v.memset(self.consts[:, 3:4], 128.5)
            v.drain()
            v.memset(self.consts[:, 4:5], -10.0)
            v.drain()
            b.inc(v.memset(self.consts[:, 5:6], -50.0), "sV", 1)

    def ph_wquant(self):
        b, block, din, dsc = self.b, self.block, self.din, self.dsc
        b.push()
        jobs = [("w_gate", "qwg", 0, 22, 1024), ("w_up", "qwu", 22, 44, 1024),
                ("w_down", "qwd", 44, 52, 2816)]
        wt = [b.scr(f"wt{i}", [128, 2816], F32) for i in range(2)]
        u8t = b.scr("wu8", [128, 2816], U8)
        qbt = [b.scr(f"wqb{i}", [128, 2816], BF16) for i in range(2)]
        ld_t, v_t, st_t = {}, {}, {}
        it = 0
        for (src, dst, c0, c1, W) in jobs:
            for r in range(c1 - c0):
                i = it; it += 1
                bi = i % 2

                @block.sync
                def _(q, src=src, r=r, W=W, bi=bi, i=i):
                    if i >= 2:
                        q.wait_ge(b.sems["sV"], v_t[i - 2])
                    ld_t[i] = b.dma(q, wt[bi][:, 0:W],
                                    din[src].ap()[128 * r:128 * (r + 1), :], "S")

                @block.vector
                def _(v, c0=c0, r=r, W=W, bi=bi, i=i):
                    b.wtok(v, ld_t[i])
                    if i >= 2:
                        b.wtok(v, st_t[i - 2])
                    v.drain()
                    v.reduce_max(out=self.wsc[:, c0 + r:c0 + r + 1], in_=wt[bi][:, 0:W],
                                 axis=AX.X, apply_absolute_value=True)
                    v.drain()
                    v.reciprocal(out=self.stat[:, 2:3], in_=self.wsc[:, c0 + r:c0 + r + 1])
                    v.drain()
                    v.tensor_scalar(out=self.stat[:, 3:4], in0=self.stat[:, 2:3],
                                    scalar1=127.0, scalar2=None, op0=OP.mult)
                    v.drain()
                    v.tensor_scalar(out=u8t[:, 0:W], in0=wt[bi][:, 0:W],
                                    scalar1=self.stat[:, 3:4], scalar2=128.0,
                                    op0=OP.mult, op1=OP.add)
                    v.drain()
                    b.inc(v.tensor_scalar(out=qbt[bi][:, 0:W], in0=u8t[:, 0:W],
                                          scalar1=128.0, scalar2=None, op0=OP.subtract),
                          "sV", 1)
                    v_t[i] = b.cnt["sV"]

                @block.gpsimd
                def _(g, dst=dst, r=r, W=W, bi=bi, i=i):
                    g.wait_ge(b.sems["sV"], v_t[i])
                    st_t[i] = b.dma(g, dsc[dst].ap()[128 * r:128 * (r + 1), :],
                                    qbt[bi][:, 0:W], "G")
        self.wq_store_done = [st_t[it - 1 - k] for k in range(min(NROT, it))]
        self.tap_now("qwg", dsc["qwg"].ap())
        b.pop()

    def ph_l0ffn(self, call):
        b, block, din, dsc, pb = self.b, self.block, self.din, self.dsc, self.pb
        X, st = self.X, self.stat
        b.push()
        rb = b.scr("rb", [128, T], F32)
        cb = b.scr("cb", [128, T], F32)
        b.push()
        xbf = b.scr("xbf", [128, CC * T], BF16)
        x2bf = b.scr("x2bf", [128, CC * T], BF16)
        self.emit_xbf_copies(xbf, x2bf)
        self.emit_ln_stats(xbf, x2bf, rb, cb)
        b.pop()
        xq = b.scr("xq", [128, CC * T], BF16)
        xtmp = b.scr("xtmp", [128, T], F32)
        u8t = b.scr("u8t", [128, T], U8)

        @block.vector
        def _(v):
            v.drain()
            v.memset(st[:, 0:1], 0.0)
            for cc in range(CC):
                v.drain()
                v.tensor_tensor(out=xtmp[:], in0=X[:, T * cc:T * (cc + 1)], in1=rb[:],
                                op=OP.mult)
                v.drain()
                v.tensor_tensor(out=xtmp[:], in0=xtmp[:], in1=cb[:], op=OP.subtract)
                v.drain()
                v.reduce_max(out=st[:, 1:2], in_=xtmp[:], axis=AX.X,
                             apply_absolute_value=True)
                v.drain()
                inst = v.tensor_tensor(out=st[:, 0:1], in0=st[:, 0:1], in1=st[:, 1:2],
                                       op=OP.max)
            b.inc(inst, "sV", 1)
        self.emit_flat_allreduce_max(st[:, 0:1], st)
        self.emit_act_quant_scalars(st[0:1, 64:65], self.clip_sb[0:1, 0:1],
                                    st[:, 5:6], st[:, 6:7], 1.0 / 127.0)

        @block.vector
        def _(v):
            b.wait(v, "sV")
            v.drain()
            v.tensor_scalar(out=st[:, 8:30], in0=self.wsc[:, 0:22],
                            scalar1=st[:, 6:7], scalar2=None, op0=OP.mult)
            v.drain()
            b.inc(v.tensor_scalar(out=st[:, 30:52], in0=self.wsc[:, 22:44],
                                  scalar1=st[:, 6:7], scalar2=None, op0=OP.mult), "sV", 1)

        xq_cc_t = {}

        @block.vector
        def _(v):
            for cc in range(CC):
                v.drain()
                v.tensor_tensor(out=xtmp[:], in0=X[:, T * cc:T * (cc + 1)], in1=rb[:],
                                op=OP.mult)
                v.drain()
                v.tensor_tensor(out=xtmp[:], in0=xtmp[:], in1=cb[:], op=OP.subtract)
                v.drain()
                v.tensor_scalar(out=u8t[:], in0=xtmp[:], scalar1=st[:, 5:6],
                                scalar2=128.0, op0=OP.mult, op1=OP.add)
                v.drain()
                b.inc(v.tensor_scalar(out=xq[:, T * cc:T * (cc + 1)], in0=u8t[:],
                                      scalar1=128.0, scalar2=None, op0=OP.subtract),
                      "sV", 1)
                xq_cc_t[cc] = b.cnt["sV"]
        xq_ready = b.cnt["sV"]
        self.tap_now("xq", xq[:], wait=("sV",))
        self.tap_now("rb", rb[:], wait=("sV",))
        self.tap_now("cb", cb[:], wait=("sV",))
        self.tap_now("gmaxrow", st[0:1, 64:80], wait=("sV",))

        gblk = [b.scr(f"gblk{c}", [128, 1408], BF16) for c in range(CC)]
        ublk = [b.scr(f"ublk{c}", [128, 1408], BF16) for c in range(CC)]
        sg4 = b.scr("sg4", [128, 4 * 512], BF16)
        htl = [b.scr(f"htl{i}", [128, T], F32) for i in range(2)]
        ICG, ICN = 2, 11
        half_mm_t = {}
        half_ev_t = {}
        h_st_t = {}
        wld_t = {}

        @block.vector
        def _(v):
            v.drain()
            b.inc(v.memset(st[:, 2:3], 0.0), "sV", 1)

        for icg in range(ICG):
            @block.sync
            def _(q, icg=icg):
                if icg >= 1:
                    q.wait_ge(b.sems["sT"], half_mm_t[(icg - 1, ICN - 1, 3)])
                else:
                    b.wait(q, "sT")
                for tk in self.wq_store_done:
                    b.wtok(q, tk)
                r0 = icg * 1408
                b.gbegin("S", q)
                for cc in range(CC):
                    b.dinc(q.dma_start_transpose(
                        gblk[cc][:], dsc["qwg"].ap()[r0:r0 + 1408,
                                                     128 * cc:128 * (cc + 1)]), "S")
                    b.dinc(q.dma_start_transpose(
                        ublk[cc][:], dsc["qwu"].ap()[r0:r0 + 1408,
                                                     128 * cc:128 * (cc + 1)]), "S")
                wld_t[icg] = b.gend("S")

            for ic in range(ICN):
                icga = icg * ICN + ic
                for half in range(4):
                    bG = pb[2 * (half % 4)]
                    bU = pb[2 * (half % 4) + 1]

                    @block.tensor
                    def _(t, icg=icg, ic=ic, half=half, bG=bG, bU=bU, icga=icga):
                        b.wtok(t, wld_t[icg])
                        prev = ((icg, ic, half - 4) if half >= 4 else
                                ((icg, ic - 1, half) if ic > 0
                                 else ((icg - 1, ICN - 1, half) if icg > 0 else None)))
                        if prev is not None and prev in half_ev_t:
                            t.wait_ge(b.sems["sV"], half_ev_t[prev])
                        tc = half
                        for wblk, bank in ((gblk, bG), (ublk, bU)):
                            for cc in range(CC):
                                if icg == 0 and ic == 0 and wblk is gblk:
                                    t.wait_ge(b.sems["sV"], xq_cc_t[cc])
                                lhsT = wblk[cc][:, 128 * ic:128 * (ic + 1)]
                                inst = t.matmul(
                                    bank[:, :], lhsT,
                                    xq[:, T * cc + 512 * tc:T * cc + 512 * tc + 512],
                                    start=(cc == 0), stop=(cc == CC - 1))
                        b.inc(inst, "sT", 1)
                        half_mm_t[(icg, ic, half)] = b.cnt["sT"]

                    @block.scalar
                    def _(sc, icg=icg, ic=ic, half=half, bG=bG, icga=icga):
                        sc.wait_ge(b.sems["sT"], half_mm_t[(icg, ic, half)])
                        tc = half
                        inst = sc.activation(out=sg4[:, 512 * tc:512 * (tc + 1)],
                                             in_=bG[:, :], func=AF.Silu,
                                             scale=st[:, 8 + icga:9 + icga])
                        b.inc(inst, "sS", 1)

                    @block.vector
                    def _(v, icg=icg, ic=ic, half=half, bU=bU, icga=icga):
                        b.wait(v, "sS")
                        if icga >= 2 and half == 0:
                            b.wtok(v, h_st_t[icga - 2])
                        tc = half
                        v.drain()
                        v.scalar_tensor_tensor(
                            out=htl[icga % 2][:, 512 * tc:512 * (tc + 1)],
                            in0=bU[:, :], scalar=st[:, 30 + icga:31 + icga],
                            in1=sg4[:, 512 * tc:512 * (tc + 1)],
                            op0=OP.mult, op1=OP.mult)
                        v.drain()
                        v.reduce_max(out=st[:, 1:2],
                                     in_=htl[icga % 2][:, 512 * tc:512 * (tc + 1)],
                                     axis=AX.X, apply_absolute_value=True)
                        v.drain()
                        inst = v.tensor_tensor(out=st[:, 2:3], in0=st[:, 2:3],
                                               in1=st[:, 1:2], op=OP.max)
                        b.inc(inst, "sV", 1)
                        half_ev_t[(icg, ic, half)] = b.cnt["sV"]

                @block.gpsimd
                def _(g, icg=icg, ic=ic, icga=icga):
                    g.wait_ge(b.sems["sV"], half_ev_t[(icg, ic, 3)])
                    h_st_t[icga] = b.dma(
                        g, dsc["hsp"].ap()[128 * icga:128 * (icga + 1), :],
                        htl[icga % 2][:], "G")

        @block.gpsimd
        def _(g):
            if "hrows" in self.taps:
                for tk in h_st_t.values():
                    b.wtok(g, tk)
        self.tap_now("hrows", dsc["hsp"].ap()[0:256, :], wait=())

        @block.vector
        def _(v):
            v.drain()
            b.inc(v.tensor_copy(out=st[:, 0:1], in_=st[:, 2:3]), "sV", 1)
        self.emit_flat_allreduce_max(st[:, 0:1], st)
        self.emit_act_quant_scalars(st[0:1, 64:65], self.clip_sb[0:1, 1:2],
                                    st[:, 5:6], st[:, 6:7], 0.5 / 127.0)

        @block.vector
        def _(v):
            b.wait(v, "sV")
            v.drain()
            b.inc(v.tensor_scalar(out=st[:, 8:16], in0=self.wsc[:, 44:52],
                                  scalar1=st[:, 6:7], scalar2=None, op0=OP.mult), "sV", 1)
        dsc_ready = b.cnt["sV"]
        b.pop()

        b.push()
        ICN2 = 11
        hq = b.scr("hq", [128, ICN2 * T], BF16)
        wdb = [b.scr(f"wdb{i}", [128, 1024], BF16) for i in range(ICN2)]
        ht2 = [b.scr(f"ht2{i}", [128, T], F32) for i in range(2)]
        u8b = b.scr("u8b", [128, T], U8)
        hq_q_t = {}
        hl_t = {}
        dn_mm_t = {}
        dn_ev_t = {}
        for rnd in range(2):
            for ic in range(ICN2):
                icga = rnd * ICN2 + ic

                @block.gpsimd
                def _(g, rnd=rnd, ic=ic, icga=icga):
                    b.wtok(g, h_st_t[icga])
                    if icga >= 2:
                        g.wait_ge(b.sems["sV"], hq_q_t[icga - 2])
                    if icga < 2:
                        g.wait_ge(b.sems["sT"], half_mm_t[(ICG - 1, ICN - 1, 1)])
                    if rnd == 1 and ic == 0:
                        g.wait_ge(b.sems["sT"], dn_mm_t[(0, 3)])
                    hl_t[icga] = b.dma(
                        g, ht2[icga % 2][:],
                        dsc["hsp"].ap()[128 * icga:128 * (icga + 1), :], "G")

                @block.vector
                def _(v, ic=ic, icga=icga):
                    b.wtok(v, hl_t[icga])
                    v.wait_ge(b.sems["sV"], dsc_ready)
                    v.drain()
                    v.tensor_scalar(out=u8b[:], in0=ht2[icga % 2][:], scalar1=st[:, 5:6],
                                    scalar2=128.0, op0=OP.mult, op1=OP.add)
                    v.drain()
                    b.inc(v.tensor_scalar(out=hq[:, T * ic:T * (ic + 1)], in0=u8b[:],
                                          scalar1=128.0, scalar2=None, op0=OP.subtract),
                          "sV", 1)
                    hq_q_t[icga] = b.cnt["sV"]

            @block.sync
            def _(q, rnd=rnd):
                if rnd == 1:
                    q.wait_ge(b.sems["sT"], dn_mm_t[(0, 3)])
                else:
                    q.wait_ge(b.sems["sT"], half_mm_t[(ICG - 1, ICN - 1, 3)])
                b.gbegin("S", q)
                for ic in range(ICN2):
                    icga = rnd * ICN2 + ic
                    b.dinc(q.dma_start_transpose(
                        wdb[ic][:, :],
                        dsc["qwd"].ap()[:, 128 * icga:128 * (icga + 1)]), "S")
            wdl = b.gend("S")

            for sw in range(4):
                hcg, tcg = sw // 2, sw % 2

                @block.tensor
                def _(t, rnd=rnd, sw=sw, hcg=hcg, tcg=tcg, wdl=wdl):
                    b.wtok(t, wdl)
                    t.wait_ge(b.sems["sV"], hq_q_t[rnd * ICN2 + ICN2 - 1])
                    prev = (rnd, sw - 1) if sw > 0 else ((rnd - 1, 3) if rnd > 0 else None)
                    if prev is not None:
                        t.wait_ge(b.sems["sV"], dn_ev_t[prev])
                    for ic in range(ICN2):
                        for hj in range(4):
                            hc = 4 * hcg + hj
                            for tj in range(2):
                                tc = 2 * tcg + tj
                                inst = t.matmul(
                                    pb[2 * hj + tj][:, :],
                                    wdb[ic][:, 128 * hc:128 * (hc + 1)],
                                    hq[:, T * ic + 512 * tc:T * ic + 512 * tc + 512],
                                    start=(ic == 0), stop=(ic == ICN2 - 1))
                    b.inc(inst, "sT", 1)
                    dn_mm_t[(rnd, sw)] = b.cnt["sT"]

                @block.vector
                def _(v, rnd=rnd, sw=sw, hcg=hcg, tcg=tcg):
                    v.wait_ge(b.sems["sT"], dn_mm_t[(rnd, sw)])
                    for hj in range(4):
                        hc = 4 * hcg + hj
                        for tj in range(2):
                            tc = 2 * tcg + tj
                            v.drain()
                            inst = v.scalar_tensor_tensor(
                                out=X[:, T * hc + 512 * tc:T * hc + 512 * tc + 512],
                                in0=pb[2 * hj + tj][:, :], scalar=st[:, 8 + hc:9 + hc],
                                in1=X[:, T * hc + 512 * tc:T * hc + 512 * tc + 512],
                                op0=OP.mult, op1=OP.add)
                    b.inc(inst, "sV", 1)
                    dn_ev_t[(rnd, sw)] = b.cnt["sV"]
        b.pop()

    def emit_ln_bf16_apply(self, xbf, rb, cb, rbbf, cbbf):
        b, block = self.b, self.block

        @block.vector
        def _(v):
            b.wait(v, "sS")
            v.drain()
            v.tensor_copy(out=rbbf[:], in_=rb[:])
            v.drain()
            v.tensor_copy(out=cbbf[:], in_=cb[:])
            v.drain()
            for cc in range(CC):
                sl = slice(T * cc, T * (cc + 1))
                v.tensor_tensor(out=xbf[:, sl], in0=xbf[:, sl], in1=rbbf[:], op=OP.mult)
                v.drain()
                inst = v.tensor_tensor(out=xbf[:, sl], in0=xbf[:, sl], in1=cbbf[:],
                                       op=OP.subtract)
                v.drain()
            b.inc(inst, "sV", 1)

    def ph_l0att(self):
        b, block, din, dsc, pb, PBt = self.b, self.block, self.din, self.dsc, self.pb, self.PBt
        X, st = self.X, self.stat
        NB = 32
        b.push()
        xbf = b.scr("xbf2", [128, CC * T], BF16)
        b.push()
        x2bf = b.scr("x2bf2", [128, CC * T], BF16)
        rb = b.scr("rb2", [128, T], F32)
        cb = b.scr("cb2", [128, T], F32)
        rbbf = b.scr("rbbf", [128, T], BF16)
        cbbf = b.scr("cbbf", [128, T], BF16)
        self.emit_xbf_copies(xbf, x2bf)
        self.emit_ln_stats(xbf, x2bf, rb, cb)
        self.emit_ln_bf16_apply(xbf, rb, cb, rbbf, cbbf)
        b.pop()
        xn = xbf
        xn_ready = b.cnt["sV"]

        acc = b.scr("acc", [128, 8 * 66], F32)
        accbf = b.scr("accbf", [128, 8 * 66], BF16)
        sst = b.scr("sst", [128, 512], F32)
        qw = [b.scr_at(f"qw{i}", [128, 128], BF16, "acc") if i < 8 else
              b.scr_at(f"qw{i}", [128, 128], BF16, "accbf") for i in range(0)]
        ktm = b.scr("ktm", [128, NT * 1024], BF16)
        vtm = b.scr("vtm", [128, NT * 1024], BF16)
        kv_done = {}

        b.push()
        kvw = [b.scr(f"kvw{c}", [128, 1024], BF16) for c in range(CC)]
        for which, dst in (("k", ktm), ("v", vtm)):
            col0 = 1024 if which == "k" else 2048
            wtoks = []

            @block.gpsimd
            def _(g, which=which, col0=col0, wtoks=wtoks):
                b.wait(g, "sT")
                if which == "k":
                    b.wait(g, "sV")
                for c in range(CC):
                    wtoks.append(b.dma(
                        g, kvw[c][:],
                        din["w_qkv_t"].ap()[128 * c:128 * (c + 1), col0:col0 + 1024], "G"))

            mm_t = {}
            ev_t = {}
            for tt in range(NT):
                bk0 = pb[4 + 2 * (tt % 2)]
                bk1 = pb[5 + 2 * (tt % 2)]

                @block.tensor
                def _(t, which=which, tt=tt, bk0=bk0, bk1=bk1, wtoks=wtoks):
                    if tt == 0:
                        t.wait_ge(b.sems["sV"], xn_ready)
                        for tk in wtoks:
                            b.wtok(t, tk)
                    if tt < 2 and which == "v":
                        t.wait_ge(b.sems["sS"], kv_done["k"][1])
                    if tt >= 2:
                        t.wait_ge(b.sems["sS"], ev_t[tt - 2])
                    for c in range(CC):
                        lhsT = xn[:, T * c + 128 * tt:T * c + 128 * (tt + 1)]
                        t.matmul(bk0[:, :], lhsT, kvw[c][:, 0:512],
                                 start=(c == 0), stop=(c == CC - 1))
                        inst = t.matmul(bk1[:, :], lhsT, kvw[c][:, 512:1024],
                                        start=(c == 0), stop=(c == CC - 1))
                    b.inc(inst, "sT", 1)
                    mm_t[tt] = b.cnt["sT"]

                @block.scalar
                def _(sc, which=which, tt=tt, bk0=bk0, bk1=bk1):
                    sc.wait_ge(b.sems["sT"], mm_t[tt])
                    if which == "k":
                        o = ktm[:, 1024 * tt:1024 * (tt + 1)]
                        sc.activation(out=o[:, 0:512], in_=bk0[:, :], func=AF.Exp,
                                      bias=self.consts[:, 0:1], scale=1.0)
                        inst = sc.activation(out=o[:, 512:1024], in_=bk1[:, :], func=AF.Exp,
                                             bias=self.consts[:, 0:1], scale=1.0)
                    else:
                        o = vtm[:, 1024 * tt:1024 * (tt + 1)]
                        sc.copy(out=o[:, 0:512], in_=bk0[:, :])
                        inst = sc.copy(out=o[:, 512:1024], in_=bk1[:, :])
                    b.inc(inst, "sS", 1)
                    ev_t[tt] = b.cnt["sS"]
            kv_done[which] = (mm_t[NT - 1], b.cnt["sS"])
        b.pop()

        b.push()
        expq = b.scr("expq", [128, CC * T], BF16)
        qwbase = self.nc.alloc_sbuf_tensor_at(
            "scr_qwall", [128, 16 * 128], BF16,
            offset=b.scratch_addr + b.offs["acc"][0])
        qw = [qwbase[:, 128 * i:128 * (i + 1)] for i in range(16)]
        q_mm_t = {}
        q_ev_t = {}
        qw_tok = {}
        for hp in range(CC):
            @block.gpsimd
            def _(g, hp=hp):
                if hp >= 2:
                    g.wait_ge(b.sems["sT"], q_mm_t[hp - 2])
                else:
                    b.wait(g, "sT")
                toks = []
                for c in range(CC):
                    toks.append(b.dma(
                        g, qw[8 * (hp % 2) + c][:],
                        din["w_qkv_t"].ap()[128 * c:128 * (c + 1),
                                            128 * hp:128 * (hp + 1)], "G"))
                qw_tok[hp] = toks

            bks = [pb[4 * (hp % 2) + j] for j in range(4)]

            @block.tensor
            def _(t, hp=hp, bks=bks):
                for tk in qw_tok[hp]:
                    b.wtok(t, tk)
                if hp < 2:
                    t.wait_ge(b.sems["sS"], kv_done["v"][1])
                if hp >= 2:
                    t.wait_ge(b.sems["sS"], q_ev_t[hp - 2])
                for c in range(CC):
                    lhsT = qw[8 * (hp % 2) + c][:]
                    for tc in range(TC):
                        inst = t.matmul(bks[tc][:, :], lhsT,
                                        xn[:, T * c + 512 * tc:T * c + 512 * (tc + 1)],
                                        start=(c == 0), stop=(c == CC - 1))
                b.inc(inst, "sT", 1)
                q_mm_t[hp] = b.cnt["sT"]

            @block.scalar
            def _(sc, hp=hp, bks=bks):
                sc.wait_ge(b.sems["sT"], q_mm_t[hp])
                for tc in range(TC):
                    inst = sc.activation(out=expq[:, T * hp + 512 * tc:T * hp + 512 * (tc + 1)],
                                         in_=bks[tc][:, :], func=AF.Exp,
                                         bias=self.consts[:, 0:1], scale=1.0)
                b.inc(inst, "sS", 1)
                q_ev_t[hp] = b.cnt["sS"]

        attnstore = xn

        @block.vector
        def _(v):
            v.wait_ge(b.sems["sS"], q_ev_t[CC - 1])
            v.wait_ge(b.sems["sS"], kv_done["v"][1])
            v.drain()
            v.memset(acc[:], 0.0)
            v.drain()
            v.memset(accbf[:], 0.0)
            v.drain()
            v.memset(acc.rearrange("p (h z) -> p h z", h=8)[:, :, 65:66], 1.0)
            v.drain()
            b.inc(v.memset(accbf.rearrange("p (h z) -> p h z", h=8)[:, :, 65:66], 1.0),
                  "sV", 1)
        acc_ready = b.cnt["sV"]

        att_mm_t = {}
        ctx_mm_t = {}
        v1_t = {}
        v2_t = {}
        sst_dma_t = {}
        for n in range(NB):
            r = n % 2
            pctx = PBt[:, 1024 * r:1024 * r + 512]
            patt = PBt[:, 2048 + 1024 * r:2048 + 1024 * r + 512]
            psts = PBt[:, 2560 + 1024 * r:2560 + 1024 * r + 512]
            psum_k = PBt[:, 1024 * r + 512:1024 * r + 512 + 8]
            tt, half = n // 2, n % 2
            krows = slice(64 * half, 64 * half + 64)

            @block.tensor
            def _(t, n=n, pctx=pctx, patt=patt, psts=psts, psum_k=psum_k,
                  tt=tt, krows=krows):
                if n == 0:
                    t.wait_ge(b.sems["sV"], acc_ready)
                if n >= 1:
                    t.wait_ge(b.sems["sV"], v1_t[n - 1])
                if n >= 2:
                    t.wait_ge(b.sems["sV"], v2_t[n - 2])
                for h in range(16):
                    hp, par = h // 2, h % 2
                    qrows = slice(64 * par, 64 * par + 64)
                    rhs = expq[qrows, T * hp + 64 * n:T * hp + 64 * (n + 1)]
                    acol = accbf[qrows, 66 * hp:66 * hp + 66]
                    rbase = 32 * (h // 4)
                    cbase = 128 * (h % 4)
                    t.matmul(patt[slice(64 * par, 64 * par + 64),
                                  64 * hp:64 * (hp + 1)],
                             acol[:, 0:64], rhs, start=True, stop=True)
                    t.matmul(psts[rbase:rbase + 1, cbase:cbase + 64],
                             acol[:, 64:65], rhs, start=True, stop=True,
                             tile_position=(64 * par, rbase))
                    inst = t.matmul(psts[rbase:rbase + 1, cbase + 64:cbase + 128],
                                    acol[:, 65:66], rhs, start=True, stop=True,
                                    tile_position=(64 * par, rbase))
                b.inc(inst, "sT", 1)
                att_mm_t[n] = b.cnt["sT"]
                for h in range(16):
                    hp, par = h // 2, h % 2
                    kl = ktm[krows, 1024 * tt + 64 * h:1024 * tt + 64 * (h + 1)]
                    vl = vtm[krows, 1024 * tt + 64 * h:1024 * tt + 64 * (h + 1)]
                    ol = pctx[slice(64 * par, 64 * par + 64), 64 * hp:64 * (hp + 1)]
                    t.matmul(ol, kl, vl, start=True, stop=True)
                    inst = t.matmul(psum_k[slice(64 * par, 64 * par + 64), hp:hp + 1],
                                    kl, self.onecol_bf[krows, 0:1],
                                    start=True, stop=True)
                b.inc(inst, "sT", 1)
                ctx_mm_t[n] = b.cnt["sT"]

            @block.vector
            def _(v, n=n, pctx=pctx, psum_k=psum_k):
                v.wait_ge(b.sems["sT"], ctx_mm_t[n])
                v.drain()
                acc3 = acc.rearrange("p (h z) -> p h z", h=8)
                v.tensor_tensor(out=acc3[:, :, 0:64], in0=acc3[:, :, 0:64],
                                in1=pctx.rearrange("p (h z) -> p h z", h=8), op=OP.add)
                v.drain()
                v.tensor_tensor(out=acc3[:, :, 64:65], in0=acc3[:, :, 64:65],
                                in1=psum_k.rearrange("p (h z) -> p h z", h=8), op=OP.add)
                v.drain()
                inst = v.tensor_copy(
                    out=accbf.rearrange("p (h z) -> p h z", h=8)[:, :, 0:65],
                    in_=acc3[:, :, 0:65])
                v.drain()
                b.inc(inst, "sV", 1)
                v1_t[n] = b.cnt["sV"]

            @block.vector
            def _(v, n=n, patt=patt, psts=psts):
                if n >= 1:
                    b.wtok(v, sst_dma_t[n - 1])
                v.drain()
                a3 = attnstore.rearrange("p (hp tcol) -> p hp tcol", hp=8)
                v.tensor_copy(out=a3[:, :, 64 * n:64 * (n + 1)],
                              in_=patt.rearrange("p (hp z) -> p hp z", hp=8))
                v.drain()
                inst = v.tensor_copy(out=sst[:], in_=psts[:, :])
                v.drain()
                b.inc(inst, "sV", 1)
                v2_t[n] = b.cnt["sV"]

            @block.gpsimd
            def _(g, n=n):
                g.wait_ge(b.sems["sV"], v2_t[n])
                sst_dma_t[n] = b.dma(
                    g, dsc["statsd"].ap()[:, 512 * n:512 * (n + 1)], sst[:], "G")

        s2 = b.scr_at("s2", [128, 512], F32, "sst")
        minvt = b.scr_at("minvt", [128, 256], F32, "accbf")

        @block.gpsimd
        def _(g):
            for n in range(NB):
                b.wtok(g, sst_dma_t[n])
            src = dsc["statsd"].ap().rearrange("(rg s) c -> rg s c", rg=4)[:, 0:1, :] \
                .squeeze(1)
            src = src.rearrange("rg (n c) -> rg n c", n=NB)
            self._l0_sst_rd = b.dma(g, s2[:], src, "G")

        @block.vector
        def _(v):
            b.wtok(v, self._l0_sst_rd)
            v.drain()
            s3 = s2[:].rearrange("p (hs w t) -> p hs w t", hs=4, w=2)
            m3 = minvt[:].rearrange("p (hs t) -> p hs t", hs=4)
            v.scalar_tensor_tensor(out=m3, in0=s3[:, :, 1, :], scalar=8e-6,
                                   in1=s3[:, :, 0, :], op0=OP.mult, op1=OP.max)
            v.drain()
            b.inc(v.reciprocal(out=minvt[:], in_=minvt[:]), "sV", 1)
        minv_ready = b.cnt["sV"]

        @block.gpsimd
        def _(g):
            g.wait_ge(b.sems["sV"], minv_ready)
            toks = []
            for rg in range(4):
                dst = dsc["minvd"].ap()[4 * rg:4 * rg + 4, :] \
                    .rearrange("hs (n t) -> n hs t", n=NB)
                toks.append(b.dma(
                    g, dst,
                    minvt[32 * rg:32 * (rg + 1), :].rearrange(
                        "n (hs t) -> n hs t", hs=4), "G"))
            self._l0_minv_wr = toks

        b.pop()
        gsc = [b.scr(f"gsc{i}", [128, T], BF16) for i in range(2)]
        gsc_tok = {}
        scale_t = {}
        for hp in range(CC):
            @block.gpsimd
            def _(g, hp=hp):
                if hp == 0:
                    for tk in self._l0_minv_wr:
                        b.wtok(g, tk)
                    b.wait(g, "sT")
                if hp >= 2:
                    g.wait_ge(b.sems["sV"], scale_t[hp - 2])
                src = dsc["minvd"].ap()[2 * hp:2 * hp + 2, :]
                gsc_tok[hp] = b.dma(
                    g, gsc[hp % 2][:],
                    src.unsqueeze(1).broadcast_to([2, 64, T]), "G")

            @block.vector
            def _(v, hp=hp):
                b.wtok(v, gsc_tok[hp])
                v.drain()
                sl = slice(T * hp, T * (hp + 1))
                b.inc(v.tensor_tensor(out=attnstore[:, sl], in0=attnstore[:, sl],
                                      in1=gsc[hp % 2][:], op=OP.mult), "sV", 1)
                scale_t[hp] = b.cnt["sV"]

        wob = [b.scr(f"wob{i}", [128, 1024], BF16) for i in range(CC)]

        @block.gpsimd
        def _(g):
            b.wait(g, "sT")
            toks = []
            for c in range(CC):
                toks.append(b.dma(g, wob[c][:],
                                  din["w_out_t"].ap()[128 * c:128 * (c + 1), :], "G"))
            self._l0_wo_tok = toks

        wo_mm_t = {}
        wo_ev_t = {}
        for dc in range(CC):
            bks = [pb[4 * (dc % 2) + j] for j in range(4)]

            @block.tensor
            def _(t, dc=dc, bks=bks):
                if dc == 0:
                    for tk in self._l0_wo_tok:
                        b.wtok(t, tk)
                    t.wait_ge(b.sems["sV"], scale_t[CC - 1])
                if dc >= 2:
                    t.wait_ge(b.sems["sV"], wo_ev_t[dc - 2])
                for c in range(CC):
                    lhsT = wob[c][:, 128 * dc:128 * (dc + 1)]
                    for tc in range(TC):
                        inst = t.matmul(bks[tc][:, :], lhsT,
                                        attnstore[:, T * c + 512 * tc:T * c + 512 * (tc + 1)],
                                        start=(c == 0), stop=(c == CC - 1))
                b.inc(inst, "sT", 1)
                wo_mm_t[dc] = b.cnt["sT"]

            @block.vector
            def _(v, dc=dc, bks=bks):
                v.wait_ge(b.sems["sT"], wo_mm_t[dc])
                for tc in range(TC):
                    v.drain()
                    inst = v.scalar_tensor_tensor(
                        out=X[:, T * dc + 512 * tc:T * dc + 512 * (tc + 1)],
                        in0=bks[tc][:, :], scalar=1.0,
                        in1=X[:, T * dc + 512 * tc:T * dc + 512 * (tc + 1)],
                        op0=OP.mult, op1=OP.add)
                b.inc(inst, "sV", 1)
                wo_ev_t[dc] = b.cnt["sV"]
        b.pop()

    def ph_l1stash(self):
        b, block, din, dsc = self.b, self.block, self.din, self.dsc

        @block.gpsimd
        def _(g):
            toks = []
            for e in range(8):
                toks.append(b.dma(g, dsc["ved"].ap()[e], din["ve_r"].ap()[e], "G"))
            for j in range(8):
                toks.append(b.dma(g, dsc["oed"].ap()[512 * j:512 * (j + 1), :],
                                  din["oe_flat"].ap()[512 * j:512 * (j + 1), :], "G"))
            self.l1stash_tok = toks

    def ph_l1ffn(self, call):
        b, block, din, dsc, pb, PBt = self.b, self.block, self.din, self.dsc, self.pb, self.PBt
        X, st = self.X, self.stat
        b.push()
        xbf = b.scr("xbf3", [128, CC * T], BF16)
        b.push()
        x2bf = b.scr("x2bf3", [128, CC * T], BF16)
        rb = b.scr("rb3", [128, T], F32)
        cb = b.scr("cb3", [128, T], F32)
        rbbf = b.scr("rbbf3", [128, T], BF16)
        cbbf = b.scr("cbbf3", [128, T], BF16)
        self.emit_xbf_copies(xbf, x2bf)
        self.emit_ln_stats(xbf, x2bf, rb, cb)
        self.emit_ln_bf16_apply(xbf, rb, cb, rbbf, cbbf)
        b.pop()
        xn = xbf
        xn_ready = b.cnt["sV"]

        gates = b.scr("gates", [128, NT * 64], F32)
        maskbf = b.scr("maskbf", [128, NT * 64], BF16)
        qkT = b.scr("qkT", [128, 8 * T], BF16)
        values = b.scr("valuesl1", [128, NT * 520], BF16)

        b.push()
        wg1 = b.scr("wg1", [128, 8 * 64], BF16)
        wq1 = b.scr("wq1", [128, 8 * 512], BF16)
        wk1 = b.scr("wk1", [128, 8 * 512], BF16)

        @block.gpsimd
        def _(g):
            b.wait(g, "sT")
            b.wait(g, "sV")
            toks = []
            for c in range(CC):
                toks.append(b.dma(g, wg1[:, 64 * c:64 * (c + 1)],
                                  din["w_g1"].ap()[128 * c:128 * (c + 1), :], "G"))
                toks.append(b.dma(g, wq1[:, 512 * c:512 * (c + 1)],
                                  din["w_q1"].ap()[128 * c:128 * (c + 1), :], "G"))
                toks.append(b.dma(g, wk1[:, 512 * c:512 * (c + 1)],
                                  din["w_k1"].ap()[128 * c:128 * (c + 1), :], "G"))
            self._l1_w_tok = toks

        g_mm_t = {}
        g_ev_t = {}
        for tt in range(NT):
            bkg = pb[6 + (tt % 2)]

            @block.tensor
            def _(t, tt=tt, bkg=bkg):
                if tt == 0:
                    t.wait_ge(b.sems["sV"], xn_ready)
                    for tk in self._l1_w_tok:
                        b.wtok(t, tk)
                if tt >= 2:
                    t.wait_ge(b.sems["sS"], g_ev_t[tt - 2])
                for c in range(CC):
                    inst = t.matmul(bkg[:, 0:64],
                                    xn[:, T * c + 128 * tt:T * c + 128 * (tt + 1)],
                                    wg1[:, 64 * c:64 * (c + 1)],
                                    start=(c == 0), stop=(c == CC - 1))
                b.inc(inst, "sT", 1)
                g_mm_t[tt] = b.cnt["sT"]

            @block.scalar
            def _(sc, tt=tt, bkg=bkg):
                sc.wait_ge(b.sems["sT"], g_mm_t[tt])
                b.inc(sc.activation(out=gates[:, 64 * tt:64 * (tt + 1)], in_=bkg[:, 0:64],
                                    func=AF.Sigmoid, bias=self.consts[:, 0:1],
                                    scale=1.0), "sS", 1)
                g_ev_t[tt] = b.cnt["sS"]
        gates_done = b.cnt["sS"]

        @block.vector
        def _(v):
            v.wait_ge(b.sems["sS"], gates_done)
            v.drain()
            g4 = gates[:].rearrange("p (tt h e) -> p tt h e", tt=NT, h=8)
            v.reduce_max(out=st[:, 16:16 + 128].rearrange("p (tt h) -> p tt h", tt=NT),
                         in_=g4, axis=AX.X)
            v.drain()
            mxb = st[:, 16:16 + 128].rearrange("p (tt h) -> p tt h", tt=NT) \
                .unsqueeze(3).broadcast_to([128, NT, 8, 8])
            b.inc(v.tensor_tensor(out=maskbf[:].rearrange("p (tt h e) -> p tt h e",
                                                          tt=NT, h=8),
                                  in0=g4, in1=mxb, op=OP.is_ge), "sV", 1)
        mask_ready = b.cnt["sV"]

        qk_mm_t = {}
        qk_ev_t = {}
        for i, (which, mc) in enumerate([(w, m) for w in ("q", "k") for m in range(4)]):
            wsb = wq1 if which == "q" else wk1
            qoff = (0 if which == "q" else 4 * T)
            bks = [pb[4 * (i % 2) + j] for j in range(4)]

            @block.tensor
            def _(t, which=which, mc=mc, i=i, wsb=wsb, bks=bks):
                if i < 2:
                    t.wait_ge(b.sems["sS"], gates_done)
                if i >= 2:
                    t.wait_ge(b.sems["sV"], qk_ev_t[i - 2])
                for c in range(CC):
                    lhsT = wsb[:, 512 * c + 128 * mc:512 * c + 128 * (mc + 1)]
                    for tc in range(TC):
                        inst = t.matmul(bks[tc][:, :], lhsT,
                                        xn[:, T * c + 512 * tc:T * c + 512 * (tc + 1)],
                                        start=(c == 0), stop=(c == CC - 1))
                b.inc(inst, "sT", 1)
                qk_mm_t[i] = b.cnt["sT"]

            @block.vector
            def _(v, which=which, mc=mc, i=i, qoff=qoff, bks=bks):
                v.wait_ge(b.sems["sT"], qk_mm_t[i])
                scale = 0.125 if which == "q" else 1.0
                for tc in range(TC):
                    v.drain()
                    inst = v.tensor_scalar(
                        out=qkT[:, qoff + T * mc + 512 * tc:qoff + T * mc + 512 * (tc + 1)],
                        in0=bks[tc][:, :], scalar1=scale, scalar2=None, op0=OP.mult)
                b.inc(inst, "sV", 1)
                qk_ev_t[i] = b.cnt["sV"]
        qk_done = b.cnt["sV"]
        b.pop()

        b.push()
        vew = [b.scr(f"vew{j}", [128, 512], BF16) for j in range(32)]
        vtmp = b.scr("vtmp", [128, 512], F32)
        val_mm_t = {}
        val_ev_t = {}
        ve_tok = {}
        for eg in range(2):
            @block.gpsimd
            def _(g, eg=eg):
                for tk in getattr(self, "l1stash_tok", []):
                    b.wtok(g, tk)
                if eg == 0:
                    b.wait(g, "sT")
                    b.wait(g, "sV")
                if eg == 1:
                    t_last = val_mm_t[(NT - 1, 0)]
                    g.wait_ge(b.sems["sT"], t_last)
                for j in range(4):
                    e = 4 * eg + j
                    for c in range(CC):
                        ve_tok[(eg, j, c)] = b.dma(
                            g, vew[8 * j + c][:],
                            dsc["ved"].ap()[e, 128 * c:128 * (c + 1), :], "G")

            for tt in range(NT):
                bks = [pb[4 * (tt % 2) + j] for j in range(4)]

                @block.tensor
                def _(t, tt=tt, eg=eg, bks=bks):
                    if tt == 0:
                        if eg == 0:
                            t.wait_ge(b.sems["sV"], qk_done)
                        for k in list(ve_tok):
                            if k[0] == eg:
                                b.wtok(t, ve_tok[k])
                    prev = (tt - 2, eg) if tt >= 2 else ((NT - 2 + tt, eg - 1) if eg > 0 else None)
                    if prev in val_ev_t:
                        t.wait_ge(b.sems["sV"], val_ev_t[prev])
                    for c in range(CC):
                        lhsT = xn[:, T * c + 128 * tt:T * c + 128 * (tt + 1)]
                        for j in range(4):
                            inst = t.matmul(bks[j][:, :], lhsT, vew[8 * j + c][:],
                                            start=(c == 0), stop=(c == CC - 1))
                    b.inc(inst, "sT", 1)
                    val_mm_t[(tt, eg)] = b.cnt["sT"]

                @block.vector
                def _(v, tt=tt, eg=eg, bks=bks):
                    v.wait_ge(b.sems["sT"], val_mm_t[(tt, eg)])
                    if tt == 0 and eg == 0:
                        v.wait_ge(b.sems["sV"], mask_ready)
                    v3 = values[:, 520 * tt:520 * (tt + 1)].rearrange(
                        "p (h z) -> p h z", h=8)
                    for j in range(4):
                        e = 4 * eg + j
                        mask_ap = maskbf[:].rearrange(
                            "p (tt h e) -> p tt h e", tt=NT, h=8) \
                            [:, tt:tt + 1, :, e:e + 1].squeeze(1) \
                            .broadcast_to([128, 8, 64])
                        dat = bks[j][:, :].rearrange("p (h z) -> p h z", h=8)
                        v.drain()
                        if eg == 0 and j == 0:
                            v.tensor_tensor(out=v3[:, :, 0:64], in0=dat,
                                            in1=mask_ap, op=OP.mult)
                        else:
                            v.tensor_tensor(out=vtmp[:].rearrange(
                                "p (h z) -> p h z", h=8), in0=dat,
                                in1=mask_ap, op=OP.mult)
                            v.drain()
                            v.tensor_tensor(out=v3[:, :, 0:64], in0=v3[:, :, 0:64],
                                            in1=vtmp[:].rearrange(
                                                "p (h z) -> p h z", h=8),
                                            op=OP.add)
                    v.drain()
                    if eg == 1:
                        gsap = st[:, 16 + 8 * tt:16 + 8 * (tt + 1)] \
                            .unsqueeze(2).broadcast_to([128, 8, 64])
                        inst = v.tensor_tensor(out=v3[:, :, 0:64], in0=v3[:, :, 0:64],
                                               in1=gsap, op=OP.mult)
                        v.drain()
                    else:
                        inst = v.nop()
                    b.inc(inst, "sV", 1)
                    val_ev_t[(tt, eg)] = b.cnt["sV"]

        @block.vector
        def _(v):
            v.drain()
            b.inc(v.memset(values[:].rearrange("p (tt h z) -> p tt h z", tt=NT, h=8)
                           [:, :, :, 64:65], 1.0), "sV", 1)
        values_done = b.cnt["sV"]
        b.pop()

        pttm = [b.scr(f"pttm{i}", [128, 1024], BF16) for i in range(2)]
        araw = b.scr("araw", [128, 4 * T], BF16)
        lbuf = b.scr("lbuf", [128, 512], F32)
        s_mm_t = {}
        p_ev_t = {}
        msk_t = {}
        a_mm_t = {}
        a_ev_t = {}
        l_dma_t = {}
        QT = qkT
        pa = PBt[:, 0:512]
        pl = PBt[:, 512:1024]
        for hp in range(4):
            for qc in range(TC):
                jt_list = list(range(4 * qc + 4))
                nji = len(jt_list)
                for ji, jt in enumerate(jt_list):
                    psS = [PBt[:, 1024 + 1024 * (ji % 2) + 512 * par:
                               1024 + 1024 * (ji % 2) + 512 * (par + 1)]
                           for par in range(2)]
                    psSm = PBt[:, 1024 + 1024 * (ji % 2):2048 + 1024 * (ji % 2)]

                    @block.tensor
                    def _(t, hp=hp, qc=qc, jt=jt, ji=ji, nji=nji, psS=psS):
                        if (hp, qc) == (0, 0) and ji == 0:
                            t.wait_ge(b.sems["sV"], values_done)
                        if ji >= 2:
                            t.wait_ge(b.sems["sS"], p_ev_t[(hp, qc, ji - 2)])
                        if ji == 0:
                            prev = ((hp, qc - 1) if qc > 0 else
                                    ((hp - 1, TC - 1) if hp > 0 else None))
                            if prev is not None:
                                t.wait_ge(b.sems["sV"], a_ev_t[prev])
                        for par in range(2):
                            rows = slice(64 * par, 64 * par + 64)
                            inst = t.matmul(
                                psS[par][:, :],
                                QT[rows, 4 * T + T * hp + 128 * jt:
                                   4 * T + T * hp + 128 * (jt + 1)],
                                QT[rows, T * hp + 512 * qc:T * hp + 512 * (qc + 1)],
                                start=True, stop=True)
                        b.inc(inst, "sT", 1)
                        s_mm_t[(hp, qc, ji)] = b.cnt["sT"]

                    @block.scalar
                    def _(sc, hp=hp, qc=qc, jt=jt, ji=ji, psSm=psSm):
                        sc.wait_ge(b.sems["sT"], s_mm_t[(hp, qc, ji)])
                        if ji >= 2:
                            sc.wait_ge(b.sems["sT"], a_mm_t[(hp, qc, ji - 2)])
                        inst = sc.activation(out=pttm[ji % 2][:], in_=psSm[:, :],
                                             func=AF.Exp,
                                             bias=self.consts[:, 4:5], scale=1.0)
                        b.inc(inst, "sS", 1)
                        p_ev_t[(hp, qc, ji)] = b.cnt["sS"]

                    if jt >= 4 * qc:
                        @block.vector
                        def _(v, hp=hp, qc=qc, jt=jt, ji=ji):
                            v.wait_ge(b.sems["sS"], p_ev_t[(hp, qc, ji)])
                            bidx = jt - 4 * qc
                            v.drain()
                            for par in range(2):
                                inst = v.tensor_tensor(
                                    out=pttm[ji % 2][:, 512 * par:512 * (par + 1)],
                                    in0=pttm[ji % 2][:, 512 * par:512 * (par + 1)],
                                    in1=self.tri[:, 512 * bidx:512 * (bidx + 1)],
                                    op=OP.mult)
                                v.drain()
                            b.inc(inst, "sV", 1)
                            msk_t[(hp, qc, ji)] = b.cnt["sV"]

                    @block.tensor
                    def _(t, hp=hp, qc=qc, jt=jt, ji=ji, nji=nji):
                        t.wait_ge(b.sems["sS"], p_ev_t[(hp, qc, ji)])
                        if (hp, qc, ji) in msk_t:
                            t.wait_ge(b.sems["sV"], msk_t[(hp, qc, ji)])
                        for par in range(2):
                            h = 2 * hp + par
                            vl = values[:, 520 * jt + 65 * h:520 * jt + 65 * h + 64]
                            ol = values[:, 520 * jt + 65 * h + 64:520 * jt + 65 * h + 65]
                            pt = pttm[ji % 2][:, 512 * par:512 * (par + 1)]
                            t.matmul(pa[slice(64 * par, 64 * par + 64), :], vl, pt,
                                     start=(ji == 0), stop=(ji == nji - 1))
                            inst = t.matmul(pl[32 * par:32 * par + 1, :], ol, pt,
                                            start=(ji == 0), stop=(ji == nji - 1),
                                            tile_position=(0, 32 * par))
                        b.inc(inst, "sT", 1)
                        a_mm_t[(hp, qc, ji)] = b.cnt["sT"]

                @block.vector
                def _(v, hp=hp, qc=qc, nji=nji):
                    v.wait_ge(b.sems["sT"], a_mm_t[(hp, qc, nji - 1)])
                    if (hp, qc) != (0, 0):
                        prev = ((hp, qc - 1) if qc > 0 else (hp - 1, TC - 1))
                        b.wtok(v, l_dma_t[prev])
                    v.drain()
                    v.tensor_copy(out=araw[:, T * hp + 512 * qc:T * hp + 512 * (qc + 1)],
                                  in_=pa[:, :])
                    v.drain()
                    inst = v.tensor_copy(out=lbuf[:], in_=pl[:, :])
                    v.drain()
                    b.inc(inst, "sV", 1)
                    a_ev_t[(hp, qc)] = b.cnt["sV"]

                @block.gpsimd
                def _(g, hp=hp, qc=qc):
                    g.wait_ge(b.sems["sV"], a_ev_t[(hp, qc)])
                    l_dma_t[(hp, qc)] = b.dma(
                        g, dsc["lst"].ap()[:, 512 * (4 * hp + qc):512 * (4 * hp + qc + 1)],
                        lbuf[:], "G")

        lsb2 = b.scr("lsb2", [128, 128], F32)

        @block.gpsimd
        def _(g):
            for k in l_dma_t.values():
                b.wtok(g, k)
            src = dsc["lst"].ap()[:, 0:16 * 512] \
                .rearrange("(ra rb) c -> ra rb c", ra=4)[0:2, 0:1, :].squeeze(1)
            src = src.rearrange("par (hp qc q) -> par hp qc q", hp=4, qc=4)
            dst = dsc["ld8"].ap().rearrange("(hp par) (qc q) -> par hp qc q",
                                            hp=4, qc=4)
            tk = b.dma(g, dst, src, "G")
            b.wtok(g, tk)
            tk = b.dma(g, lsb2[:],
                       dsc["ld8"].ap().rearrange("h (qa qb) -> h qa qb", qa=16), "G")
            self._l1_l_rd = tk

        @block.vector
        def _(v):
            b.wtok(v, self._l1_l_rd)
            v.drain()
            v.reciprocal(out=lsb2[:], in_=lsb2[:])
            v.drain()
            b.inc(v.tensor_scalar(out=lsb2[:], in0=lsb2[:], scalar1=0.5,
                                  scalar2=None, op0=OP.mult), "sV", 1)
        minv_ready = b.cnt["sV"]

        @block.gpsimd
        def _(g):
            g.wait_ge(b.sems["sV"], minv_ready)
            self._l1_minv_wr = b.dma(
                g, dsc["minvd1"].ap().rearrange("h (qa qb) -> h qa qb", qa=16),
                lsb2[:], "G")

        gmT = b.scr("gmT", [64, T], F32)
        minvB = b.scr("minvB", [64, T], F32)
        gmtm = gates
        gm_tr_t = {}

        @block.vector
        def _(v):
            b.wait(v, "sV")
            v.drain()
            b.inc(v.tensor_tensor(out=gmtm[:], in0=gates[:], in1=maskbf[:],
                                  op=OP.mult), "sV", 1)
        gmtm_ready = b.cnt["sV"]

        for tt in range(NT):
            ptr = PBt[:, 3072 + 512 * (tt % 2):3072 + 512 * (tt % 2) + 128]

            @block.tensor
            def _(t, tt=tt, ptr=ptr):
                t.wait_ge(b.sems["sV"], gmtm_ready)
                if tt >= 2:
                    t.wait_ge(b.sems["sV"], gm_tr_t[tt - 2])
                b.inc(t.transpose(ptr[0:64, 0:128], gmtm[:, 64 * tt:64 * (tt + 1)],
                                  self.ident[:]), "sT", 1)
                gm_tr_t[(tt, "mm")] = b.cnt["sT"]

            @block.vector
            def _(v, tt=tt, ptr=ptr):
                v.wait_ge(b.sems["sT"], gm_tr_t[(tt, "mm")])
                v.drain()
                b.inc(v.tensor_copy(out=gmT[0:64, 128 * tt:128 * (tt + 1)],
                                    in_=ptr[0:64, :]), "sV", 1)
                gm_tr_t[tt] = b.cnt["sV"]

        @block.gpsimd
        def _(g):
            b.wtok(g, self._l1_minv_wr)
            self._l1_minvB = b.dma(
                g, minvB[0:64, :],
                dsc["minvd1"].ap().unsqueeze(1).broadcast_to([8, 8, T]), "G")

        @block.vector
        def _(v):
            b.wait(v, "sV")
            b.wtok(v, self._l1_minvB)
            v.drain()
            b.inc(v.tensor_tensor(out=gmT[0:64, :], in0=gmT[0:64, :],
                                  in1=minvB[0:64, :], op=OP.mult), "sV", 1)
        sT_ready = b.cnt["sV"]

        @block.gpsimd
        def _(g):
            g.wait_ge(b.sems["sV"], sT_ready)
            self._l1_sd_wr = b.dma(g, dsc["sd"].ap(), gmT[0:64, :], "G")

        oew = [b.scr_at(f"oew{i}", [128, 1024], BF16, "qkT", delta=2048 * i)
               for i in range(4)]
        sBt = [b.scr_at(f"sBt{i}", [128, 512], BF16, "qkT", delta=8192 + 1024 * i)
               for i in range(4)]
        Gt = [b.scr_at(f"Gt{i}", [128, 512], BF16, "qkT", delta=12288 + 1024 * i)
              for i in range(4)]
        oe_tok = {}
        sB_tok = {}
        g_bld_t = {}
        oe_mm_t = {}
        oe_ev_t = {}
        for qc in range(TC):
            for ch in range(32):
                e, hp = ch // 4, ch % 4
                ci = 32 * qc + ch
                rot = ci % 4

                @block.gpsimd
                def _(g, qc=qc, ch=ch, e=e, hp=hp, ci=ci, rot=rot):
                    if ci == 0:
                        b.wtok(g, self._l1_sd_wr)
                    if ci >= 4:
                        g.wait_ge(b.sems["sT"], oe_mm_t[ci - 4])
                    oe_tok[ci] = b.dma(
                        g, oew[rot][:],
                        dsc["oed"].ap()[512 * e + 128 * hp:512 * e + 128 * (hp + 1), :],
                        "G")
                    src = dsc["sd"].ap().rearrange("(h e2) t -> h e2 t", e2=8) \
                        [2 * hp:2 * hp + 2, e, 512 * qc:512 * (qc + 1)] \
                        .unsqueeze(1).broadcast_to([2, 64, 512])
                    sB_tok[ci] = b.dma(g, sBt[rot][:], src, "G")

                @block.vector
                def _(v, qc=qc, hp=hp, ci=ci, rot=rot):
                    b.wtok(v, sB_tok[ci])
                    if ci >= 4:
                        v.wait_ge(b.sems["sT"], oe_mm_t[ci - 4])
                    v.drain()
                    b.inc(v.tensor_tensor(
                        out=Gt[rot][:],
                        in0=araw[:, T * hp + 512 * qc:T * hp + 512 * (qc + 1)],
                        in1=sBt[rot][:], op=OP.mult), "sV", 1)
                    g_bld_t[ci] = b.cnt["sV"]

                @block.tensor
                def _(t, qc=qc, ch=ch, ci=ci, rot=rot):
                    b.wtok(t, oe_tok[ci])
                    t.wait_ge(b.sems["sV"], g_bld_t[ci])
                    if ch == 0 and qc > 0:
                        t.wait_ge(b.sems["sV"], oe_ev_t[qc - 1])
                    for dc in range(CC):
                        inst = t.matmul(pb[dc][:, :],
                                        oew[rot][:, 128 * dc:128 * (dc + 1)],
                                        Gt[rot][:], start=(ch == 0), stop=(ch == 31))
                    b.inc(inst, "sT", 1)
                    oe_mm_t[ci] = b.cnt["sT"]

            @block.vector
            def _(v, qc=qc):
                v.wait_ge(b.sems["sT"], oe_mm_t[32 * qc + 31])
                for dc in range(CC):
                    v.drain()
                    inst = v.scalar_tensor_tensor(
                        out=X[:, T * dc + 512 * qc:T * dc + 512 * (qc + 1)],
                        in0=pb[dc][:, :], scalar=1.0,
                        in1=X[:, T * dc + 512 * qc:T * dc + 512 * (qc + 1)],
                        op0=OP.mult, op1=OP.add)
                b.inc(inst, "sV", 1)
                oe_ev_t[qc] = b.cnt["sV"]
        b.pop()

    def ph_l1att(self):
        b, block, din, dsc, pb, PBt = self.b, self.block, self.din, self.dsc, self.pb, self.PBt
        X, st = self.X, self.stat
        b.push()
        yt = b.scr("yt", [128, CC * T], F16)
        b.push()
        xbf = b.scr("xbf4", [128, CC * T], BF16)
        b.push()
        x2bf = b.scr("x2bf4", [128, CC * T], BF16)
        rb = b.scr("rb4", [128, T], F32)
        cb = b.scr("cb4", [128, T], F32)
        rbbf = b.scr("rbbf4", [128, T], BF16)
        cbbf = b.scr("cbbf4", [128, T], BF16)
        self.emit_xbf_copies(xbf, x2bf)
        self.emit_ln_stats(xbf, x2bf, rb, cb)
        self.emit_ln_bf16_apply(xbf, rb, cb, rbbf, cbbf)
        b.pop()
        xn = xbf
        xn_ready = b.cnt["sV"]

        araw = b.scr("araw2", [128, 8 * T], BF16)
        lbuf = b.scr("lbuf2", [128, 512], F32)
        b.push()
        qp = [b.scr(f"qp{i}", [128, T], BF16) for i in range(2)]
        kp = [b.scr(f"kp{i}", [128, T], BF16) for i in range(2)]
        vp = [b.scr(f"vp{i}", [128, NT * 128], BF16) for i in range(2)]
        ptt = [b.scr(f"pttA{i}", [128, 1024], BF16) for i in range(2)]
        qkvw = [b.scr(f"qkvw{i}", [128, 3 * 128], BF16) for i in range(8)]

        qkv_tok = {}
        qkv_mm_t = {}
        qkv_ev_t = {}
        s_mm_t = {}
        p_ev_t = {}
        a_mm_t = {}
        a_ev_t = {}
        l_dma_t = {}
        pa = PBt[:, 0:512]
        pl = PBt[:, 512:1024]
        for hp in range(CC):
            rot = hp % 2

            @block.gpsimd
            def _(g, hp=hp):
                if hp == 0:
                    b.wait(g, "sT")
                    b.wait(g, "sV")
                if hp >= 1:
                    g.wait_ge(b.sems["sT"], self._la_vmm_last[hp - 1])
                toks = []
                for c in range(CC):
                    for wi, wn in enumerate(("wq_t", "wk_t", "wv_t")):
                        toks.append(b.dma(
                            g, qkvw[c][:, 128 * wi:128 * (wi + 1)],
                            din[wn].ap()[128 * c:128 * (c + 1),
                                         128 * hp:128 * (hp + 1)], "G"))
                qkv_tok[hp] = toks

            qk_half_t = {}
            for which, wslice, bcol, scale2 in (("q", slice(0, 128), 0, 0.125),
                                                ("k", slice(128, 256), 1, None)):
                dstp = qp if which == "q" else kp
                for qh in range(2):
                    bks = [pb[6], pb[7]]

                    @block.tensor
                    def _(t, hp=hp, rot=rot, which=which, wslice=wslice, qh=qh, bks=bks):
                        if hp == 0 and which == "q" and qh == 0:
                            t.wait_ge(b.sems["sV"], xn_ready)
                        for tk in qkv_tok[hp]:
                            b.wtok(t, tk)
                        prev = qk_half_t.get((hp, which, qh - 1))
                        if prev is None and which == "k":
                            prev = qk_half_t.get((hp, "q", 1))
                        if prev is not None:
                            t.wait_ge(b.sems["sV"], prev)
                        elif hp >= 1:
                            t.wait_ge(b.sems["sV"], qkv_ev_t[(hp - 1, "v", 3)])
                        if hp >= 2:
                            t.wait_ge(b.sems["sT"], s_mm_t[(hp - 2, TC - 1, NT - 1)])
                        for c in range(CC):
                            lhsT = qkvw[c][:, wslice]
                            for j in range(2):
                                tc = 2 * qh + j
                                inst = t.matmul(bks[j][:, :], lhsT,
                                                xn[:, T * c + 512 * tc:T * c + 512 * (tc + 1)],
                                                start=(c == 0), stop=(c == CC - 1))
                        b.inc(inst, "sT", 1)
                        qkv_mm_t[(hp, which, qh)] = b.cnt["sT"]

                    @block.vector
                    def _(v, hp=hp, rot=rot, which=which, bcol=bcol, scale2=scale2,
                          qh=qh, bks=bks, dstp=dstp):
                        v.wait_ge(b.sems["sT"], qkv_mm_t[(hp, which, qh)])
                        bb = self.biases[:, bcol * CC + hp:bcol * CC + hp + 1]
                        for j in range(2):
                            tc = 2 * qh + j
                            v.drain()
                            if scale2 is None:
                                inst = v.tensor_scalar(
                                    out=dstp[rot][:, 512 * tc:512 * (tc + 1)],
                                    in0=bks[j][:, :], scalar1=bb, scalar2=None,
                                    op0=OP.add)
                            else:
                                inst = v.tensor_scalar(
                                    out=dstp[rot][:, 512 * tc:512 * (tc + 1)],
                                    in0=bks[j][:, :], scalar1=bb, scalar2=scale2,
                                    op0=OP.add, op1=OP.mult)
                        b.inc(inst, "sV", 1)
                        qk_half_t[(hp, which, qh)] = b.cnt["sV"]
                        qkv_ev_t[(hp, which)] = b.cnt["sV"]

            vv_mm = {}
            for tg in range(4):
                bkv = [pb[6 + (tg % 2)]]

                @block.tensor
                def _(t, hp=hp, rot=rot, tg=tg, bkv=bkv):
                    if tg == 0:
                        t.wait_ge(b.sems["sV"], qk_half_t[(hp, "k", 1)])
                    if tg == 1:
                        pass
                    if tg >= 2:
                        t.wait_ge(b.sems["sV"], qkv_ev_t[(hp, "v", tg - 2)])
                    if hp >= 2 and tg < 2:
                        t.wait_ge(b.sems["sT"], a_mm_t[(hp - 2, TC - 1, NT - 1)])
                    for tj in range(4):
                        tt = 4 * tg + tj
                        for c in range(CC):
                            inst = t.matmul(
                                bkv[0][:, 128 * tj:128 * (tj + 1)],
                                xn[:, T * c + 128 * tt:T * c + 128 * (tt + 1)],
                                qkvw[c][:, 256:384],
                                start=(c == 0), stop=(c == CC - 1))
                    b.inc(inst, "sT", 1)
                    vv_mm[tg] = b.cnt["sT"]

                @block.vector
                def _(v, hp=hp, rot=rot, tg=tg, bkv=bkv):
                    v.wait_ge(b.sems["sT"], vv_mm[tg])
                    if hp >= 2 and tg == 0:
                        t_l = a_mm_t[(hp - 2, TC - 1, NT - 1)]
                        v.wait_ge(b.sems["sT"], t_l)
                    v.drain()
                    inst = v.tensor_copy(out=vp[rot][:, 512 * tg:512 * (tg + 1)],
                                         in_=bkv[0][:, :])
                    v.drain()
                    b.inc(inst, "sV", 1)
                    qkv_ev_t[(hp, "v", tg)] = b.cnt["sV"]
            if not hasattr(self, "_la_vmm_last"):
                self._la_vmm_last = {}
            self._la_vmm_last[hp] = vv_mm[3]
            vready = b.cnt["sV"]

            for qc in range(TC):
                for jt in range(NT):
                    psS = PBt[:, 1024 + 1024 * (jt % 2):2048 + 1024 * (jt % 2)]

                    @block.tensor
                    def _(t, hp=hp, rot=rot, qc=qc, jt=jt, vready=vready, psS=psS):
                        if qc == 0 and jt == 0:
                            t.wait_ge(b.sems["sV"], vready)
                        if jt >= 2:
                            t.wait_ge(b.sems["sS"], p_ev_t[(hp, qc, jt - 2)])
                        if jt == 0:
                            prev = ((hp, qc - 1) if qc > 0 else
                                    ((hp - 1, TC - 1) if hp >= 1 else None))
                            if prev is not None and prev in a_ev_t:
                                t.wait_ge(b.sems["sV"], a_ev_t[prev])
                        for par in range(2):
                            rows = slice(64 * par, 64 * par + 64)
                            inst = t.matmul(
                                psS[:, 512 * par:512 * (par + 1)],
                                kp[rot][rows, 128 * jt:128 * (jt + 1)],
                                qp[rot][rows, 512 * qc:512 * (qc + 1)],
                                start=True, stop=True)
                        b.inc(inst, "sT", 1)
                        s_mm_t[(hp, qc, jt)] = b.cnt["sT"]

                    @block.scalar
                    def _(sc, hp=hp, qc=qc, jt=jt, psS=psS):
                        sc.wait_ge(b.sems["sT"], s_mm_t[(hp, qc, jt)])
                        if jt >= 2:
                            sc.wait_ge(b.sems["sT"], a_mm_t[(hp, qc, jt - 2)])
                        b.inc(sc.activation(out=ptt[jt % 2][:], in_=psS[:, :],
                                            func=AF.Exp, bias=self.consts[:, 5:6],
                                            scale=1.0), "sS", 1)
                        p_ev_t[(hp, qc, jt)] = b.cnt["sS"]

                    @block.tensor
                    def _(t, hp=hp, rot=rot, qc=qc, jt=jt, psS=psS):
                        t.wait_ge(b.sems["sS"], p_ev_t[(hp, qc, jt)])
                        for par in range(2):
                            pt = ptt[jt % 2][:, 512 * par:512 * (par + 1)]
                            vl = vp[rot][:, 128 * jt:128 * jt + 64] if False else \
                                vp[rot].rearrange("p (tt h) -> p tt h", tt=NT) \
                                [:, jt:jt + 1, :].squeeze(1)
                            t.matmul(pa[slice(64 * par, 64 * par + 64), :],
                                     vl[:, 64 * par:64 * (par + 1)] if False else
                                     vp[rot][:, 128 * jt + 64 * par:128 * jt + 64 * (par + 1)]
                                     if False else vp[rot][:, 128 * jt + 64 * 0:128 * jt + 64],
                                     pt, start=(jt == 0), stop=(jt == NT - 1))
                            inst = t.matmul(pl[32 * par:32 * par + 1, :],
                                            self.onecol_bf[:, 0:1], pt,
                                            start=(jt == 0), stop=(jt == NT - 1),
                                            tile_position=(0, 32 * par))
                        b.inc(inst, "sT", 1)
                        a_mm_t[(hp, qc, jt)] = b.cnt["sT"]

                @block.vector
                def _(v, hp=hp, qc=qc):
                    v.wait_ge(b.sems["sT"], a_mm_t[(hp, qc, NT - 1)])
                    if (hp, qc) != (0, 0):
                        prev = ((hp, qc - 1) if qc > 0 else (hp - 1, TC - 1))
                        b.wtok(v, l_dma_t[prev])
                    v.drain()
                    v.tensor_copy(out=araw[:, T * hp + 512 * qc:T * hp + 512 * (qc + 1)],
                                  in_=pa[:, :])
                    v.drain()
                    inst = v.tensor_copy(out=lbuf[:], in_=pl[:, :])
                    v.drain()
                    b.inc(inst, "sV", 1)
                    a_ev_t[(hp, qc)] = b.cnt["sV"]

                @block.gpsimd
                def _(g, hp=hp, qc=qc):
                    g.wait_ge(b.sems["sV"], a_ev_t[(hp, qc)])
                    l_dma_t[(hp, qc)] = b.dma(
                        g, dsc["lst"].ap()[:, 512 * (4 * hp + qc):512 * (4 * hp + qc + 1)],
                        lbuf[:], "G")
        b.pop()

        lsb2 = b.scr("lsb2A", [128, 256], F32)

        @block.gpsimd
        def _(g):
            for k in l_dma_t.values():
                b.wtok(g, k)
            src = dsc["lst"].ap().rearrange("(ra rb) c -> ra rb c", ra=4)[0:2, 0:1, :] \
                .squeeze(1)
            src = src.rearrange("par (hp qc q) -> par hp qc q", hp=8, qc=4)
            dst = dsc["ld16"].ap().rearrange("(hp par) (qc q) -> par hp qc q",
                                             hp=8, qc=4)
            tk = b.dma(g, dst, src, "G")
            b.wtok(g, tk)
            tk = b.dma(g, lsb2[:],
                       dsc["ld16"].ap().rearrange("h (qa qb) -> h qa qb", qa=8), "G")
            self._la_l_rd = tk

        @block.vector
        def _(v):
            b.wtok(v, self._la_l_rd)
            v.drain()
            b.inc(v.reciprocal(out=lsb2[:], in_=lsb2[:]), "sV", 1)
        minv_ready = b.cnt["sV"]

        @block.gpsimd
        def _(g):
            g.wait_ge(b.sems["sV"], minv_ready)
            self._la_minv_wr = b.dma(
                g, dsc["minvd"].ap().rearrange("h (qa qb) -> h qa qb", qa=8),
                lsb2[:], "G")

        gsc = [b.scr(f"gscA{i}", [128, T], BF16) for i in range(2)]
        gsc_tok = {}
        scale_t = {}
        for hp in range(CC):
            @block.gpsimd
            def _(g, hp=hp):
                if hp == 0:
                    b.wtok(g, self._la_minv_wr)
                if hp >= 2:
                    g.wait_ge(b.sems["sV"], scale_t[hp - 2])
                src = dsc["minvd"].ap()[2 * hp:2 * hp + 2, :]
                gsc_tok[hp] = b.dma(g, gsc[hp % 2][:],
                                    src.unsqueeze(1).broadcast_to([2, 64, T]), "G")

            @block.vector
            def _(v, hp=hp):
                b.wtok(v, gsc_tok[hp])
                v.drain()
                sl = slice(T * hp, T * (hp + 1))
                b.inc(v.tensor_tensor(out=araw[:, sl], in0=araw[:, sl],
                                      in1=gsc[hp % 2][:], op=OP.mult), "sV", 1)
                scale_t[hp] = b.cnt["sV"]

        wob = [b.scr(f"wobA{i}", [128, 1024], BF16) for i in range(CC)]
        bvb8 = b.scr("bvb8", [128, CC], BF16)

        @block.gpsimd
        def _(g):
            b.wait(g, "sT")
            toks = []
            for c in range(CC):
                toks.append(b.dma(g, wob[c][:],
                                  din["wo_t"].ap()[128 * c:128 * (c + 1), :], "G"))
            self._la_wo_tok = toks

        @block.vector
        def _(v):
            b.wait(v, "sV")
            v.drain()
            b.inc(v.tensor_copy(out=bvb8[:], in_=self.biases[:, 2 * CC:3 * CC]), "sV", 1)
        bv_ready = b.cnt["sV"]

        cv_t = {}

        @block.tensor
        def _(t):
            for tk in self._la_wo_tok:
                b.wtok(t, tk)
            t.wait_ge(b.sems["sV"], bv_ready)
            pc = PBt[:, 2048:2176]
            for dc in range(CC):
                for c in range(CC):
                    inst = t.matmul(pc[:, dc:dc + 1],
                                    wob[c][:, 128 * dc:128 * (dc + 1)],
                                    bvb8[:, c:c + 1],
                                    start=(c == 0), stop=(c == CC - 1))
            b.inc(inst, "sT", 1)
        cv_done = b.cnt["sT"]

        @block.vector
        def _(v):
            v.wait_ge(b.sems["sT"], cv_done)
            v.drain()
            v.tensor_copy(out=st[:, 32:40], in_=PBt[:, 2048:2056])
            v.drain()
            b.inc(v.tensor_tensor(out=st[:, 40:48], in0=st[:, 32:40],
                                  in1=self.biases[:, 3 * CC:4 * CC], op=OP.add), "sV", 1)
        boc_ready = b.cnt["sV"]

        wo_mm_t = {}
        wo_ev_t = {}
        for dc in range(CC):
            bks = [pb[4 * (dc % 2) + j] for j in range(4)]

            @block.tensor
            def _(t, dc=dc, bks=bks):
                if dc == 0:
                    t.wait_ge(b.sems["sV"], scale_t[CC - 1])
                    t.wait_ge(b.sems["sV"], boc_ready)
                if dc >= 2:
                    t.wait_ge(b.sems["sV"], wo_ev_t[dc - 2])
                for c in range(CC):
                    lhsT = wob[c][:, 128 * dc:128 * (dc + 1)]
                    for tc in range(TC):
                        inst = t.matmul(bks[tc][:, :], lhsT,
                                        araw[:, T * c + 512 * tc:T * c + 512 * (tc + 1)],
                                        start=(c == 0), stop=(c == CC - 1))
                b.inc(inst, "sT", 1)
                wo_mm_t[dc] = b.cnt["sT"]

            @block.vector
            def _(v, dc=dc, bks=bks):
                v.wait_ge(b.sems["sT"], wo_mm_t[dc])
                for tc in range(TC):
                    v.drain()
                    inst = v.scalar_tensor_tensor(
                        out=yt[:, T * dc + 512 * tc:T * dc + 512 * (tc + 1)],
                        in0=bks[tc][:, :], scalar=st[:, 40 + dc:41 + dc],
                        in1=xn[:, T * dc + 512 * tc:T * dc + 512 * (tc + 1)],
                        op0=OP.add, op1=OP.add)
                b.inc(inst, "sV", 1)
                wo_ev_t[dc] = b.cnt["sV"]
        y_done = b.cnt["sV"]
        b.pop()

        b.push()
        ybf = b.scr("ybf", [128, CC * T], BF16)
        y2bf = b.scr("y2bf", [128, CC * T], BF16)
        rb2 = b.scr("rb5", [128, T], F32)
        cb2 = b.scr("cb5", [128, T], F32)
        t1 = b.scr("t1", [128, T], F32)

        @block.vector
        def _(v):
            v.wait_ge(b.sems["sV"], y_done)
            v.drain()
            b.inc(v.tensor_copy(out=ybf[:], in_=yt[:]), "sV", 1)

        @block.scalar
        def _(s):
            b.wait(s, "sV")
            b.inc(s.activation(out=y2bf[:], in_=yt[:], func=AF.Square,
                               bias=self.consts[:, 0:1], scale=1.0), "sS", 1)
        self.emit_ln_stats(ybf, y2bf, rb2, cb2)

        @block.vector
        def _(v):
            b.wait(v, "sV")
            v.drain()
            v.reciprocal(out=st[:, 48:56], in_=self.biases[:, 4 * CC:5 * CC])
            v.drain()
            b.inc(v.tensor_tensor(out=st[:, 48:56], in0=st[:, 48:56],
                                  in1=self.biases[:, 5 * CC:6 * CC], op=OP.mult),
                  "sV", 1)

        xup_t = {}
        for dc in range(CC):
            @block.vector
            def _(v, dc=dc):
                b.wait(v, "sV")
                v.drain()
                v.tensor_tensor(out=t1[:], in0=yt[:, T * dc:T * (dc + 1)], in1=rb2[:],
                                op=OP.mult)
                v.drain()
                v.scalar_tensor_tensor(out=t1[:], in0=t1[:],
                                       scalar=st[:, 48 + dc:49 + dc],
                                       in1=cb2[:], op0=OP.add, op1=OP.subtract)
                v.drain()
                inst = v.scalar_tensor_tensor(
                    out=X[:, T * dc:T * (dc + 1)], in0=t1[:],
                    scalar=self.biases[:, 4 * CC + dc:4 * CC + dc + 1],
                    in1=X[:, T * dc:T * (dc + 1)], op0=OP.mult, op1=OP.add)
                v.drain()
                b.inc(inst, "sV", 1)
                xup_t[dc] = b.cnt["sV"]
        b.pop()
        b.pop()
    def ph_store(self):
        b, block = self.b, self.block
        self.stored = True

        @block.sync
        def _(q):
            b.wait(q, "sV")
            b.wait(q, "sT")
            b.wait(q, "sS")
            tok = b.dma(q, self.dout.ap().rearrange("(c p) t -> p c t", p=128),
                        self.X[:].rearrange("p (c t) -> p c t", c=CC), "S")
            b.wtok(q, tok)

    def ph_store_taps(self):
        b, block = self.b, self.block
        if not self.stored:
            self.ph_store()

        @block.sync
        def _(q):
            b.wait_allq(q, "S")

        @block.gpsimd
        def _(g):
            b.wait_allq(g, "G")


def kernel(**inputs):
    in_maps = _prep_inputs(inputs)
    nc = build_program()
    res = run_bass_kernel_spmd(nc, in_maps, core_ids=list(range(NCORES)))
    out = np.stack([np.ascontiguousarray(r["out_t"].T) for r in res.results])
    return out.astype(np.float32)


# version 03 (final)
# speedup vs baseline: 1.0784x; 1.0219x over previous
import numpy as np
import ml_dtypes
import concourse.bass as bass
import concourse.mybir as mybir
from concourse.alu_op_type import AluOpType
from concourse.bass_utils import run_bass_kernel_spmd

F32 = mybir.dt.float32
BF16 = mybir.dt.bfloat16
F16 = mybir.dt.float16
U8 = mybir.dt.uint8
AF = mybir.ActivationFunctionType
AX = mybir.AxisListType
OP = AluOpType
BF = ml_dtypes.bfloat16

NCORES = 8
T = 2048
D = 1024
CC = D // 128
TC = T // 512
NT = T // 128
EPS_LN = 1e-5
NROT = 4


class Bld:

    def __init__(self, nc):
        self.nc = nc
        self.sems = {}
        self.cnt = {}
        self.rot = {"S": -1, "G": -1, "C": -1}
        self.cur = {}
        self.scr_off = 0
        self.scr_stack = []
        self.scratch_addr = None
        self.scratch_bytes = 0
        self.uid = 0

    def sem(self, name, handle):
        self.sems[name] = handle
        self.cnt[name] = 0

    def inc(self, inst, name, k=1):
        inst.then_inc(self.sems[name], k)
        self.cnt[name] += k
        return inst

    def wait(self, eng, name, tgt=None):
        eng.wait_ge(self.sems[name], self.cnt[name] if tgt is None else tgt)

    def gbegin(self, q, eng=None):
        self.rot[q] = (self.rot[q] + 1) % NROT
        s = f"d{q}{self.rot[q]}"
        self.cur[q] = s
        if eng is not None and self.cnt[s] > 0:
            eng.wait_ge(self.sems[s], self.cnt[s])
        return s

    def dinc(self, inst, q):
        s = self.cur[q]
        inst.then_inc(self.sems[s], 16)
        self.cnt[s] += 16
        return inst

    def gend(self, q):
        s = self.cur[q]
        return (s, self.cnt[s])

    def dma(self, eng, out, in_, q):
        self.gbegin(q, eng)
        self.dinc(eng.dma_start(out=out, in_=in_), q)
        return self.gend(q)

    def wtok(self, eng, tok):
        eng.wait_ge(self.sems[tok[0]], tok[1])

    def wait_allq(self, eng, q):
        for i in range(NROT):
            s = f"d{q}{i}"
            eng.wait_ge(self.sems[s], self.cnt[s])

    def push(self):
        self.scr_stack.append(self.scr_off)

    def pop(self):
        self.scr_off = self.scr_stack.pop()

    def scr(self, name, shape, dtype, align=32):
        per_part = int(np.prod(shape[1:])) * mybir.dt.size(dtype)
        off = (self.scr_off + align - 1) // align * align
        assert off + per_part <= self.scratch_bytes, (
            f"scratch overflow: {name} needs {per_part} at {off}, cap {self.scratch_bytes}")
        self.scr_off = off + per_part
        self.uid += 1
        self.offs = getattr(self, "offs", {})
        self.offs[name] = (off, per_part)
        return self.nc.alloc_sbuf_tensor_at(
            f"scr_{name}_{self.uid}", list(shape), dtype,
            offset=self.scratch_addr + off)

    def scr_at(self, name, shape, dtype, at, delta=0):
        per_part = int(np.prod(shape[1:])) * mybir.dt.size(dtype)
        off, sz = self.offs[at]
        assert delta + per_part <= sz, (name, per_part, at, sz, delta)
        off = off + delta
        self.uid += 1
        return self.nc.alloc_sbuf_tensor_at(
            f"scr_{name}_{self.uid}", list(shape), dtype,
            offset=self.scratch_addr + off)


def _prep_inputs(I):
    C = np.ascontiguousarray
    f32 = np.float32
    shared = {
        "w_qkv_t": C(np.asarray(I["L0_attn_W_qkv"]).T),
        "w_out_t": C(np.asarray(I["L0_attn_W_out"]).T),
        "w_gate": C(np.asarray(I["L0_ffn_w_gate"])),
        "w_up": C(np.asarray(I["L0_ffn_w_up"])),
        "w_down": C(np.asarray(I["L0_ffn_w_down"])),
        "clip_x": C(np.reshape(np.asarray(I["L0_ffn_clip_x"]), (1, 1))),
        "clip_down": C(np.reshape(np.asarray(I["L0_ffn_clip_down"]), (1, 1))),
        "wq_t": C(np.asarray(I["L1_attn_Wq"]).T), "wk_t": C(np.asarray(I["L1_attn_Wk"]).T),
        "wv_t": C(np.asarray(I["L1_attn_Wv"]).T), "wo_t": C(np.asarray(I["L1_attn_Wo"]).T),
        "bq": C(np.reshape(np.asarray(I["L1_attn_bq"]), (CC, 128)).T),
        "bk": C(np.reshape(np.asarray(I["L1_attn_bk"]), (CC, 128)).T),
        "bv": C(np.reshape(np.asarray(I["L1_attn_bv"]), (CC, 128)).T),
        "bo": C(np.reshape(np.asarray(I["L1_attn_bo"]), (CC, 128)).T),
        "ln_g": C(np.reshape(np.asarray(I["L1_attn_ln_gamma"]), (CC, 128)).T),
        "ln_b": C(np.reshape(np.asarray(I["L1_attn_ln_beta"]), (CC, 128)).T),
        "w_q1": C(np.asarray(I["L1_ffn_w_q"])),
        "w_k1": C(np.asarray(I["L1_ffn_w_k"])),
        "w_g1": C(np.asarray(I["L1_ffn_w_gate"])),
        "ve_r": C(np.transpose(np.asarray(I["L1_ffn_value_experts"]),
                               (0, 2, 1, 3)).reshape(8, 1024, 512)),
        "oe_flat": C(np.reshape(np.asarray(I["L1_ffn_output_experts"]), (4096, 1024))),
        "ident": np.eye(128, dtype=f32),
    }
    tri = np.zeros((128, 4 * 512), dtype=f32)
    jj = np.arange(128)[:, None]
    qq = np.arange(512)[None, :]
    for bb in range(4):
        tri[:, 512 * bb:512 * (bb + 1)] = (jj + 128 * bb <= qq)
    shared["tri"] = tri.astype(BF)
    x = np.asarray(I["L0_attn_x"])
    in_maps = []
    for i in range(NCORES):
        m = dict(shared)
        m["x_t"] = C(x[i].T)
        in_maps.append(m)
    return in_maps


def build_program(stop_after=None, taps=()):
    nc = bass.Bass("TRN2", target_bir_lowering=False, debug=False)

    din = {}
    for name, shape in [
        ("x_t", [D, T]), ("w_qkv_t", [D, 3 * D]), ("w_out_t", [D, D]),
        ("w_gate", [2816, D]), ("w_up", [2816, D]), ("w_down", [D, 2816]),
        ("clip_x", [1, 1]), ("clip_down", [1, 1]),
        ("wq_t", [D, D]), ("wk_t", [D, D]), ("wv_t", [D, D]), ("wo_t", [D, D]),
        ("bq", [128, CC]), ("bk", [128, CC]), ("bv", [128, CC]), ("bo", [128, CC]),
        ("ln_g", [128, CC]), ("ln_b", [128, CC]),
        ("w_q1", [D, 512]), ("w_k1", [D, 512]), ("w_g1", [D, 64]),
        ("ve_r", [8, D, 512]), ("oe_flat", [4096, D]), ("ident", [128, 128]),
    ]:
        din[name] = nc.dram_tensor(name, shape, F32, kind="ExternalInput")
    din["tri"] = nc.dram_tensor("tri", [128, 4 * 512], BF16, kind="ExternalInput")
    dout = nc.dram_tensor("out_t", [D, T], F32, kind="ExternalOutput")

    dsc = {
        "qwg": nc.dram_tensor("qwg", [2816, D], BF16),
        "qwu": nc.dram_tensor("qwu", [2816, D], BF16),
        "qwd": nc.dram_tensor("qwd", [D, 2816], BF16),
        "hsp": nc.dram_tensor("hsp", [2816, T], F32),
        "vec": nc.dram_tensor("vec", [128, 16], F32),
        "vec1": nc.dram_tensor("vec1", [128, 1], F32),
        "rowb": nc.dram_tensor("rowb", [64, T], BF16),
        "ccin": nc.dram_tensor("ccin", [1, 1], F32),
        "statsd": nc.dram_tensor("statsd", [128, 32 * 512], F32),
        "minvd": nc.dram_tensor("minvd", [16, T], F32),
        "ved": nc.dram_tensor("ved", [8, D, 512], BF16),
        "oed": nc.dram_tensor("oed", [4096, D], BF16),
        "lst": nc.dram_tensor("lst", [128, 32 * 512], F32),
        "ld16": nc.dram_tensor("ld16", [16, T], F32),
        "ld8": nc.dram_tensor("ld8", [8, T], F32),
        "minvd1": nc.dram_tensor("minvd1", [8, T], F32),
        "sd": nc.dram_tensor("sd", [64, T], BF16),
        "ccout": nc.dram_tensor("ccout", [NCORES, 1], F32),
    }
    tap_outs = {}
    for tname, tshape in taps:
        tap_outs[tname] = nc.dram_tensor("tap_" + tname, tshape, F32, kind="ExternalOutput")

    from contextlib import ExitStack
    ctx = ExitStack()
    with ctx:
        X = ctx.enter_context(nc.sbuf_tensor("X", [128, CC * T], F32))
        ident = ctx.enter_context(nc.sbuf_tensor("identity_sb", [128, 128], F32))
        tri = ctx.enter_context(nc.sbuf_tensor("tri_sb", [128, 4 * 512], BF16))
        onecol_bf = ctx.enter_context(nc.sbuf_tensor("onecol_bf", [128, 1], BF16))
        consts = ctx.enter_context(nc.sbuf_tensor("consts", [128, 8], F32))
        biases = ctx.enter_context(nc.sbuf_tensor("biases", [128, 6 * CC], F32))
        stat = ctx.enter_context(nc.sbuf_tensor("stat", [128, 256], F32))
        ones_bf = ctx.enter_context(nc.sbuf_tensor("ones_bf", [128, 128], BF16))
        wsc = ctx.enter_context(nc.sbuf_tensor("wsc", [128, 52], F32))
        stat2col = ctx.enter_context(nc.sbuf_tensor("stat2col", [128, 2], F32))
        clip_sb = ctx.enter_context(nc.sbuf_tensor("clip_sb", [128, 2], F32))
        SCR_BYTES = 137 * 1024
        scratch = ctx.enter_context(nc.sbuf_tensor("scratch", [128, SCR_BYTES], U8))
        PBt = ctx.enter_context(nc.psum_tensor("PBt", [128, 4096], F32))
        pb = [PBt[:, 512 * i:512 * (i + 1)] for i in range(8)]

        sem_names = (["sT", "sV", "sS", "cc"]
                     + [f"d{q}{i}" for q in "SGC" for i in range(NROT)])
        sems = {n: ctx.enter_context(nc.semaphore(n)) for n in sem_names}

        block = ctx.enter_context(nc.Block())
        b = Bld(nc)
        b.scratch_bytes = SCR_BYTES
        b.scratch_addr = nc.lookup_mloc(scratch).addr
        for n in sem_names:
            b.sem(n, sems[n])

        st = _State(nc, block, b, din, dout, dsc, X, ident, tri, onecol_bf,
                    consts, biases, stat, ones_bf, pb, tap_outs)
        st.PBt = PBt
        st.wsc, st.stat2col, st.clip_sb = wsc, stat2col, clip_sb

        phases = [
            ("init", st.ph_init),
            ("wquant", st.ph_wquant),
            ("l1stash", st.ph_l1stash),
            ("l0f1", lambda: st.ph_l0ffn(0)),
            ("l0att", st.ph_l0att),
            ("l0f2", lambda: st.ph_l0ffn(1)),
            ("l1f1", lambda: st.ph_l1ffn(0)),
            ("l1att", st.ph_l1att),
            ("l1f2", lambda: st.ph_l1ffn(1)),
            ("store", st.ph_store),
        ]
        for name, fn in phases:
            fn()
            if stop_after is not None and name == stop_after:
                break
        st.ph_store_taps()
    return nc


class _State:
    def __init__(self, nc, block, b, din, dout, dsc, X, ident, tri, onecol_bf,
                 consts, biases, stat, ones_bf, pb, tap_outs):
        self.nc, self.block, self.b = nc, block, b
        self.din, self.dout, self.dsc = din, dout, dsc
        self.X, self.ident, self.tri = X, ident, tri
        self.onecol_bf, self.consts = onecol_bf, consts
        self.biases, self.stat, self.ones_bf, self.pb = biases, stat, ones_bf, pb
        self.taps = tap_outs
        self.stored = False

    def tap_now(self, name, ap, wait=("sV", "sS", "sT")):
        if name not in self.taps:
            return
        b, block = self.b, self.block
        dram = self.taps[name]

        @block.gpsimd
        def _(g):
            for w in wait:
                b.wait(g, w)
            b.wait_allq(g, "G")
            tok = b.dma(g, dram.ap(), ap, "G")
            b.wtok(g, tok)

    def maybe_tap(self, name, ap):
        if name not in self.taps:
            return
        b, block = self.b, self.block
        dram = self.taps[name]

        @block.gpsimd
        def _(g):
            b.wait(g, "sV"); b.wait(g, "sS"); b.wait(g, "sT")
            tok = b.dma(g, dram.ap(), ap, "G")
            b.wtok(g, tok)

    def emit_xbf_copies(self, xbf, x2bf):
        b, block = self.b, self.block

        @block.vector
        def _(v):
            b.wait_allq(v, "S")
            b.wait_allq(v, "G")
            b.wait(v, "sT")
            v.drain()
            b.inc(v.tensor_copy(out=xbf[:], in_=self.X[:]), "sV", 1)

        @block.scalar
        def _(s):
            b.wait_allq(s, "S")
            b.wait_allq(s, "G")
            b.wait(s, "sV")
            b.wait(s, "sT")
            b.inc(s.activation(out=x2bf[:], in_=self.X[:], func=AF.Square,
                               bias=self.consts[:, 0:1], scale=1.0), "sS", 1)

    def emit_ln_stats(self, xbf, x2bf, rb, cb):
        b, block, pb = self.b, self.block, self.pb
        D_inv = 1.0 / D
        epscol = self.consts[:, 2:3]
        vc0 = b.cnt["sV"]
        sc0 = b.cnt["sS"]

        @block.tensor
        def _(t):
            t.wait_ge(b.sems["sV"], vc0)
            t.wait_ge(b.sems["sS"], sc0)
            for tc in range(TC):
                if tc >= 2:
                    t.wait_ge(b.sems["sV"], vc0 + 2 * (tc - 2) + 1)
                for cc in range(CC):
                    csl = slice(T * cc + 512 * tc, T * cc + 512 * tc + 512)
                    t.matmul(pb[2 * (tc % 2)][:, :], self.ones_bf[:], xbf[:, csl],
                             start=(cc == 0), stop=(cc == CC - 1))
                    inst = t.matmul(pb[2 * (tc % 2) + 1][:, :], self.ones_bf[:],
                                    x2bf[:, csl], start=(cc == 0), stop=(cc == CC - 1))
                b.inc(inst, "sT", 1)
        tmm = b.cnt["sT"]

        for tc in range(TC):
            sl = slice(512 * tc, 512 * (tc + 1))
            p_sx = pb[2 * (tc % 2)]
            p_sxx = pb[2 * (tc % 2) + 1]
            p_tmp = pb[4 + (tc % 2)]

            @block.vector
            def _(v, tc=tc, sl=sl, p_sx=p_sx, p_sxx=p_sxx):
                v.wait_ge(b.sems["sT"], tmm - TC + tc + 1)
                v.drain()
                v.tensor_scalar(out=cb[:, sl], in0=p_sx[:, :], scalar1=D_inv,
                                scalar2=None, op0=OP.mult)
                v.drain()
                v.tensor_tensor(out=rb[:, sl], in0=cb[:, sl], in1=cb[:, sl], op=OP.mult)
                v.drain()
                b.inc(v.scalar_tensor_tensor(out=rb[:, sl], in0=p_sxx[:, :],
                                             scalar=D_inv, in1=rb[:, sl],
                                             op0=OP.mult, op1=OP.subtract), "sV", 1)

            @block.scalar
            def _(s, tc=tc, sl=sl, p_tmp=p_tmp):
                s.wait_ge(b.sems["sV"], vc0 + 2 * tc + 1)
                if tc >= 2:
                    s.wait_ge(b.sems["sV"], vc0 + 2 * (tc - 2) + 2)
                b.inc(s.activation(out=p_tmp[:, :], in_=rb[:, sl], func=AF.Sqrt,
                                   bias=epscol, scale=1.0), "sS", 1)

            @block.vector
            def _(v, tc=tc, sl=sl, p_tmp=p_tmp):
                v.wait_ge(b.sems["sS"], sc0 + tc + 1)
                v.drain()
                v.reciprocal(out=rb[:, sl], in_=p_tmp[:, :])
                v.drain()
                b.inc(v.tensor_tensor(out=cb[:, sl], in0=cb[:, sl], in1=rb[:, sl],
                                      op=OP.mult), "sV", 1)

    def emit_flat_allreduce_max(self, col_ap, st):
        b, block, dsc = self.b, self.block, self.dsc
        vc = b.cnt["sV"]

        @block.gpsimd
        def _(g):
            g.wait_ge(b.sems["sV"], vc)
            tok = b.dma(g, dsc["vec1"].ap(), col_ap, "G")
            b.wtok(g, tok)
            tok = b.dma(g, st[0:1, 65:193],
                        dsc["vec1"].ap().rearrange("a b -> b a"), "G")
            b.wtok(g, tok)
        g1 = b.cnt["dG" + str(b.rot["G"])], b.rot["G"]

        @block.vector
        def _(v):
            b.wait_allq(v, "G")
            v.drain()
            b.inc(v.reduce_max(out=st[0:1, 64:65], in_=st[0:1, 65:193],
                               axis=AX.X), "sV", 1)
        vc2 = b.cnt["sV"]

        @block.gpsimd
        def _(g):
            g.wait_ge(b.sems["sV"], vc2)
            tok = b.dma(g, dsc["ccin"].ap(), st[0:1, 64:65], "G")
            b.wtok(g, tok)
            b.inc(g.collective_compute(
                "AllGather", OP.bypass, replica_groups=[list(range(NCORES))],
                ins=[dsc["ccin"].ap()], outs=[dsc["ccout"].ap()]), "cc", 1)
            b.wait(g, "cc")
            tok = b.dma(g, st[0:1, 65:65 + NCORES],
                        dsc["ccout"].ap().rearrange("a b -> b a"), "G")
            b.wtok(g, tok)

        @block.vector
        def _(v):
            b.wait_allq(v, "G")
            v.drain()
            b.inc(v.reduce_max(out=st[0:1, 64:65], in_=st[0:1, 65:65 + NCORES],
                               axis=AX.X), "sV", 1)

    def emit_act_quant_scalars(self, gmax_cell, clip_cell, inv_col, fac_col, fac_scale):
        b, block = self.b, self.block
        st = self.stat

        @block.scalar
        def _(sc):
            b.wait(sc, "sV")
            b.inc(sc.activation(out=st[0:1, 70:71], in_=clip_cell, func=AF.Sigmoid,
                                bias=self.consts[0:1, 0:1], scale=1.0), "sS", 1)

        @block.vector
        def _(v):
            b.wait(v, "sS")
            v.drain()
            v.tensor_tensor(out=st[0:1, 71:72], in0=gmax_cell, in1=st[0:1, 70:71],
                            op=OP.mult)
            v.drain()
            v.tensor_scalar(out=st[0:1, 72:73], in0=st[0:1, 71:72], scalar1=1.0 / 127.0,
                            scalar2=None, op0=OP.mult)
            v.drain()
            v.reciprocal(out=st[0:1, 73:74], in_=st[0:1, 72:73])
            v.drain()
            b.inc(v.tensor_scalar(out=st[0:1, 74:75], in0=st[0:1, 72:73],
                                  scalar1=fac_scale, scalar2=None, op0=OP.mult), "sV", 1)
        vc = b.cnt["sV"]

        @block.gpsimd
        def _(g):
            g.wait_ge(b.sems["sV"], vc)
            tok = b.dma(g, self.dsc["vec"].ap()[0:1, 0:2], st[0:1, 73:75], "G")
            b.wtok(g, tok)
            tok = b.dma(g, self.stat2col[:, 0:2],
                        self.dsc["vec"].ap()[0:1, 0:2].partition_broadcast(128), "G")
            b.wtok(g, tok)

        @block.vector
        def _(v):
            b.wait_allq(v, "G")
            v.drain()
            v.tensor_copy(out=inv_col, in_=self.stat2col[:, 0:1])
            v.drain()
            b.inc(v.tensor_copy(out=fac_col, in_=self.stat2col[:, 1:2]), "sV", 1)

    def ph_init(self):
        b, block, din = self.b, self.block, self.din

        @block.sync
        def _(q):
            b.gbegin("S", q)
            b.dinc(q.dma_start(out=self.X[:].rearrange("p (c t) -> p c t", c=CC),
                               in_=din["x_t"].ap().rearrange("(c p) t -> p c t", p=128)), "S")
            b.dinc(q.dma_start(out=self.ident[:], in_=din["ident"].ap()), "S")
            b.dinc(q.dma_start(out=self.tri[:], in_=din["tri"].ap()), "S")
            for i, nm in enumerate(["bq", "bk", "bv", "bo", "ln_g", "ln_b"]):
                b.dinc(q.dma_start(out=self.biases[:, i * CC:(i + 1) * CC],
                                   in_=din[nm].ap()), "S")
            b.dinc(q.dma_start(out=self.clip_sb[0:1, 0:1], in_=din["clip_x"].ap()), "S")
            b.dinc(q.dma_start(out=self.clip_sb[0:1, 1:2], in_=din["clip_down"].ap()), "S")

        @block.vector
        def _(v):
            v.memset(self.onecol_bf[:], 1.0)
            v.memset(self.ones_bf[:], 1.0)
            v.memset(self.consts[:], 0.0)
            v.drain()
            v.memset(self.consts[:, 1:2], 1.0)
            v.drain()
            v.memset(self.consts[:, 2:3], EPS_LN)
            v.drain()
            v.memset(self.consts[:, 3:4], 128.5)
            v.drain()
            v.memset(self.consts[:, 4:5], -10.0)
            v.drain()
            b.inc(v.memset(self.consts[:, 5:6], -50.0), "sV", 1)

    def ph_wquant(self):
        b, block, din, dsc = self.b, self.block, self.din, self.dsc
        b.push()
        jobs = [("w_gate", "qwg", 0, 22, 1024), ("w_up", "qwu", 22, 44, 1024),
                ("w_down", "qwd", 44, 52, 2816)]
        wt = [b.scr(f"wt{i}", [128, 2816], F32) for i in range(2)]
        u8t = b.scr("wu8", [128, 2816], U8)
        qbt = [b.scr(f"wqb{i}", [128, 2816], BF16) for i in range(2)]
        ld_t, v_t, st_t = {}, {}, {}
        it = 0
        for (src, dst, c0, c1, W) in jobs:
            for r in range(c1 - c0):
                i = it; it += 1
                bi = i % 2

                @block.sync
                def _(q, src=src, r=r, W=W, bi=bi, i=i):
                    if i >= 2:
                        q.wait_ge(b.sems["sV"], v_t[i - 2])
                    ld_t[i] = b.dma(q, wt[bi][:, 0:W],
                                    din[src].ap()[128 * r:128 * (r + 1), :], "S")

                @block.vector
                def _(v, c0=c0, r=r, W=W, bi=bi, i=i):
                    b.wtok(v, ld_t[i])
                    if i >= 2:
                        b.wtok(v, st_t[i - 2])
                    v.drain()
                    v.reduce_max(out=self.wsc[:, c0 + r:c0 + r + 1], in_=wt[bi][:, 0:W],
                                 axis=AX.X, apply_absolute_value=True)
                    v.drain()
                    v.reciprocal(out=self.stat[:, 2:3], in_=self.wsc[:, c0 + r:c0 + r + 1])
                    v.drain()
                    v.tensor_scalar(out=self.stat[:, 3:4], in0=self.stat[:, 2:3],
                                    scalar1=127.0, scalar2=None, op0=OP.mult)
                    v.drain()
                    v.tensor_scalar(out=u8t[:, 0:W], in0=wt[bi][:, 0:W],
                                    scalar1=self.stat[:, 3:4], scalar2=128.0,
                                    op0=OP.mult, op1=OP.add)
                    v.drain()
                    b.inc(v.tensor_scalar(out=qbt[bi][:, 0:W], in0=u8t[:, 0:W],
                                          scalar1=128.0, scalar2=None, op0=OP.subtract),
                          "sV", 1)
                    v_t[i] = b.cnt["sV"]

                @block.gpsimd
                def _(g, dst=dst, r=r, W=W, bi=bi, i=i):
                    g.wait_ge(b.sems["sV"], v_t[i])
                    st_t[i] = b.dma(g, dsc[dst].ap()[128 * r:128 * (r + 1), :],
                                    qbt[bi][:, 0:W], "G")
        self.wq_store_done = [st_t[it - 1 - k] for k in range(min(NROT, it))]
        self.tap_now("qwg", dsc["qwg"].ap())
        b.pop()

    def ph_l0ffn(self, call):
        b, block, din, dsc, pb = self.b, self.block, self.din, self.dsc, self.pb
        X, st = self.X, self.stat
        b.push()
        rb = b.scr("rb", [128, T], F32)
        cb = b.scr("cb", [128, T], F32)
        b.push()
        xbf = b.scr("xbf", [128, CC * T], BF16)
        x2bf = b.scr("x2bf", [128, CC * T], BF16)
        self.emit_xbf_copies(xbf, x2bf)
        self.emit_ln_stats(xbf, x2bf, rb, cb)
        b.pop()
        xq = b.scr("xq", [128, CC * T], BF16)
        xtmp = b.scr("xtmp", [128, T], F32)
        u8t = b.scr("u8t", [128, T], U8)

        @block.vector
        def _(v):
            v.drain()
            v.memset(st[:, 0:1], 0.0)
            for cc in range(CC):
                v.drain()
                v.tensor_tensor(out=xtmp[:], in0=X[:, T * cc:T * (cc + 1)], in1=rb[:],
                                op=OP.mult)
                v.drain()
                v.tensor_tensor(out=xtmp[:], in0=xtmp[:], in1=cb[:], op=OP.subtract)
                v.drain()
                v.reduce_max(out=st[:, 1:2], in_=xtmp[:], axis=AX.X,
                             apply_absolute_value=True)
                v.drain()
                inst = v.tensor_tensor(out=st[:, 0:1], in0=st[:, 0:1], in1=st[:, 1:2],
                                       op=OP.max)
            b.inc(inst, "sV", 1)
        self.emit_flat_allreduce_max(st[:, 0:1], st)
        self.emit_act_quant_scalars(st[0:1, 64:65], self.clip_sb[0:1, 0:1],
                                    st[:, 5:6], st[:, 6:7], 1.0 / 127.0)

        @block.vector
        def _(v):
            b.wait(v, "sV")
            v.drain()
            v.tensor_scalar(out=st[:, 8:30], in0=self.wsc[:, 0:22],
                            scalar1=st[:, 6:7], scalar2=None, op0=OP.mult)
            v.drain()
            b.inc(v.tensor_scalar(out=st[:, 30:52], in0=self.wsc[:, 22:44],
                                  scalar1=st[:, 6:7], scalar2=None, op0=OP.mult), "sV", 1)

        xq_cc_t = {}

        @block.vector
        def _(v):
            for cc in range(CC):
                v.drain()
                v.tensor_tensor(out=xtmp[:], in0=X[:, T * cc:T * (cc + 1)], in1=rb[:],
                                op=OP.mult)
                v.drain()
                v.tensor_tensor(out=xtmp[:], in0=xtmp[:], in1=cb[:], op=OP.subtract)
                v.drain()
                v.tensor_scalar(out=u8t[:], in0=xtmp[:], scalar1=st[:, 5:6],
                                scalar2=128.0, op0=OP.mult, op1=OP.add)
                v.drain()
                b.inc(v.tensor_scalar(out=xq[:, T * cc:T * (cc + 1)], in0=u8t[:],
                                      scalar1=128.0, scalar2=None, op0=OP.subtract),
                      "sV", 1)
                xq_cc_t[cc] = b.cnt["sV"]
        xq_ready = b.cnt["sV"]
        self.tap_now("xq", xq[:], wait=("sV",))
        self.tap_now("rb", rb[:], wait=("sV",))
        self.tap_now("cb", cb[:], wait=("sV",))
        self.tap_now("gmaxrow", st[0:1, 64:80], wait=("sV",))

        gblk = [b.scr(f"gblk{c}", [128, 1408], BF16) for c in range(CC)]
        ublk = [b.scr(f"ublk{c}", [128, 1408], BF16) for c in range(CC)]
        sg4 = b.scr("sg4", [128, 4 * 512], BF16)
        htl = [b.scr(f"htl{i}", [128, T], F32) for i in range(2)]
        ICG, ICN = 2, 11
        half_mm_t = {}
        half_ev_t = {}
        h_st_t = {}
        wld_t = {}

        @block.vector
        def _(v):
            v.drain()
            b.inc(v.memset(st[:, 2:3], 0.0), "sV", 1)

        for icg in range(ICG):
            @block.sync
            def _(q, icg=icg):
                if icg >= 1:
                    q.wait_ge(b.sems["sT"], half_mm_t[(icg - 1, ICN - 1, 3)])
                else:
                    b.wait(q, "sT")
                for tk in self.wq_store_done:
                    b.wtok(q, tk)
                r0 = icg * 1408
                b.gbegin("S", q)
                for cc in range(CC):
                    b.dinc(q.dma_start_transpose(
                        gblk[cc][:], dsc["qwg"].ap()[r0:r0 + 1408,
                                                     128 * cc:128 * (cc + 1)]), "S")
                    b.dinc(q.dma_start_transpose(
                        ublk[cc][:], dsc["qwu"].ap()[r0:r0 + 1408,
                                                     128 * cc:128 * (cc + 1)]), "S")
                wld_t[icg] = b.gend("S")

            for ic in range(ICN):
                icga = icg * ICN + ic
                for half in range(4):
                    bG = pb[2 * (half % 4)]
                    bU = pb[2 * (half % 4) + 1]

                    @block.tensor
                    def _(t, icg=icg, ic=ic, half=half, bG=bG, bU=bU, icga=icga):
                        b.wtok(t, wld_t[icg])
                        prev = ((icg, ic, half - 4) if half >= 4 else
                                ((icg, ic - 1, half) if ic > 0
                                 else ((icg - 1, ICN - 1, half) if icg > 0 else None)))
                        if prev is not None and prev in half_ev_t:
                            t.wait_ge(b.sems["sV"], half_ev_t[prev])
                        tc = half
                        for wblk, bank in ((gblk, bG), (ublk, bU)):
                            for cc in range(CC):
                                if icg == 0 and ic == 0 and wblk is gblk:
                                    t.wait_ge(b.sems["sV"], xq_cc_t[cc])
                                lhsT = wblk[cc][:, 128 * ic:128 * (ic + 1)]
                                inst = t.matmul(
                                    bank[:, :], lhsT,
                                    xq[:, T * cc + 512 * tc:T * cc + 512 * tc + 512],
                                    start=(cc == 0), stop=(cc == CC - 1))
                        b.inc(inst, "sT", 1)
                        half_mm_t[(icg, ic, half)] = b.cnt["sT"]

                    @block.scalar
                    def _(sc, icg=icg, ic=ic, half=half, bG=bG, icga=icga):
                        sc.wait_ge(b.sems["sT"], half_mm_t[(icg, ic, half)])
                        tc = half
                        inst = sc.activation(out=sg4[:, 512 * tc:512 * (tc + 1)],
                                             in_=bG[:, :], func=AF.Silu,
                                             scale=st[:, 8 + icga:9 + icga])
                        b.inc(inst, "sS", 1)

                    @block.vector
                    def _(v, icg=icg, ic=ic, half=half, bU=bU, icga=icga):
                        b.wait(v, "sS")
                        if icga >= 2 and half == 0:
                            b.wtok(v, h_st_t[icga - 2])
                        tc = half
                        v.drain()
                        v.scalar_tensor_tensor(
                            out=htl[icga % 2][:, 512 * tc:512 * (tc + 1)],
                            in0=bU[:, :], scalar=st[:, 30 + icga:31 + icga],
                            in1=sg4[:, 512 * tc:512 * (tc + 1)],
                            op0=OP.mult, op1=OP.mult)
                        v.drain()
                        v.reduce_max(out=st[:, 1:2],
                                     in_=htl[icga % 2][:, 512 * tc:512 * (tc + 1)],
                                     axis=AX.X, apply_absolute_value=True)
                        v.drain()
                        inst = v.tensor_tensor(out=st[:, 2:3], in0=st[:, 2:3],
                                               in1=st[:, 1:2], op=OP.max)
                        b.inc(inst, "sV", 1)
                        half_ev_t[(icg, ic, half)] = b.cnt["sV"]

                @block.gpsimd
                def _(g, icg=icg, ic=ic, icga=icga):
                    g.wait_ge(b.sems["sV"], half_ev_t[(icg, ic, 3)])
                    h_st_t[icga] = b.dma(
                        g, dsc["hsp"].ap()[128 * icga:128 * (icga + 1), :],
                        htl[icga % 2][:], "G")

        @block.gpsimd
        def _(g):
            if "hrows" in self.taps:
                for tk in h_st_t.values():
                    b.wtok(g, tk)
        self.tap_now("hrows", dsc["hsp"].ap()[0:256, :], wait=())

        @block.vector
        def _(v):
            v.drain()
            b.inc(v.tensor_copy(out=st[:, 0:1], in_=st[:, 2:3]), "sV", 1)
        self.emit_flat_allreduce_max(st[:, 0:1], st)
        self.emit_act_quant_scalars(st[0:1, 64:65], self.clip_sb[0:1, 1:2],
                                    st[:, 5:6], st[:, 6:7], 0.5 / 127.0)

        @block.vector
        def _(v):
            b.wait(v, "sV")
            v.drain()
            b.inc(v.tensor_scalar(out=st[:, 8:16], in0=self.wsc[:, 44:52],
                                  scalar1=st[:, 6:7], scalar2=None, op0=OP.mult), "sV", 1)
        dsc_ready = b.cnt["sV"]
        b.pop()

        b.push()
        ICN2 = 11
        hq = b.scr("hq", [128, ICN2 * T], BF16)
        wdb = [b.scr(f"wdb{i}", [128, 1024], BF16) for i in range(ICN2)]
        ht2 = [b.scr(f"ht2{i}", [128, T], F32) for i in range(2)]
        u8b = b.scr("u8b", [128, T], U8)
        hq_q_t = {}
        hl_t = {}
        dn_mm_t = {}
        dn_ev_t = {}
        for rnd in range(2):
            for ic in range(ICN2):
                icga = rnd * ICN2 + ic

                @block.gpsimd
                def _(g, rnd=rnd, ic=ic, icga=icga):
                    b.wtok(g, h_st_t[icga])
                    if icga >= 2:
                        g.wait_ge(b.sems["sV"], hq_q_t[icga - 2])
                    if icga < 2:
                        g.wait_ge(b.sems["sT"], half_mm_t[(ICG - 1, ICN - 1, 1)])
                    if rnd == 1 and ic == 0:
                        g.wait_ge(b.sems["sT"], dn_mm_t[(0, 3, 1)])
                    hl_t[icga] = b.dma(
                        g, ht2[icga % 2][:],
                        dsc["hsp"].ap()[128 * icga:128 * (icga + 1), :], "G")

                @block.vector
                def _(v, ic=ic, icga=icga):
                    b.wtok(v, hl_t[icga])
                    v.wait_ge(b.sems["sV"], dsc_ready)
                    v.drain()
                    v.tensor_scalar(out=u8b[:], in0=ht2[icga % 2][:], scalar1=st[:, 5:6],
                                    scalar2=128.0, op0=OP.mult, op1=OP.add)
                    v.drain()
                    b.inc(v.tensor_scalar(out=hq[:, T * ic:T * (ic + 1)], in0=u8b[:],
                                          scalar1=128.0, scalar2=None, op0=OP.subtract),
                          "sV", 1)
                    hq_q_t[icga] = b.cnt["sV"]

            @block.sync
            def _(q, rnd=rnd):
                if rnd == 1:
                    q.wait_ge(b.sems["sT"], dn_mm_t[(0, 3, 1)])
                else:
                    q.wait_ge(b.sems["sT"], half_mm_t[(ICG - 1, ICN - 1, 3)])
                b.gbegin("S", q)
                for ic in range(ICN2):
                    icga = rnd * ICN2 + ic
                    b.dinc(q.dma_start_transpose(
                        wdb[ic][:, :],
                        dsc["qwd"].ap()[:, 128 * icga:128 * (icga + 1)]), "S")
            wdl = b.gend("S")

            for sw in range(4):
                hcg, tcg = sw // 2, sw % 2
                for sb in range(2):
                    k = (rnd * 4 + sw) * 2 + sb

                    @block.tensor
                    def _(t, rnd=rnd, sw=sw, sb=sb, hcg=hcg, tcg=tcg, wdl=wdl, k=k):
                        b.wtok(t, wdl)
                        t.wait_ge(b.sems["sV"], hq_q_t[rnd * ICN2 + ICN2 - 1])
                        if k >= 2:
                            t.wait_ge(b.sems["sV"], dn_ev_t[k - 2])
                        for ic in range(ICN2):
                            for hjl in range(2):
                                hj = 2 * sb + hjl
                                hc = 4 * hcg + hj
                                for tj in range(2):
                                    tc = 2 * tcg + tj
                                    inst = t.matmul(
                                        pb[4 * sb + 2 * hjl + tj][:, :],
                                        wdb[ic][:, 128 * hc:128 * (hc + 1)],
                                        hq[:, T * ic + 512 * tc:T * ic + 512 * tc + 512],
                                        start=(ic == 0), stop=(ic == ICN2 - 1))
                        b.inc(inst, "sT", 1)
                        dn_mm_t[(rnd, sw, sb)] = b.cnt["sT"]

                    @block.vector
                    def _(v, rnd=rnd, sw=sw, sb=sb, hcg=hcg, tcg=tcg, k=k):
                        v.wait_ge(b.sems["sT"], dn_mm_t[(rnd, sw, sb)])
                        for hjl in range(2):
                            hj = 2 * sb + hjl
                            hc = 4 * hcg + hj
                            for tj in range(2):
                                tc = 2 * tcg + tj
                                v.drain()
                                inst = v.scalar_tensor_tensor(
                                    out=X[:, T * hc + 512 * tc:T * hc + 512 * tc + 512],
                                    in0=pb[4 * sb + 2 * hjl + tj][:, :],
                                    scalar=st[:, 8 + hc:9 + hc],
                                    in1=X[:, T * hc + 512 * tc:T * hc + 512 * tc + 512],
                                    op0=OP.mult, op1=OP.add)
                        v.drain()
                        b.inc(inst, "sV", 1)
                        dn_ev_t[k] = b.cnt["sV"]
        b.pop()

    def emit_ln_bf16_apply(self, xbf, rb, cb, rbbf, cbbf):
        b, block = self.b, self.block

        @block.vector
        def _(v):
            b.wait(v, "sS")
            v.drain()
            v.tensor_copy(out=rbbf[:], in_=rb[:])
            v.drain()
            v.tensor_copy(out=cbbf[:], in_=cb[:])
            v.drain()
            for cc in range(CC):
                sl = slice(T * cc, T * (cc + 1))
                v.tensor_tensor(out=xbf[:, sl], in0=xbf[:, sl], in1=rbbf[:], op=OP.mult)
                v.drain()
                inst = v.tensor_tensor(out=xbf[:, sl], in0=xbf[:, sl], in1=cbbf[:],
                                       op=OP.subtract)
                v.drain()
            b.inc(inst, "sV", 1)

    def ph_l0att(self):
        b, block, din, dsc, pb, PBt = self.b, self.block, self.din, self.dsc, self.pb, self.PBt
        X, st = self.X, self.stat
        NB = 32
        b.push()
        xbf = b.scr("xbf2", [128, CC * T], BF16)
        b.push()
        x2bf = b.scr("x2bf2", [128, CC * T], BF16)
        rb = b.scr("rb2", [128, T], F32)
        cb = b.scr("cb2", [128, T], F32)
        rbbf = b.scr("rbbf", [128, T], BF16)
        cbbf = b.scr("cbbf", [128, T], BF16)
        self.emit_xbf_copies(xbf, x2bf)
        self.emit_ln_stats(xbf, x2bf, rb, cb)
        self.emit_ln_bf16_apply(xbf, rb, cb, rbbf, cbbf)
        b.pop()
        xn = xbf
        xn_ready = b.cnt["sV"]

        acc = b.scr("acc", [128, 8 * 66], F32)
        accbf = b.scr("accbf", [128, 8 * 66], BF16)
        sst = b.scr("sst", [128, 512], F32)
        qw = [b.scr_at(f"qw{i}", [128, 128], BF16, "acc") if i < 8 else
              b.scr_at(f"qw{i}", [128, 128], BF16, "accbf") for i in range(0)]
        ktm = b.scr("ktm", [128, NT * 1024], BF16)
        vtm = b.scr("vtm", [128, NT * 1024], BF16)
        kv_done = {}

        b.push()
        kvw = [b.scr(f"kvw{c}", [128, 1024], BF16) for c in range(CC)]
        for which, dst in (("k", ktm), ("v", vtm)):
            col0 = 1024 if which == "k" else 2048
            wtoks = []

            @block.gpsimd
            def _(g, which=which, col0=col0, wtoks=wtoks):
                b.wait(g, "sT")
                if which == "k":
                    b.wait(g, "sV")
                for c in range(CC):
                    wtoks.append(b.dma(
                        g, kvw[c][:],
                        din["w_qkv_t"].ap()[128 * c:128 * (c + 1), col0:col0 + 1024], "G"))

            mm_t = {}
            ev_t = {}
            for tt in range(NT):
                bk0 = pb[4 + 2 * (tt % 2)]
                bk1 = pb[5 + 2 * (tt % 2)]

                @block.tensor
                def _(t, which=which, tt=tt, bk0=bk0, bk1=bk1, wtoks=wtoks):
                    if tt == 0:
                        t.wait_ge(b.sems["sV"], xn_ready)
                        for tk in wtoks:
                            b.wtok(t, tk)
                    if tt < 2 and which == "v":
                        t.wait_ge(b.sems["sS"], kv_done["k"][1])
                    if tt >= 2:
                        t.wait_ge(b.sems["sS"], ev_t[tt - 2])
                    for c in range(CC):
                        lhsT = xn[:, T * c + 128 * tt:T * c + 128 * (tt + 1)]
                        t.matmul(bk0[:, :], lhsT, kvw[c][:, 0:512],
                                 start=(c == 0), stop=(c == CC - 1))
                        inst = t.matmul(bk1[:, :], lhsT, kvw[c][:, 512:1024],
                                        start=(c == 0), stop=(c == CC - 1))
                    b.inc(inst, "sT", 1)
                    mm_t[tt] = b.cnt["sT"]

                @block.scalar
                def _(sc, which=which, tt=tt, bk0=bk0, bk1=bk1):
                    sc.wait_ge(b.sems["sT"], mm_t[tt])
                    if which == "k":
                        o = ktm[:, 1024 * tt:1024 * (tt + 1)]
                        sc.activation(out=o[:, 0:512], in_=bk0[:, :], func=AF.Exp,
                                      bias=self.consts[:, 0:1], scale=1.0)
                        inst = sc.activation(out=o[:, 512:1024], in_=bk1[:, :], func=AF.Exp,
                                             bias=self.consts[:, 0:1], scale=1.0)
                    else:
                        o = vtm[:, 1024 * tt:1024 * (tt + 1)]
                        sc.copy(out=o[:, 0:512], in_=bk0[:, :])
                        inst = sc.copy(out=o[:, 512:1024], in_=bk1[:, :])
                    b.inc(inst, "sS", 1)
                    ev_t[tt] = b.cnt["sS"]
            kv_done[which] = (mm_t[NT - 1], b.cnt["sS"])
        b.pop()

        b.push()
        expq = b.scr("expq", [128, CC * T], BF16)
        qwbase = self.nc.alloc_sbuf_tensor_at(
            "scr_qwall", [128, 16 * 128], BF16,
            offset=b.scratch_addr + b.offs["acc"][0])
        qw = [qwbase[:, 128 * i:128 * (i + 1)] for i in range(16)]
        q_mm_t = {}
        q_ev_t = {}
        qw_tok = {}
        for hp in range(CC):
            @block.gpsimd
            def _(g, hp=hp):
                if hp >= 2:
                    g.wait_ge(b.sems["sT"], q_mm_t[hp - 2])
                else:
                    b.wait(g, "sT")
                toks = []
                for c in range(CC):
                    toks.append(b.dma(
                        g, qw[8 * (hp % 2) + c][:],
                        din["w_qkv_t"].ap()[128 * c:128 * (c + 1),
                                            128 * hp:128 * (hp + 1)], "G"))
                qw_tok[hp] = toks

            bks = [pb[4 * (hp % 2) + j] for j in range(4)]

            @block.tensor
            def _(t, hp=hp, bks=bks):
                for tk in qw_tok[hp]:
                    b.wtok(t, tk)
                if hp < 2:
                    t.wait_ge(b.sems["sS"], kv_done["v"][1])
                if hp >= 2:
                    t.wait_ge(b.sems["sS"], q_ev_t[hp - 2])
                for c in range(CC):
                    lhsT = qw[8 * (hp % 2) + c][:]
                    for tc in range(TC):
                        inst = t.matmul(bks[tc][:, :], lhsT,
                                        xn[:, T * c + 512 * tc:T * c + 512 * (tc + 1)],
                                        start=(c == 0), stop=(c == CC - 1))
                b.inc(inst, "sT", 1)
                q_mm_t[hp] = b.cnt["sT"]

            @block.scalar
            def _(sc, hp=hp, bks=bks):
                sc.wait_ge(b.sems["sT"], q_mm_t[hp])
                for tc in range(TC):
                    inst = sc.activation(out=expq[:, T * hp + 512 * tc:T * hp + 512 * (tc + 1)],
                                         in_=bks[tc][:, :], func=AF.Exp,
                                         bias=self.consts[:, 0:1], scale=1.0)
                b.inc(inst, "sS", 1)
                q_ev_t[hp] = b.cnt["sS"]

        attnstore = xn

        @block.vector
        def _(v):
            v.wait_ge(b.sems["sS"], q_ev_t[CC - 1])
            v.wait_ge(b.sems["sS"], kv_done["v"][1])
            v.drain()
            v.memset(acc[:], 0.0)
            v.drain()
            v.memset(accbf[:], 0.0)
            v.drain()
            v.memset(acc.rearrange("p (h z) -> p h z", h=8)[:, :, 65:66], 1.0)
            v.drain()
            b.inc(v.memset(accbf.rearrange("p (h z) -> p h z", h=8)[:, :, 65:66], 1.0),
                  "sV", 1)
        acc_ready = b.cnt["sV"]

        att_mm_t = {}
        ctx_mm_t = {}
        v1_t = {}
        v2_t = {}
        sst_dma_t = {}
        for n in range(NB):
            r = n % 2
            pctx = PBt[:, 1024 * r:1024 * r + 512]
            patt = PBt[:, 2048 + 1024 * r:2048 + 1024 * r + 512]
            psts = PBt[:, 2560 + 1024 * r:2560 + 1024 * r + 512]
            psum_k = PBt[:, 1024 * r + 512:1024 * r + 512 + 8]
            tt, half = n // 2, n % 2
            krows = slice(64 * half, 64 * half + 64)

            @block.tensor
            def _(t, n=n, pctx=pctx, patt=patt, psts=psts, psum_k=psum_k,
                  tt=tt, krows=krows):
                if n == 0:
                    t.wait_ge(b.sems["sV"], acc_ready)
                if n >= 1:
                    t.wait_ge(b.sems["sV"], v1_t[n - 1])
                if n >= 2:
                    t.wait_ge(b.sems["sV"], v2_t[n - 2])
                for h in range(16):
                    hp, par = h // 2, h % 2
                    qrows = slice(64 * par, 64 * par + 64)
                    rhs = expq[qrows, T * hp + 64 * n:T * hp + 64 * (n + 1)]
                    acol = accbf[qrows, 66 * hp:66 * hp + 66]
                    rbase = 32 * (h // 4)
                    cbase = 128 * (h % 4)
                    t.matmul(patt[slice(64 * par, 64 * par + 64),
                                  64 * hp:64 * (hp + 1)],
                             acol[:, 0:64], rhs, start=True, stop=True)
                    t.matmul(psts[rbase:rbase + 1, cbase:cbase + 64],
                             acol[:, 64:65], rhs, start=True, stop=True,
                             tile_position=(64 * par, rbase))
                    inst = t.matmul(psts[rbase:rbase + 1, cbase + 64:cbase + 128],
                                    acol[:, 65:66], rhs, start=True, stop=True,
                                    tile_position=(64 * par, rbase))
                b.inc(inst, "sT", 1)
                att_mm_t[n] = b.cnt["sT"]
                for h in range(16):
                    hp, par = h // 2, h % 2
                    kl = ktm[krows, 1024 * tt + 64 * h:1024 * tt + 64 * (h + 1)]
                    vl = vtm[krows, 1024 * tt + 64 * h:1024 * tt + 64 * (h + 1)]
                    ol = pctx[slice(64 * par, 64 * par + 64), 64 * hp:64 * (hp + 1)]
                    t.matmul(ol, kl, vl, start=True, stop=True)
                    inst = t.matmul(psum_k[slice(64 * par, 64 * par + 64), hp:hp + 1],
                                    kl, self.onecol_bf[krows, 0:1],
                                    start=True, stop=True)
                b.inc(inst, "sT", 1)
                ctx_mm_t[n] = b.cnt["sT"]

            @block.vector
            def _(v, n=n, pctx=pctx, psum_k=psum_k):
                v.wait_ge(b.sems["sT"], ctx_mm_t[n])
                v.drain()
                acc3 = acc.rearrange("p (h z) -> p h z", h=8)
                v.tensor_tensor(out=acc3[:, :, 0:64], in0=acc3[:, :, 0:64],
                                in1=pctx.rearrange("p (h z) -> p h z", h=8), op=OP.add)
                v.drain()
                v.tensor_tensor(out=acc3[:, :, 64:65], in0=acc3[:, :, 64:65],
                                in1=psum_k.rearrange("p (h z) -> p h z", h=8), op=OP.add)
                v.drain()
                inst = v.tensor_copy(
                    out=accbf.rearrange("p (h z) -> p h z", h=8)[:, :, 0:65],
                    in_=acc3[:, :, 0:65])
                v.drain()
                b.inc(inst, "sV", 1)
                v1_t[n] = b.cnt["sV"]

            @block.vector
            def _(v, n=n, patt=patt, psts=psts):
                if n >= 1:
                    b.wtok(v, sst_dma_t[n - 1])
                v.drain()
                a3 = attnstore.rearrange("p (hp tcol) -> p hp tcol", hp=8)
                v.tensor_copy(out=a3[:, :, 64 * n:64 * (n + 1)],
                              in_=patt.rearrange("p (hp z) -> p hp z", hp=8))
                v.drain()
                inst = v.tensor_copy(out=sst[:], in_=psts[:, :])
                v.drain()
                b.inc(inst, "sV", 1)
                v2_t[n] = b.cnt["sV"]

            @block.gpsimd
            def _(g, n=n):
                g.wait_ge(b.sems["sV"], v2_t[n])
                sst_dma_t[n] = b.dma(
                    g, dsc["statsd"].ap()[:, 512 * n:512 * (n + 1)], sst[:], "G")

        s2 = b.scr_at("s2", [128, 512], F32, "sst")
        minvt = b.scr_at("minvt", [128, 256], F32, "accbf")

        @block.gpsimd
        def _(g):
            for n in range(NB):
                b.wtok(g, sst_dma_t[n])
            src = dsc["statsd"].ap().rearrange("(rg s) c -> rg s c", rg=4)[:, 0:1, :] \
                .squeeze(1)
            src = src.rearrange("rg (n c) -> rg n c", n=NB)
            self._l0_sst_rd = b.dma(g, s2[:], src, "G")

        @block.vector
        def _(v):
            b.wtok(v, self._l0_sst_rd)
            v.drain()
            s3 = s2[:].rearrange("p (hs w t) -> p hs w t", hs=4, w=2)
            m3 = minvt[:].rearrange("p (hs t) -> p hs t", hs=4)
            v.scalar_tensor_tensor(out=m3, in0=s3[:, :, 1, :], scalar=8e-6,
                                   in1=s3[:, :, 0, :], op0=OP.mult, op1=OP.max)
            v.drain()
            b.inc(v.reciprocal(out=minvt[:], in_=minvt[:]), "sV", 1)
        minv_ready = b.cnt["sV"]

        @block.gpsimd
        def _(g):
            g.wait_ge(b.sems["sV"], minv_ready)
            toks = []
            for rg in range(4):
                dst = dsc["minvd"].ap()[4 * rg:4 * rg + 4, :] \
                    .rearrange("hs (n t) -> n hs t", n=NB)
                toks.append(b.dma(
                    g, dst,
                    minvt[32 * rg:32 * (rg + 1), :].rearrange(
                        "n (hs t) -> n hs t", hs=4), "G"))
            self._l0_minv_wr = toks

        b.pop()
        gsc = [b.scr(f"gsc{i}", [128, T], BF16) for i in range(2)]
        gsc_tok = {}
        scale_t = {}
        for hp in range(CC):
            @block.gpsimd
            def _(g, hp=hp):
                if hp == 0:
                    for tk in self._l0_minv_wr:
                        b.wtok(g, tk)
                    b.wait(g, "sT")
                if hp >= 2:
                    g.wait_ge(b.sems["sV"], scale_t[hp - 2])
                src = dsc["minvd"].ap()[2 * hp:2 * hp + 2, :]
                gsc_tok[hp] = b.dma(
                    g, gsc[hp % 2][:],
                    src.unsqueeze(1).broadcast_to([2, 64, T]), "G")

            @block.vector
            def _(v, hp=hp):
                b.wtok(v, gsc_tok[hp])
                v.drain()
                sl = slice(T * hp, T * (hp + 1))
                b.inc(v.tensor_tensor(out=attnstore[:, sl], in0=attnstore[:, sl],
                                      in1=gsc[hp % 2][:], op=OP.mult), "sV", 1)
                scale_t[hp] = b.cnt["sV"]

        wob = [b.scr(f"wob{i}", [128, 1024], BF16) for i in range(CC)]

        @block.gpsimd
        def _(g):
            b.wait(g, "sT")
            toks = []
            for c in range(CC):
                toks.append(b.dma(g, wob[c][:],
                                  din["w_out_t"].ap()[128 * c:128 * (c + 1), :], "G"))
            self._l0_wo_tok = toks

        wo_mm_t = {}
        wo_ev_t = {}
        for dc in range(CC):
            bks = [pb[4 * (dc % 2) + j] for j in range(4)]

            @block.tensor
            def _(t, dc=dc, bks=bks):
                if dc == 0:
                    for tk in self._l0_wo_tok:
                        b.wtok(t, tk)
                    t.wait_ge(b.sems["sV"], scale_t[CC - 1])
                if dc >= 2:
                    t.wait_ge(b.sems["sV"], wo_ev_t[dc - 2])
                for c in range(CC):
                    lhsT = wob[c][:, 128 * dc:128 * (dc + 1)]
                    for tc in range(TC):
                        inst = t.matmul(bks[tc][:, :], lhsT,
                                        attnstore[:, T * c + 512 * tc:T * c + 512 * (tc + 1)],
                                        start=(c == 0), stop=(c == CC - 1))
                b.inc(inst, "sT", 1)
                wo_mm_t[dc] = b.cnt["sT"]

            @block.vector
            def _(v, dc=dc, bks=bks):
                v.wait_ge(b.sems["sT"], wo_mm_t[dc])
                for tc in range(TC):
                    v.drain()
                    inst = v.scalar_tensor_tensor(
                        out=X[:, T * dc + 512 * tc:T * dc + 512 * (tc + 1)],
                        in0=bks[tc][:, :], scalar=1.0,
                        in1=X[:, T * dc + 512 * tc:T * dc + 512 * (tc + 1)],
                        op0=OP.mult, op1=OP.add)
                b.inc(inst, "sV", 1)
                wo_ev_t[dc] = b.cnt["sV"]
        b.pop()

    def ph_l1stash(self):
        b, block, din, dsc = self.b, self.block, self.din, self.dsc

        @block.gpsimd
        def _(g):
            toks = []
            for e in range(8):
                toks.append(b.dma(g, dsc["ved"].ap()[e], din["ve_r"].ap()[e], "G"))
            for j in range(8):
                toks.append(b.dma(g, dsc["oed"].ap()[512 * j:512 * (j + 1), :],
                                  din["oe_flat"].ap()[512 * j:512 * (j + 1), :], "G"))
            self.l1stash_tok = toks

    def ph_l1ffn(self, call):
        b, block, din, dsc, pb, PBt = self.b, self.block, self.din, self.dsc, self.pb, self.PBt
        X, st = self.X, self.stat
        b.push()
        xbf = b.scr("xbf3", [128, CC * T], BF16)
        b.push()
        x2bf = b.scr("x2bf3", [128, CC * T], BF16)
        rb = b.scr("rb3", [128, T], F32)
        cb = b.scr("cb3", [128, T], F32)
        rbbf = b.scr("rbbf3", [128, T], BF16)
        cbbf = b.scr("cbbf3", [128, T], BF16)
        self.emit_xbf_copies(xbf, x2bf)
        self.emit_ln_stats(xbf, x2bf, rb, cb)
        self.emit_ln_bf16_apply(xbf, rb, cb, rbbf, cbbf)
        b.pop()
        xn = xbf
        xn_ready = b.cnt["sV"]

        gates = b.scr("gates", [128, NT * 64], F32)
        maskbf = b.scr("maskbf", [128, NT * 64], BF16)
        qkT = b.scr("qkT", [128, 8 * T], BF16)
        values = b.scr("valuesl1", [128, NT * 520], BF16)

        b.push()
        wg1 = b.scr("wg1", [128, 8 * 64], BF16)
        wq1 = b.scr("wq1", [128, 8 * 512], BF16)
        wk1 = b.scr("wk1", [128, 8 * 512], BF16)

        @block.gpsimd
        def _(g):
            b.wait(g, "sT")
            b.wait(g, "sV")
            toks = []
            for c in range(CC):
                toks.append(b.dma(g, wg1[:, 64 * c:64 * (c + 1)],
                                  din["w_g1"].ap()[128 * c:128 * (c + 1), :], "G"))
                toks.append(b.dma(g, wq1[:, 512 * c:512 * (c + 1)],
                                  din["w_q1"].ap()[128 * c:128 * (c + 1), :], "G"))
                toks.append(b.dma(g, wk1[:, 512 * c:512 * (c + 1)],
                                  din["w_k1"].ap()[128 * c:128 * (c + 1), :], "G"))
            self._l1_w_tok = toks

        g_mm_t = {}
        g_ev_t = {}
        for tt in range(NT):
            bkg = pb[6 + (tt % 2)]

            @block.tensor
            def _(t, tt=tt, bkg=bkg):
                if tt == 0:
                    t.wait_ge(b.sems["sV"], xn_ready)
                    for tk in self._l1_w_tok:
                        b.wtok(t, tk)
                if tt >= 2:
                    t.wait_ge(b.sems["sS"], g_ev_t[tt - 2])
                for c in range(CC):
                    inst = t.matmul(bkg[:, 0:64],
                                    xn[:, T * c + 128 * tt:T * c + 128 * (tt + 1)],
                                    wg1[:, 64 * c:64 * (c + 1)],
                                    start=(c == 0), stop=(c == CC - 1))
                b.inc(inst, "sT", 1)
                g_mm_t[tt] = b.cnt["sT"]

            @block.scalar
            def _(sc, tt=tt, bkg=bkg):
                sc.wait_ge(b.sems["sT"], g_mm_t[tt])
                b.inc(sc.activation(out=gates[:, 64 * tt:64 * (tt + 1)], in_=bkg[:, 0:64],
                                    func=AF.Sigmoid, bias=self.consts[:, 0:1],
                                    scale=1.0), "sS", 1)
                g_ev_t[tt] = b.cnt["sS"]
        gates_done = b.cnt["sS"]

        @block.vector
        def _(v):
            v.wait_ge(b.sems["sS"], gates_done)
            v.drain()
            g4 = gates[:].rearrange("p (tt h e) -> p tt h e", tt=NT, h=8)
            v.reduce_max(out=st[:, 16:16 + 128].rearrange("p (tt h) -> p tt h", tt=NT),
                         in_=g4, axis=AX.X)
            v.drain()
            mxb = st[:, 16:16 + 128].rearrange("p (tt h) -> p tt h", tt=NT) \
                .unsqueeze(3).broadcast_to([128, NT, 8, 8])
            b.inc(v.tensor_tensor(out=maskbf[:].rearrange("p (tt h e) -> p tt h e",
                                                          tt=NT, h=8),
                                  in0=g4, in1=mxb, op=OP.is_ge), "sV", 1)
        mask_ready = b.cnt["sV"]

        qk_mm_t = {}
        qk_ev_t = {}
        for i, (which, mc) in enumerate([(w, m) for w in ("q", "k") for m in range(4)]):
            wsb = wq1 if which == "q" else wk1
            qoff = (0 if which == "q" else 4 * T)
            bks = [pb[4 * (i % 2) + j] for j in range(4)]

            @block.tensor
            def _(t, which=which, mc=mc, i=i, wsb=wsb, bks=bks):
                if i < 2:
                    t.wait_ge(b.sems["sS"], gates_done)
                if i >= 2:
                    t.wait_ge(b.sems["sV"], qk_ev_t[i - 2])
                for c in range(CC):
                    lhsT = wsb[:, 512 * c + 128 * mc:512 * c + 128 * (mc + 1)]
                    for tc in range(TC):
                        inst = t.matmul(bks[tc][:, :], lhsT,
                                        xn[:, T * c + 512 * tc:T * c + 512 * (tc + 1)],
                                        start=(c == 0), stop=(c == CC - 1))
                b.inc(inst, "sT", 1)
                qk_mm_t[i] = b.cnt["sT"]

            @block.vector
            def _(v, which=which, mc=mc, i=i, qoff=qoff, bks=bks):
                v.wait_ge(b.sems["sT"], qk_mm_t[i])
                scale = 0.125 if which == "q" else 1.0
                for tc in range(TC):
                    v.drain()
                    inst = v.tensor_scalar(
                        out=qkT[:, qoff + T * mc + 512 * tc:qoff + T * mc + 512 * (tc + 1)],
                        in0=bks[tc][:, :], scalar1=scale, scalar2=None, op0=OP.mult)
                b.inc(inst, "sV", 1)
                qk_ev_t[i] = b.cnt["sV"]
        qk_done = b.cnt["sV"]
        b.pop()

        b.push()
        vew = [b.scr(f"vew{j}", [128, 512], BF16) for j in range(32)]
        vtmp = b.scr("vtmp", [128, 512], F32)
        val_mm_t = {}
        val_ev_t = {}
        ve_tok = {}
        for eg in range(2):
            @block.gpsimd
            def _(g, eg=eg):
                for tk in getattr(self, "l1stash_tok", []):
                    b.wtok(g, tk)
                if eg == 0:
                    b.wait(g, "sT")
                    b.wait(g, "sV")
                if eg == 1:
                    t_last = val_mm_t[(NT - 1, 0)]
                    g.wait_ge(b.sems["sT"], t_last)
                for j in range(4):
                    e = 4 * eg + j
                    for c in range(CC):
                        ve_tok[(eg, j, c)] = b.dma(
                            g, vew[8 * j + c][:],
                            dsc["ved"].ap()[e, 128 * c:128 * (c + 1), :], "G")

            for tt in range(NT):
                bks = [pb[4 * (tt % 2) + j] for j in range(4)]

                @block.tensor
                def _(t, tt=tt, eg=eg, bks=bks):
                    if tt == 0:
                        if eg == 0:
                            t.wait_ge(b.sems["sV"], qk_done)
                        for k in list(ve_tok):
                            if k[0] == eg:
                                b.wtok(t, ve_tok[k])
                    prev = (tt - 2, eg) if tt >= 2 else ((NT - 2 + tt, eg - 1) if eg > 0 else None)
                    if prev in val_ev_t:
                        t.wait_ge(b.sems["sV"], val_ev_t[prev])
                    for c in range(CC):
                        lhsT = xn[:, T * c + 128 * tt:T * c + 128 * (tt + 1)]
                        for j in range(4):
                            inst = t.matmul(bks[j][:, :], lhsT, vew[8 * j + c][:],
                                            start=(c == 0), stop=(c == CC - 1))
                    b.inc(inst, "sT", 1)
                    val_mm_t[(tt, eg)] = b.cnt["sT"]

                @block.vector
                def _(v, tt=tt, eg=eg, bks=bks):
                    v.wait_ge(b.sems["sT"], val_mm_t[(tt, eg)])
                    if tt == 0 and eg == 0:
                        v.wait_ge(b.sems["sV"], mask_ready)
                    v3 = values[:, 520 * tt:520 * (tt + 1)].rearrange(
                        "p (h z) -> p h z", h=8)
                    for j in range(4):
                        e = 4 * eg + j
                        mask_ap = maskbf[:].rearrange(
                            "p (tt h e) -> p tt h e", tt=NT, h=8) \
                            [:, tt:tt + 1, :, e:e + 1].squeeze(1) \
                            .broadcast_to([128, 8, 64])
                        dat = bks[j][:, :].rearrange("p (h z) -> p h z", h=8)
                        v.drain()
                        if eg == 0 and j == 0:
                            v.tensor_tensor(out=v3[:, :, 0:64], in0=dat,
                                            in1=mask_ap, op=OP.mult)
                        else:
                            v.tensor_tensor(out=vtmp[:].rearrange(
                                "p (h z) -> p h z", h=8), in0=dat,
                                in1=mask_ap, op=OP.mult)
                            v.drain()
                            v.tensor_tensor(out=v3[:, :, 0:64], in0=v3[:, :, 0:64],
                                            in1=vtmp[:].rearrange(
                                                "p (h z) -> p h z", h=8),
                                            op=OP.add)
                    v.drain()
                    if eg == 1:
                        gsap = st[:, 16 + 8 * tt:16 + 8 * (tt + 1)] \
                            .unsqueeze(2).broadcast_to([128, 8, 64])
                        inst = v.tensor_tensor(out=v3[:, :, 0:64], in0=v3[:, :, 0:64],
                                               in1=gsap, op=OP.mult)
                        v.drain()
                    else:
                        inst = v.nop()
                    b.inc(inst, "sV", 1)
                    val_ev_t[(tt, eg)] = b.cnt["sV"]

        @block.vector
        def _(v):
            v.drain()
            b.inc(v.memset(values[:].rearrange("p (tt h z) -> p tt h z", tt=NT, h=8)
                           [:, :, :, 64:65], 1.0), "sV", 1)
        values_done = b.cnt["sV"]
        b.pop()

        pttm = [b.scr(f"pttm{i}", [128, 1024], BF16) for i in range(2)]
        araw = b.scr("araw", [128, 4 * T], BF16)
        lbuf = b.scr("lbuf", [128, 512], F32)
        s_mm_t = {}
        p_ev_t = {}
        msk_t = {}
        a_mm_t = {}
        a_ev_t = {}
        l_dma_t = {}
        QT = qkT
        pa = PBt[:, 0:512]
        pl = PBt[:, 512:1024]
        for hp in range(4):
            for qc in range(TC):
                jt_list = list(range(4 * qc + 4))
                nji = len(jt_list)
                for ji, jt in enumerate(jt_list):
                    psS = [PBt[:, 1024 + 1024 * (ji % 2) + 512 * par:
                               1024 + 1024 * (ji % 2) + 512 * (par + 1)]
                           for par in range(2)]
                    psSm = PBt[:, 1024 + 1024 * (ji % 2):2048 + 1024 * (ji % 2)]

                    @block.tensor
                    def _(t, hp=hp, qc=qc, jt=jt, ji=ji, nji=nji, psS=psS):
                        if (hp, qc) == (0, 0) and ji == 0:
                            t.wait_ge(b.sems["sV"], values_done)
                        if ji >= 2:
                            t.wait_ge(b.sems["sS"], p_ev_t[(hp, qc, ji - 2)])
                        if ji == 0:
                            prev = ((hp, qc - 1) if qc > 0 else
                                    ((hp - 1, TC - 1) if hp > 0 else None))
                            if prev is not None:
                                t.wait_ge(b.sems["sV"], a_ev_t[prev])
                        for par in range(2):
                            rows = slice(64 * par, 64 * par + 64)
                            inst = t.matmul(
                                psS[par][:, :],
                                QT[rows, 4 * T + T * hp + 128 * jt:
                                   4 * T + T * hp + 128 * (jt + 1)],
                                QT[rows, T * hp + 512 * qc:T * hp + 512 * (qc + 1)],
                                start=True, stop=True)
                        b.inc(inst, "sT", 1)
                        s_mm_t[(hp, qc, ji)] = b.cnt["sT"]

                    @block.scalar
                    def _(sc, hp=hp, qc=qc, jt=jt, ji=ji, psSm=psSm):
                        sc.wait_ge(b.sems["sT"], s_mm_t[(hp, qc, ji)])
                        if ji >= 2:
                            sc.wait_ge(b.sems["sT"], a_mm_t[(hp, qc, ji - 2)])
                        inst = sc.activation(out=pttm[ji % 2][:], in_=psSm[:, :],
                                             func=AF.Exp,
                                             bias=self.consts[:, 4:5], scale=1.0)
                        b.inc(inst, "sS", 1)
                        p_ev_t[(hp, qc, ji)] = b.cnt["sS"]

                    if jt >= 4 * qc:
                        @block.vector
                        def _(v, hp=hp, qc=qc, jt=jt, ji=ji):
                            v.wait_ge(b.sems["sS"], p_ev_t[(hp, qc, ji)])
                            bidx = jt - 4 * qc
                            v.drain()
                            for par in range(2):
                                inst = v.tensor_tensor(
                                    out=pttm[ji % 2][:, 512 * par:512 * (par + 1)],
                                    in0=pttm[ji % 2][:, 512 * par:512 * (par + 1)],
                                    in1=self.tri[:, 512 * bidx:512 * (bidx + 1)],
                                    op=OP.mult)
                                v.drain()
                            b.inc(inst, "sV", 1)
                            msk_t[(hp, qc, ji)] = b.cnt["sV"]

                    @block.tensor
                    def _(t, hp=hp, qc=qc, jt=jt, ji=ji, nji=nji):
                        t.wait_ge(b.sems["sS"], p_ev_t[(hp, qc, ji)])
                        if (hp, qc, ji) in msk_t:
                            t.wait_ge(b.sems["sV"], msk_t[(hp, qc, ji)])
                        for par in range(2):
                            h = 2 * hp + par
                            vl = values[:, 520 * jt + 65 * h:520 * jt + 65 * h + 64]
                            ol = values[:, 520 * jt + 65 * h + 64:520 * jt + 65 * h + 65]
                            pt = pttm[ji % 2][:, 512 * par:512 * (par + 1)]
                            t.matmul(pa[slice(64 * par, 64 * par + 64), :], vl, pt,
                                     start=(ji == 0), stop=(ji == nji - 1))
                            inst = t.matmul(pl[32 * par:32 * par + 1, :], ol, pt,
                                            start=(ji == 0), stop=(ji == nji - 1),
                                            tile_position=(0, 32 * par))
                        b.inc(inst, "sT", 1)
                        a_mm_t[(hp, qc, ji)] = b.cnt["sT"]

                @block.vector
                def _(v, hp=hp, qc=qc, nji=nji):
                    v.wait_ge(b.sems["sT"], a_mm_t[(hp, qc, nji - 1)])
                    if (hp, qc) != (0, 0):
                        prev = ((hp, qc - 1) if qc > 0 else (hp - 1, TC - 1))
                        b.wtok(v, l_dma_t[prev])
                    v.drain()
                    v.tensor_copy(out=araw[:, T * hp + 512 * qc:T * hp + 512 * (qc + 1)],
                                  in_=pa[:, :])
                    v.drain()
                    inst = v.tensor_copy(out=lbuf[:], in_=pl[:, :])
                    v.drain()
                    b.inc(inst, "sV", 1)
                    a_ev_t[(hp, qc)] = b.cnt["sV"]

                @block.gpsimd
                def _(g, hp=hp, qc=qc):
                    g.wait_ge(b.sems["sV"], a_ev_t[(hp, qc)])
                    l_dma_t[(hp, qc)] = b.dma(
                        g, dsc["lst"].ap()[:, 512 * (4 * hp + qc):512 * (4 * hp + qc + 1)],
                        lbuf[:], "G")

        lsb2 = b.scr("lsb2", [128, 128], F32)

        @block.gpsimd
        def _(g):
            for k in l_dma_t.values():
                b.wtok(g, k)
            src = dsc["lst"].ap()[:, 0:16 * 512] \
                .rearrange("(ra rb) c -> ra rb c", ra=4)[0:2, 0:1, :].squeeze(1)
            src = src.rearrange("par (hp qc q) -> par hp qc q", hp=4, qc=4)
            dst = dsc["ld8"].ap().rearrange("(hp par) (qc q) -> par hp qc q",
                                            hp=4, qc=4)
            tk = b.dma(g, dst, src, "G")
            b.wtok(g, tk)
            tk = b.dma(g, lsb2[:],
                       dsc["ld8"].ap().rearrange("h (qa qb) -> h qa qb", qa=16), "G")
            self._l1_l_rd = tk

        @block.vector
        def _(v):
            b.wtok(v, self._l1_l_rd)
            v.drain()
            v.reciprocal(out=lsb2[:], in_=lsb2[:])
            v.drain()
            b.inc(v.tensor_scalar(out=lsb2[:], in0=lsb2[:], scalar1=0.5,
                                  scalar2=None, op0=OP.mult), "sV", 1)
        minv_ready = b.cnt["sV"]

        @block.gpsimd
        def _(g):
            g.wait_ge(b.sems["sV"], minv_ready)
            self._l1_minv_wr = b.dma(
                g, dsc["minvd1"].ap().rearrange("h (qa qb) -> h qa qb", qa=16),
                lsb2[:], "G")

        gmT = b.scr("gmT", [64, T], F32)
        minvB = b.scr("minvB", [64, T], F32)
        gmtm = gates
        gm_tr_t = {}

        @block.vector
        def _(v):
            b.wait(v, "sV")
            v.drain()
            b.inc(v.tensor_tensor(out=gmtm[:], in0=gates[:], in1=maskbf[:],
                                  op=OP.mult), "sV", 1)
        gmtm_ready = b.cnt["sV"]

        for tt in range(NT):
            ptr = PBt[:, 3072 + 512 * (tt % 2):3072 + 512 * (tt % 2) + 128]

            @block.tensor
            def _(t, tt=tt, ptr=ptr):
                t.wait_ge(b.sems["sV"], gmtm_ready)
                if tt >= 2:
                    t.wait_ge(b.sems["sV"], gm_tr_t[tt - 2])
                b.inc(t.transpose(ptr[0:64, 0:128], gmtm[:, 64 * tt:64 * (tt + 1)],
                                  self.ident[:]), "sT", 1)
                gm_tr_t[(tt, "mm")] = b.cnt["sT"]

            @block.vector
            def _(v, tt=tt, ptr=ptr):
                v.wait_ge(b.sems["sT"], gm_tr_t[(tt, "mm")])
                v.drain()
                b.inc(v.tensor_copy(out=gmT[0:64, 128 * tt:128 * (tt + 1)],
                                    in_=ptr[0:64, :]), "sV", 1)
                gm_tr_t[tt] = b.cnt["sV"]

        @block.gpsimd
        def _(g):
            b.wtok(g, self._l1_minv_wr)
            self._l1_minvB = b.dma(
                g, minvB[0:64, :],
                dsc["minvd1"].ap().unsqueeze(1).broadcast_to([8, 8, T]), "G")

        @block.vector
        def _(v):
            b.wait(v, "sV")
            b.wtok(v, self._l1_minvB)
            v.drain()
            b.inc(v.tensor_tensor(out=gmT[0:64, :], in0=gmT[0:64, :],
                                  in1=minvB[0:64, :], op=OP.mult), "sV", 1)
        sT_ready = b.cnt["sV"]

        @block.gpsimd
        def _(g):
            g.wait_ge(b.sems["sV"], sT_ready)
            self._l1_sd_wr = b.dma(g, dsc["sd"].ap(), gmT[0:64, :], "G")

        oew = [b.scr_at(f"oew{i}", [128, 1024], BF16, "qkT", delta=2048 * i)
               for i in range(4)]
        sBt = [b.scr_at(f"sBt{i}", [128, 512], BF16, "qkT", delta=8192 + 1024 * i)
               for i in range(4)]
        Gt = [b.scr_at(f"Gt{i}", [128, 512], BF16, "qkT", delta=12288 + 1024 * i)
              for i in range(4)]
        oe_tok = {}
        sB_tok = {}
        g_bld_t = {}
        oe_mm_t = {}
        oe_ev_t = {}
        for qc in range(TC):
            for ch in range(32):
                e, hp = ch // 4, ch % 4
                ci = 32 * qc + ch
                rot = ci % 4

                @block.gpsimd
                def _(g, qc=qc, ch=ch, e=e, hp=hp, ci=ci, rot=rot):
                    if ci == 0:
                        b.wtok(g, self._l1_sd_wr)
                    if ci >= 4:
                        g.wait_ge(b.sems["sT"], oe_mm_t[ci - 4])
                    oe_tok[ci] = b.dma(
                        g, oew[rot][:],
                        dsc["oed"].ap()[512 * e + 128 * hp:512 * e + 128 * (hp + 1), :],
                        "G")
                    src = dsc["sd"].ap().rearrange("(h e2) t -> h e2 t", e2=8) \
                        [2 * hp:2 * hp + 2, e, 512 * qc:512 * (qc + 1)] \
                        .unsqueeze(1).broadcast_to([2, 64, 512])
                    sB_tok[ci] = b.dma(g, sBt[rot][:], src, "G")

                @block.vector
                def _(v, qc=qc, hp=hp, ci=ci, rot=rot):
                    b.wtok(v, sB_tok[ci])
                    if ci >= 4:
                        v.wait_ge(b.sems["sT"], oe_mm_t[ci - 4])
                    v.drain()
                    b.inc(v.tensor_tensor(
                        out=Gt[rot][:],
                        in0=araw[:, T * hp + 512 * qc:T * hp + 512 * (qc + 1)],
                        in1=sBt[rot][:], op=OP.mult), "sV", 1)
                    g_bld_t[ci] = b.cnt["sV"]

                @block.tensor
                def _(t, qc=qc, ch=ch, ci=ci, rot=rot):
                    b.wtok(t, oe_tok[ci])
                    t.wait_ge(b.sems["sV"], g_bld_t[ci])
                    if ch == 0 and qc > 0:
                        t.wait_ge(b.sems["sV"], oe_ev_t[qc - 1])
                    for dc in range(CC):
                        inst = t.matmul(pb[dc][:, :],
                                        oew[rot][:, 128 * dc:128 * (dc + 1)],
                                        Gt[rot][:], start=(ch == 0), stop=(ch == 31))
                    b.inc(inst, "sT", 1)
                    oe_mm_t[ci] = b.cnt["sT"]

            @block.vector
            def _(v, qc=qc):
                v.wait_ge(b.sems["sT"], oe_mm_t[32 * qc + 31])
                for dc in range(CC):
                    v.drain()
                    inst = v.scalar_tensor_tensor(
                        out=X[:, T * dc + 512 * qc:T * dc + 512 * (qc + 1)],
                        in0=pb[dc][:, :], scalar=1.0,
                        in1=X[:, T * dc + 512 * qc:T * dc + 512 * (qc + 1)],
                        op0=OP.mult, op1=OP.add)
                b.inc(inst, "sV", 1)
                oe_ev_t[qc] = b.cnt["sV"]
        b.pop()

    def ph_l1att(self):
        b, block, din, dsc, pb, PBt = self.b, self.block, self.din, self.dsc, self.pb, self.PBt
        X, st = self.X, self.stat
        b.push()
        yt = b.scr("yt", [128, CC * T], F16)
        b.push()
        xbf = b.scr("xbf4", [128, CC * T], BF16)
        b.push()
        x2bf = b.scr("x2bf4", [128, CC * T], BF16)
        rb = b.scr("rb4", [128, T], F32)
        cb = b.scr("cb4", [128, T], F32)
        rbbf = b.scr("rbbf4", [128, T], BF16)
        cbbf = b.scr("cbbf4", [128, T], BF16)
        self.emit_xbf_copies(xbf, x2bf)
        self.emit_ln_stats(xbf, x2bf, rb, cb)
        self.emit_ln_bf16_apply(xbf, rb, cb, rbbf, cbbf)
        b.pop()
        xn = xbf
        xn_ready = b.cnt["sV"]

        araw = b.scr("araw2", [128, 8 * T], BF16)
        lbuf = b.scr("lbuf2", [128, 512], F32)
        b.push()
        qp = [b.scr(f"qp{i}", [128, T], BF16) for i in range(2)]
        kp = [b.scr(f"kp{i}", [128, T], BF16) for i in range(2)]
        vp = [b.scr(f"vp{i}", [128, NT * 128], BF16) for i in range(2)]
        ptt = [b.scr(f"pttA{i}", [128, 1024], BF16) for i in range(2)]
        qkvw = [b.scr(f"qkvw{i}", [128, 3 * 128], BF16) for i in range(8)]

        qkv_tok = {}
        qkv_mm_t = {}
        qkv_ev_t = {}
        s_mm_t = {}
        p_ev_t = {}
        a_mm_t = {}
        a_ev_t = {}
        l_dma_t = {}
        pa = PBt[:, 0:512]
        pl = PBt[:, 512:1024]
        for hp in range(CC):
            rot = hp % 2

            @block.gpsimd
            def _(g, hp=hp):
                if hp == 0:
                    b.wait(g, "sT")
                    b.wait(g, "sV")
                if hp >= 1:
                    g.wait_ge(b.sems["sT"], self._la_vmm_last[hp - 1])
                toks = []
                for c in range(CC):
                    for wi, wn in enumerate(("wq_t", "wk_t", "wv_t")):
                        toks.append(b.dma(
                            g, qkvw[c][:, 128 * wi:128 * (wi + 1)],
                            din[wn].ap()[128 * c:128 * (c + 1),
                                         128 * hp:128 * (hp + 1)], "G"))
                qkv_tok[hp] = toks

            qk_half_t = {}
            for which, wslice, bcol, scale2 in (("q", slice(0, 128), 0, 0.125),
                                                ("k", slice(128, 256), 1, None)):
                dstp = qp if which == "q" else kp
                for qh in range(2):
                    bks = [pb[6], pb[7]]

                    @block.tensor
                    def _(t, hp=hp, rot=rot, which=which, wslice=wslice, qh=qh, bks=bks):
                        if hp == 0 and which == "q" and qh == 0:
                            t.wait_ge(b.sems["sV"], xn_ready)
                        for tk in qkv_tok[hp]:
                            b.wtok(t, tk)
                        prev = qk_half_t.get((hp, which, qh - 1))
                        if prev is None and which == "k":
                            prev = qk_half_t.get((hp, "q", 1))
                        if prev is not None:
                            t.wait_ge(b.sems["sV"], prev)
                        elif hp >= 1:
                            t.wait_ge(b.sems["sV"], qkv_ev_t[(hp - 1, "v", 3)])
                        if hp >= 2:
                            t.wait_ge(b.sems["sT"], s_mm_t[(hp - 2, TC - 1, NT - 1)])
                        for c in range(CC):
                            lhsT = qkvw[c][:, wslice]
                            for j in range(2):
                                tc = 2 * qh + j
                                inst = t.matmul(bks[j][:, :], lhsT,
                                                xn[:, T * c + 512 * tc:T * c + 512 * (tc + 1)],
                                                start=(c == 0), stop=(c == CC - 1))
                        b.inc(inst, "sT", 1)
                        qkv_mm_t[(hp, which, qh)] = b.cnt["sT"]

                    @block.vector
                    def _(v, hp=hp, rot=rot, which=which, bcol=bcol, scale2=scale2,
                          qh=qh, bks=bks, dstp=dstp):
                        v.wait_ge(b.sems["sT"], qkv_mm_t[(hp, which, qh)])
                        bb = self.biases[:, bcol * CC + hp:bcol * CC + hp + 1]
                        for j in range(2):
                            tc = 2 * qh + j
                            v.drain()
                            if scale2 is None:
                                inst = v.tensor_scalar(
                                    out=dstp[rot][:, 512 * tc:512 * (tc + 1)],
                                    in0=bks[j][:, :], scalar1=bb, scalar2=None,
                                    op0=OP.add)
                            else:
                                inst = v.tensor_scalar(
                                    out=dstp[rot][:, 512 * tc:512 * (tc + 1)],
                                    in0=bks[j][:, :], scalar1=bb, scalar2=scale2,
                                    op0=OP.add, op1=OP.mult)
                        b.inc(inst, "sV", 1)
                        qk_half_t[(hp, which, qh)] = b.cnt["sV"]
                        qkv_ev_t[(hp, which)] = b.cnt["sV"]

            vv_mm = {}
            for tg in range(4):
                bkv = [pb[6 + (tg % 2)]]

                @block.tensor
                def _(t, hp=hp, rot=rot, tg=tg, bkv=bkv):
                    if tg == 0:
                        t.wait_ge(b.sems["sV"], qk_half_t[(hp, "k", 1)])
                    if tg == 1:
                        pass
                    if tg >= 2:
                        t.wait_ge(b.sems["sV"], qkv_ev_t[(hp, "v", tg - 2)])
                    if hp >= 2 and tg < 2:
                        t.wait_ge(b.sems["sT"], a_mm_t[(hp - 2, TC - 1, NT - 1)])
                    for tj in range(4):
                        tt = 4 * tg + tj
                        for c in range(CC):
                            inst = t.matmul(
                                bkv[0][:, 128 * tj:128 * (tj + 1)],
                                xn[:, T * c + 128 * tt:T * c + 128 * (tt + 1)],
                                qkvw[c][:, 256:384],
                                start=(c == 0), stop=(c == CC - 1))
                    b.inc(inst, "sT", 1)
                    vv_mm[tg] = b.cnt["sT"]

                @block.vector
                def _(v, hp=hp, rot=rot, tg=tg, bkv=bkv):
                    v.wait_ge(b.sems["sT"], vv_mm[tg])
                    if hp >= 2 and tg == 0:
                        t_l = a_mm_t[(hp - 2, TC - 1, NT - 1)]
                        v.wait_ge(b.sems["sT"], t_l)
                    v.drain()
                    inst = v.tensor_copy(out=vp[rot][:, 512 * tg:512 * (tg + 1)],
                                         in_=bkv[0][:, :])
                    v.drain()
                    b.inc(inst, "sV", 1)
                    qkv_ev_t[(hp, "v", tg)] = b.cnt["sV"]
            if not hasattr(self, "_la_vmm_last"):
                self._la_vmm_last = {}
            self._la_vmm_last[hp] = vv_mm[3]
            vready = b.cnt["sV"]

            for qc in range(TC):
                for jt in range(NT):
                    psS = PBt[:, 1024 + 1024 * (jt % 2):2048 + 1024 * (jt % 2)]

                    @block.tensor
                    def _(t, hp=hp, rot=rot, qc=qc, jt=jt, vready=vready, psS=psS):
                        if qc == 0 and jt == 0:
                            t.wait_ge(b.sems["sV"], vready)
                        if jt >= 2:
                            t.wait_ge(b.sems["sS"], p_ev_t[(hp, qc, jt - 2)])
                        if jt == 0:
                            prev = ((hp, qc - 1) if qc > 0 else
                                    ((hp - 1, TC - 1) if hp >= 1 else None))
                            if prev is not None and prev in a_ev_t:
                                t.wait_ge(b.sems["sV"], a_ev_t[prev])
                        for par in range(2):
                            rows = slice(64 * par, 64 * par + 64)
                            inst = t.matmul(
                                psS[:, 512 * par:512 * (par + 1)],
                                kp[rot][rows, 128 * jt:128 * (jt + 1)],
                                qp[rot][rows, 512 * qc:512 * (qc + 1)],
                                start=True, stop=True)
                        b.inc(inst, "sT", 1)
                        s_mm_t[(hp, qc, jt)] = b.cnt["sT"]

                    @block.scalar
                    def _(sc, hp=hp, qc=qc, jt=jt, psS=psS):
                        sc.wait_ge(b.sems["sT"], s_mm_t[(hp, qc, jt)])
                        if jt >= 2:
                            sc.wait_ge(b.sems["sT"], a_mm_t[(hp, qc, jt - 2)])
                        b.inc(sc.activation(out=ptt[jt % 2][:], in_=psS[:, :],
                                            func=AF.Exp, bias=self.consts[:, 5:6],
                                            scale=1.0), "sS", 1)
                        p_ev_t[(hp, qc, jt)] = b.cnt["sS"]

                    @block.tensor
                    def _(t, hp=hp, rot=rot, qc=qc, jt=jt, psS=psS):
                        t.wait_ge(b.sems["sS"], p_ev_t[(hp, qc, jt)])
                        for par in range(2):
                            pt = ptt[jt % 2][:, 512 * par:512 * (par + 1)]
                            vl = vp[rot][:, 128 * jt:128 * jt + 64] if False else \
                                vp[rot].rearrange("p (tt h) -> p tt h", tt=NT) \
                                [:, jt:jt + 1, :].squeeze(1)
                            t.matmul(pa[slice(64 * par, 64 * par + 64), :],
                                     vl[:, 64 * par:64 * (par + 1)] if False else
                                     vp[rot][:, 128 * jt + 64 * par:128 * jt + 64 * (par + 1)]
                                     if False else vp[rot][:, 128 * jt + 64 * 0:128 * jt + 64],
                                     pt, start=(jt == 0), stop=(jt == NT - 1))
                            inst = t.matmul(pl[32 * par:32 * par + 1, :],
                                            self.onecol_bf[:, 0:1], pt,
                                            start=(jt == 0), stop=(jt == NT - 1),
                                            tile_position=(0, 32 * par))
                        b.inc(inst, "sT", 1)
                        a_mm_t[(hp, qc, jt)] = b.cnt["sT"]

                @block.vector
                def _(v, hp=hp, qc=qc):
                    v.wait_ge(b.sems["sT"], a_mm_t[(hp, qc, NT - 1)])
                    if (hp, qc) != (0, 0):
                        prev = ((hp, qc - 1) if qc > 0 else (hp - 1, TC - 1))
                        b.wtok(v, l_dma_t[prev])
                    v.drain()
                    v.tensor_copy(out=araw[:, T * hp + 512 * qc:T * hp + 512 * (qc + 1)],
                                  in_=pa[:, :])
                    v.drain()
                    inst = v.tensor_copy(out=lbuf[:], in_=pl[:, :])
                    v.drain()
                    b.inc(inst, "sV", 1)
                    a_ev_t[(hp, qc)] = b.cnt["sV"]

                @block.gpsimd
                def _(g, hp=hp, qc=qc):
                    g.wait_ge(b.sems["sV"], a_ev_t[(hp, qc)])
                    l_dma_t[(hp, qc)] = b.dma(
                        g, dsc["lst"].ap()[:, 512 * (4 * hp + qc):512 * (4 * hp + qc + 1)],
                        lbuf[:], "G")
        b.pop()

        lsb2 = b.scr("lsb2A", [128, 256], F32)

        @block.gpsimd
        def _(g):
            for k in l_dma_t.values():
                b.wtok(g, k)
            src = dsc["lst"].ap().rearrange("(ra rb) c -> ra rb c", ra=4)[0:2, 0:1, :] \
                .squeeze(1)
            src = src.rearrange("par (hp qc q) -> par hp qc q", hp=8, qc=4)
            dst = dsc["ld16"].ap().rearrange("(hp par) (qc q) -> par hp qc q",
                                             hp=8, qc=4)
            tk = b.dma(g, dst, src, "G")
            b.wtok(g, tk)
            tk = b.dma(g, lsb2[:],
                       dsc["ld16"].ap().rearrange("h (qa qb) -> h qa qb", qa=8), "G")
            self._la_l_rd = tk

        @block.vector
        def _(v):
            b.wtok(v, self._la_l_rd)
            v.drain()
            b.inc(v.reciprocal(out=lsb2[:], in_=lsb2[:]), "sV", 1)
        minv_ready = b.cnt["sV"]

        @block.gpsimd
        def _(g):
            g.wait_ge(b.sems["sV"], minv_ready)
            self._la_minv_wr = b.dma(
                g, dsc["minvd"].ap().rearrange("h (qa qb) -> h qa qb", qa=8),
                lsb2[:], "G")

        gsc = [b.scr(f"gscA{i}", [128, T], BF16) for i in range(2)]
        gsc_tok = {}
        scale_t = {}
        for hp in range(CC):
            @block.gpsimd
            def _(g, hp=hp):
                if hp == 0:
                    b.wtok(g, self._la_minv_wr)
                if hp >= 2:
                    g.wait_ge(b.sems["sV"], scale_t[hp - 2])
                src = dsc["minvd"].ap()[2 * hp:2 * hp + 2, :]
                gsc_tok[hp] = b.dma(g, gsc[hp % 2][:],
                                    src.unsqueeze(1).broadcast_to([2, 64, T]), "G")

            @block.vector
            def _(v, hp=hp):
                b.wtok(v, gsc_tok[hp])
                v.drain()
                sl = slice(T * hp, T * (hp + 1))
                b.inc(v.tensor_tensor(out=araw[:, sl], in0=araw[:, sl],
                                      in1=gsc[hp % 2][:], op=OP.mult), "sV", 1)
                scale_t[hp] = b.cnt["sV"]

        wob = [b.scr(f"wobA{i}", [128, 1024], BF16) for i in range(CC)]
        bvb8 = b.scr("bvb8", [128, CC], BF16)

        @block.gpsimd
        def _(g):
            b.wait(g, "sT")
            toks = []
            for c in range(CC):
                toks.append(b.dma(g, wob[c][:],
                                  din["wo_t"].ap()[128 * c:128 * (c + 1), :], "G"))
            self._la_wo_tok = toks

        @block.vector
        def _(v):
            b.wait(v, "sV")
            v.drain()
            b.inc(v.tensor_copy(out=bvb8[:], in_=self.biases[:, 2 * CC:3 * CC]), "sV", 1)
        bv_ready = b.cnt["sV"]

        cv_t = {}

        @block.tensor
        def _(t):
            for tk in self._la_wo_tok:
                b.wtok(t, tk)
            t.wait_ge(b.sems["sV"], bv_ready)
            pc = PBt[:, 2048:2176]
            for dc in range(CC):
                for c in range(CC):
                    inst = t.matmul(pc[:, dc:dc + 1],
                                    wob[c][:, 128 * dc:128 * (dc + 1)],
                                    bvb8[:, c:c + 1],
                                    start=(c == 0), stop=(c == CC - 1))
            b.inc(inst, "sT", 1)
        cv_done = b.cnt["sT"]

        @block.vector
        def _(v):
            v.wait_ge(b.sems["sT"], cv_done)
            v.drain()
            v.tensor_copy(out=st[:, 32:40], in_=PBt[:, 2048:2056])
            v.drain()
            b.inc(v.tensor_tensor(out=st[:, 40:48], in0=st[:, 32:40],
                                  in1=self.biases[:, 3 * CC:4 * CC], op=OP.add), "sV", 1)
        boc_ready = b.cnt["sV"]

        wo_mm_t = {}
        wo_ev_t = {}
        for dc in range(CC):
            bks = [pb[4 * (dc % 2) + j] for j in range(4)]

            @block.tensor
            def _(t, dc=dc, bks=bks):
                if dc == 0:
                    t.wait_ge(b.sems["sV"], scale_t[CC - 1])
                    t.wait_ge(b.sems["sV"], boc_ready)
                if dc >= 2:
                    t.wait_ge(b.sems["sV"], wo_ev_t[dc - 2])
                for c in range(CC):
                    lhsT = wob[c][:, 128 * dc:128 * (dc + 1)]
                    for tc in range(TC):
                        inst = t.matmul(bks[tc][:, :], lhsT,
                                        araw[:, T * c + 512 * tc:T * c + 512 * (tc + 1)],
                                        start=(c == 0), stop=(c == CC - 1))
                b.inc(inst, "sT", 1)
                wo_mm_t[dc] = b.cnt["sT"]

            @block.vector
            def _(v, dc=dc, bks=bks):
                v.wait_ge(b.sems["sT"], wo_mm_t[dc])
                for tc in range(TC):
                    v.drain()
                    inst = v.scalar_tensor_tensor(
                        out=yt[:, T * dc + 512 * tc:T * dc + 512 * (tc + 1)],
                        in0=bks[tc][:, :], scalar=st[:, 40 + dc:41 + dc],
                        in1=xn[:, T * dc + 512 * tc:T * dc + 512 * (tc + 1)],
                        op0=OP.add, op1=OP.add)
                b.inc(inst, "sV", 1)
                wo_ev_t[dc] = b.cnt["sV"]
        y_done = b.cnt["sV"]
        b.pop()

        b.push()
        ybf = b.scr("ybf", [128, CC * T], BF16)
        y2bf = b.scr("y2bf", [128, CC * T], BF16)
        rb2 = b.scr("rb5", [128, T], F32)
        cb2 = b.scr("cb5", [128, T], F32)
        t1 = b.scr("t1", [128, T], F32)

        @block.vector
        def _(v):
            v.wait_ge(b.sems["sV"], y_done)
            v.drain()
            b.inc(v.tensor_copy(out=ybf[:], in_=yt[:]), "sV", 1)

        @block.scalar
        def _(s):
            b.wait(s, "sV")
            b.inc(s.activation(out=y2bf[:], in_=yt[:], func=AF.Square,
                               bias=self.consts[:, 0:1], scale=1.0), "sS", 1)
        self.emit_ln_stats(ybf, y2bf, rb2, cb2)

        @block.vector
        def _(v):
            b.wait(v, "sV")
            v.drain()
            v.reciprocal(out=st[:, 48:56], in_=self.biases[:, 4 * CC:5 * CC])
            v.drain()
            b.inc(v.tensor_tensor(out=st[:, 48:56], in0=st[:, 48:56],
                                  in1=self.biases[:, 5 * CC:6 * CC], op=OP.mult),
                  "sV", 1)

        xup_t = {}
        for dc in range(CC):
            @block.vector
            def _(v, dc=dc):
                b.wait(v, "sV")
                v.drain()
                v.tensor_tensor(out=t1[:], in0=yt[:, T * dc:T * (dc + 1)], in1=rb2[:],
                                op=OP.mult)
                v.drain()
                v.scalar_tensor_tensor(out=t1[:], in0=t1[:],
                                       scalar=st[:, 48 + dc:49 + dc],
                                       in1=cb2[:], op0=OP.add, op1=OP.subtract)
                v.drain()
                inst = v.scalar_tensor_tensor(
                    out=X[:, T * dc:T * (dc + 1)], in0=t1[:],
                    scalar=self.biases[:, 4 * CC + dc:4 * CC + dc + 1],
                    in1=X[:, T * dc:T * (dc + 1)], op0=OP.mult, op1=OP.add)
                v.drain()
                b.inc(inst, "sV", 1)
                xup_t[dc] = b.cnt["sV"]
        b.pop()
        b.pop()
    def ph_store(self):
        b, block = self.b, self.block
        self.stored = True

        @block.sync
        def _(q):
            b.wait(q, "sV")
            b.wait(q, "sT")
            b.wait(q, "sS")
            tok = b.dma(q, self.dout.ap().rearrange("(c p) t -> p c t", p=128),
                        self.X[:].rearrange("p (c t) -> p c t", c=CC), "S")
            b.wtok(q, tok)

    def ph_store_taps(self):
        b, block = self.b, self.block
        if not self.stored:
            self.ph_store()

        @block.sync
        def _(q):
            b.wait_allq(q, "S")

        @block.gpsimd
        def _(g):
            b.wait_allq(g, "G")


def kernel(**inputs):
    in_maps = _prep_inputs(inputs)
    nc = build_program()
    res = run_bass_kernel_spmd(nc, in_maps, core_ids=list(range(NCORES)))
    out = np.stack([np.ascontiguousarray(r["out_t"].T) for r in res.results])
    return out.astype(np.float32)


# version 04
# speedup vs baseline: 1.0792x; 1.0008x over previous
import numpy as np
import ml_dtypes
import concourse.bass as bass
import concourse.mybir as mybir
from concourse.alu_op_type import AluOpType
from concourse.bass_utils import run_bass_kernel_spmd

F32 = mybir.dt.float32
BF16 = mybir.dt.bfloat16
F16 = mybir.dt.float16
U8 = mybir.dt.uint8
AF = mybir.ActivationFunctionType
AX = mybir.AxisListType
OP = AluOpType
BF = ml_dtypes.bfloat16

NCORES = 8
T = 2048
D = 1024
CC = D // 128
TC = T // 512
NT = T // 128
EPS_LN = 1e-5
NROT = 4


class Bld:

    def __init__(self, nc):
        self.nc = nc
        self.sems = {}
        self.cnt = {}
        self.rot = {"S": -1, "G": -1, "C": -1}
        self.cur = {}
        self.scr_off = 0
        self.scr_stack = []
        self.scratch_addr = None
        self.scratch_bytes = 0
        self.uid = 0

    def sem(self, name, handle):
        self.sems[name] = handle
        self.cnt[name] = 0

    def inc(self, inst, name, k=1):
        inst.then_inc(self.sems[name], k)
        self.cnt[name] += k
        return inst

    def wait(self, eng, name, tgt=None):
        eng.wait_ge(self.sems[name], self.cnt[name] if tgt is None else tgt)

    def gbegin(self, q, eng=None):
        self.rot[q] = (self.rot[q] + 1) % NROT
        s = f"d{q}{self.rot[q]}"
        self.cur[q] = s
        if eng is not None and self.cnt[s] > 0:
            eng.wait_ge(self.sems[s], self.cnt[s])
        return s

    def dinc(self, inst, q):
        s = self.cur[q]
        inst.then_inc(self.sems[s], 16)
        self.cnt[s] += 16
        return inst

    def gend(self, q):
        s = self.cur[q]
        return (s, self.cnt[s])

    def dma(self, eng, out, in_, q):
        self.gbegin(q, eng)
        self.dinc(eng.dma_start(out=out, in_=in_), q)
        return self.gend(q)

    def wtok(self, eng, tok):
        eng.wait_ge(self.sems[tok[0]], tok[1])

    def wait_allq(self, eng, q):
        for i in range(NROT):
            s = f"d{q}{i}"
            eng.wait_ge(self.sems[s], self.cnt[s])

    def push(self):
        self.scr_stack.append(self.scr_off)

    def pop(self):
        self.scr_off = self.scr_stack.pop()

    def scr(self, name, shape, dtype, align=32):
        per_part = int(np.prod(shape[1:])) * mybir.dt.size(dtype)
        off = (self.scr_off + align - 1) // align * align
        assert off + per_part <= self.scratch_bytes, (
            f"scratch overflow: {name} needs {per_part} at {off}, cap {self.scratch_bytes}")
        self.scr_off = off + per_part
        self.uid += 1
        self.offs = getattr(self, "offs", {})
        self.offs[name] = (off, per_part)
        return self.nc.alloc_sbuf_tensor_at(
            f"scr_{name}_{self.uid}", list(shape), dtype,
            offset=self.scratch_addr + off)

    def scr_at(self, name, shape, dtype, at, delta=0):
        per_part = int(np.prod(shape[1:])) * mybir.dt.size(dtype)
        off, sz = self.offs[at]
        assert delta + per_part <= sz, (name, per_part, at, sz, delta)
        off = off + delta
        self.uid += 1
        return self.nc.alloc_sbuf_tensor_at(
            f"scr_{name}_{self.uid}", list(shape), dtype,
            offset=self.scratch_addr + off)


def _prep_inputs(I):
    C = np.ascontiguousarray
    f32 = np.float32
    shared = {
        "w_qkv_t": C(np.asarray(I["L0_attn_W_qkv"]).T),
        "w_out_t": C(np.asarray(I["L0_attn_W_out"]).T),
        "w_gate": C(np.asarray(I["L0_ffn_w_gate"])),
        "w_up": C(np.asarray(I["L0_ffn_w_up"])),
        "w_down": C(np.asarray(I["L0_ffn_w_down"])),
        "clip_x": C(np.reshape(np.asarray(I["L0_ffn_clip_x"]), (1, 1))),
        "clip_down": C(np.reshape(np.asarray(I["L0_ffn_clip_down"]), (1, 1))),
        "wq_t": C(np.asarray(I["L1_attn_Wq"]).T), "wk_t": C(np.asarray(I["L1_attn_Wk"]).T),
        "wv_t": C(np.asarray(I["L1_attn_Wv"]).T), "wo_t": C(np.asarray(I["L1_attn_Wo"]).T),
        "bq": C(np.reshape(np.asarray(I["L1_attn_bq"]), (CC, 128)).T),
        "bk": C(np.reshape(np.asarray(I["L1_attn_bk"]), (CC, 128)).T),
        "bv": C(np.reshape(np.asarray(I["L1_attn_bv"]), (CC, 128)).T),
        "bo": C(np.reshape(np.asarray(I["L1_attn_bo"]), (CC, 128)).T),
        "ln_g": C(np.reshape(np.asarray(I["L1_attn_ln_gamma"]), (CC, 128)).T),
        "ln_b": C(np.reshape(np.asarray(I["L1_attn_ln_beta"]), (CC, 128)).T),
        "w_q1": C(np.asarray(I["L1_ffn_w_q"])),
        "w_k1": C(np.asarray(I["L1_ffn_w_k"])),
        "w_g1": C(np.asarray(I["L1_ffn_w_gate"])),
        "ve_r": C(np.transpose(np.asarray(I["L1_ffn_value_experts"]),
                               (0, 2, 1, 3)).reshape(8, 1024, 512)),
        "oe_flat": C(np.reshape(np.asarray(I["L1_ffn_output_experts"]), (4096, 1024))),
        "ident": np.eye(128, dtype=f32),
    }
    tri = np.zeros((128, 4 * 512), dtype=f32)
    jj = np.arange(128)[:, None]
    qq = np.arange(512)[None, :]
    for bb in range(4):
        tri[:, 512 * bb:512 * (bb + 1)] = (jj + 128 * bb <= qq)
    shared["tri"] = tri.astype(BF)
    x = np.asarray(I["L0_attn_x"])
    in_maps = []
    for i in range(NCORES):
        m = dict(shared)
        m["x_t"] = C(x[i].T)
        in_maps.append(m)
    return in_maps


def build_program(stop_after=None, taps=()):
    nc = bass.Bass("TRN2", target_bir_lowering=False, debug=False)

    din = {}
    for name, shape in [
        ("x_t", [D, T]), ("w_qkv_t", [D, 3 * D]), ("w_out_t", [D, D]),
        ("w_gate", [2816, D]), ("w_up", [2816, D]), ("w_down", [D, 2816]),
        ("clip_x", [1, 1]), ("clip_down", [1, 1]),
        ("wq_t", [D, D]), ("wk_t", [D, D]), ("wv_t", [D, D]), ("wo_t", [D, D]),
        ("bq", [128, CC]), ("bk", [128, CC]), ("bv", [128, CC]), ("bo", [128, CC]),
        ("ln_g", [128, CC]), ("ln_b", [128, CC]),
        ("w_q1", [D, 512]), ("w_k1", [D, 512]), ("w_g1", [D, 64]),
        ("ve_r", [8, D, 512]), ("oe_flat", [4096, D]), ("ident", [128, 128]),
    ]:
        din[name] = nc.dram_tensor(name, shape, F32, kind="ExternalInput")
    din["tri"] = nc.dram_tensor("tri", [128, 4 * 512], BF16, kind="ExternalInput")
    dout = nc.dram_tensor("out_t", [D, T], F32, kind="ExternalOutput")

    dsc = {
        "qwg": nc.dram_tensor("qwg", [2816, D], BF16),
        "qwu": nc.dram_tensor("qwu", [2816, D], BF16),
        "qwd": nc.dram_tensor("qwd", [D, 2816], BF16),
        "hsp": nc.dram_tensor("hsp", [2816, T], F32),
        "vec": nc.dram_tensor("vec", [128, 16], F32),
        "vec1": nc.dram_tensor("vec1", [128, 1], F32),
        "rowb": nc.dram_tensor("rowb", [64, T], BF16),
        "ccin": nc.dram_tensor("ccin", [1, 1], F32),
        "statsd": nc.dram_tensor("statsd", [128, 32 * 512], F32),
        "minvd": nc.dram_tensor("minvd", [16, T], F32),
        "ved": nc.dram_tensor("ved", [8, D, 512], BF16),
        "oed": nc.dram_tensor("oed", [4096, D], BF16),
        "lst": nc.dram_tensor("lst", [128, 32 * 512], F32),
        "ld16": nc.dram_tensor("ld16", [16, T], F32),
        "ld8": nc.dram_tensor("ld8", [8, T], F32),
        "minvd1": nc.dram_tensor("minvd1", [8, T], F32),
        "sd": nc.dram_tensor("sd", [64, T], BF16),
        "ccout": nc.dram_tensor("ccout", [NCORES, 1], F32),
    }
    tap_outs = {}
    for tname, tshape in taps:
        tap_outs[tname] = nc.dram_tensor("tap_" + tname, tshape, F32, kind="ExternalOutput")

    from contextlib import ExitStack
    ctx = ExitStack()
    with ctx:
        X = ctx.enter_context(nc.sbuf_tensor("X", [128, CC * T], F32))
        ident = ctx.enter_context(nc.sbuf_tensor("identity_sb", [128, 128], F32))
        tri = ctx.enter_context(nc.sbuf_tensor("tri_sb", [128, 4 * 512], BF16))
        onecol_bf = ctx.enter_context(nc.sbuf_tensor("onecol_bf", [128, 1], BF16))
        consts = ctx.enter_context(nc.sbuf_tensor("consts", [128, 8], F32))
        biases = ctx.enter_context(nc.sbuf_tensor("biases", [128, 6 * CC], F32))
        stat = ctx.enter_context(nc.sbuf_tensor("stat", [128, 256], F32))
        ones_bf = ctx.enter_context(nc.sbuf_tensor("ones_bf", [128, 128], BF16))
        wsc = ctx.enter_context(nc.sbuf_tensor("wsc", [128, 52], F32))
        stat2col = ctx.enter_context(nc.sbuf_tensor("stat2col", [128, 2], F32))
        clip_sb = ctx.enter_context(nc.sbuf_tensor("clip_sb", [128, 2], F32))
        SCR_BYTES = 137 * 1024
        scratch = ctx.enter_context(nc.sbuf_tensor("scratch", [128, SCR_BYTES], U8))
        PBt = ctx.enter_context(nc.psum_tensor("PBt", [128, 4096], F32))
        pb = [PBt[:, 512 * i:512 * (i + 1)] for i in range(8)]

        sem_names = (["sT", "sV", "sS", "cc"]
                     + [f"d{q}{i}" for q in "SGC" for i in range(NROT)])
        sems = {n: ctx.enter_context(nc.semaphore(n)) for n in sem_names}

        block = ctx.enter_context(nc.Block())
        b = Bld(nc)
        b.scratch_bytes = SCR_BYTES
        b.scratch_addr = nc.lookup_mloc(scratch).addr
        for n in sem_names:
            b.sem(n, sems[n])

        st = _State(nc, block, b, din, dout, dsc, X, ident, tri, onecol_bf,
                    consts, biases, stat, ones_bf, pb, tap_outs)
        st.PBt = PBt
        st.wsc, st.stat2col, st.clip_sb = wsc, stat2col, clip_sb

        phases = [
            ("init", st.ph_init),
            ("wquant", st.ph_wquant),
            ("l1stash", st.ph_l1stash),
            ("l0f1", lambda: st.ph_l0ffn(0)),
            ("l0att", st.ph_l0att),
            ("l0f2", lambda: st.ph_l0ffn(1)),
            ("l1f1", lambda: st.ph_l1ffn(0)),
            ("l1att", st.ph_l1att),
            ("l1f2", lambda: st.ph_l1ffn(1)),
            ("store", st.ph_store),
        ]
        for name, fn in phases:
            fn()
            if stop_after is not None and name == stop_after:
                break
        st.ph_store_taps()
    return nc


class _State:
    def __init__(self, nc, block, b, din, dout, dsc, X, ident, tri, onecol_bf,
                 consts, biases, stat, ones_bf, pb, tap_outs):
        self.nc, self.block, self.b = nc, block, b
        self.din, self.dout, self.dsc = din, dout, dsc
        self.X, self.ident, self.tri = X, ident, tri
        self.onecol_bf, self.consts = onecol_bf, consts
        self.biases, self.stat, self.ones_bf, self.pb = biases, stat, ones_bf, pb
        self.taps = tap_outs
        self.stored = False

    def tap_now(self, name, ap, wait=("sV", "sS", "sT")):
        if name not in self.taps:
            return
        b, block = self.b, self.block
        dram = self.taps[name]

        @block.gpsimd
        def _(g):
            for w in wait:
                b.wait(g, w)
            b.wait_allq(g, "G")
            tok = b.dma(g, dram.ap(), ap, "G")
            b.wtok(g, tok)

    def maybe_tap(self, name, ap):
        if name not in self.taps:
            return
        b, block = self.b, self.block
        dram = self.taps[name]

        @block.gpsimd
        def _(g):
            b.wait(g, "sV"); b.wait(g, "sS"); b.wait(g, "sT")
            tok = b.dma(g, dram.ap(), ap, "G")
            b.wtok(g, tok)

    def emit_xbf_copies(self, xbf, x2bf):
        b, block = self.b, self.block

        @block.vector
        def _(v):
            b.wait_allq(v, "S")
            b.wait_allq(v, "G")
            b.wait(v, "sT")
            v.drain()
            b.inc(v.tensor_copy(out=xbf[:], in_=self.X[:]), "sV", 1)

        @block.scalar
        def _(s):
            b.wait_allq(s, "S")
            b.wait_allq(s, "G")
            b.wait(s, "sV")
            b.wait(s, "sT")
            b.inc(s.activation(out=x2bf[:], in_=self.X[:], func=AF.Square,
                               bias=self.consts[:, 0:1], scale=1.0), "sS", 1)

    def emit_ln_stats(self, xbf, x2bf, rb, cb):
        b, block, pb = self.b, self.block, self.pb
        D_inv = 1.0 / D
        epscol = self.consts[:, 2:3]
        vc0 = b.cnt["sV"]
        sc0 = b.cnt["sS"]

        @block.tensor
        def _(t):
            t.wait_ge(b.sems["sV"], vc0)
            t.wait_ge(b.sems["sS"], sc0)
            for tc in range(TC):
                if tc >= 2:
                    t.wait_ge(b.sems["sV"], vc0 + 2 * (tc - 2) + 1)
                for cc in range(CC):
                    csl = slice(T * cc + 512 * tc, T * cc + 512 * tc + 512)
                    t.matmul(pb[2 * (tc % 2)][:, :], self.ones_bf[:], xbf[:, csl],
                             start=(cc == 0), stop=(cc == CC - 1))
                    inst = t.matmul(pb[2 * (tc % 2) + 1][:, :], self.ones_bf[:],
                                    x2bf[:, csl], start=(cc == 0), stop=(cc == CC - 1))
                b.inc(inst, "sT", 1)
        tmm = b.cnt["sT"]

        for tc in range(TC):
            sl = slice(512 * tc, 512 * (tc + 1))
            p_sx = pb[2 * (tc % 2)]
            p_sxx = pb[2 * (tc % 2) + 1]
            p_tmp = pb[4 + (tc % 2)]

            @block.vector
            def _(v, tc=tc, sl=sl, p_sx=p_sx, p_sxx=p_sxx):
                v.wait_ge(b.sems["sT"], tmm - TC + tc + 1)
                v.drain()
                v.tensor_scalar(out=cb[:, sl], in0=p_sx[:, :], scalar1=D_inv,
                                scalar2=None, op0=OP.mult)
                v.drain()
                v.tensor_tensor(out=rb[:, sl], in0=cb[:, sl], in1=cb[:, sl], op=OP.mult)
                v.drain()
                b.inc(v.scalar_tensor_tensor(out=rb[:, sl], in0=p_sxx[:, :],
                                             scalar=D_inv, in1=rb[:, sl],
                                             op0=OP.mult, op1=OP.subtract), "sV", 1)

            @block.scalar
            def _(s, tc=tc, sl=sl, p_tmp=p_tmp):
                s.wait_ge(b.sems["sV"], vc0 + 2 * tc + 1)
                if tc >= 2:
                    s.wait_ge(b.sems["sV"], vc0 + 2 * (tc - 2) + 2)
                b.inc(s.activation(out=p_tmp[:, :], in_=rb[:, sl], func=AF.Sqrt,
                                   bias=epscol, scale=1.0), "sS", 1)

            @block.vector
            def _(v, tc=tc, sl=sl, p_tmp=p_tmp):
                v.wait_ge(b.sems["sS"], sc0 + tc + 1)
                v.drain()
                v.reciprocal(out=rb[:, sl], in_=p_tmp[:, :])
                v.drain()
                b.inc(v.tensor_tensor(out=cb[:, sl], in0=cb[:, sl], in1=rb[:, sl],
                                      op=OP.mult), "sV", 1)

    def emit_flat_allreduce_max(self, col_ap, st):
        b, block, dsc = self.b, self.block, self.dsc
        vc = b.cnt["sV"]

        @block.gpsimd
        def _(g):
            g.wait_ge(b.sems["sV"], vc)
            tok = b.dma(g, dsc["vec1"].ap(), col_ap, "G")
            b.wtok(g, tok)
            tok = b.dma(g, st[0:1, 65:193],
                        dsc["vec1"].ap().rearrange("a b -> b a"), "G")
            b.wtok(g, tok)
        g1 = b.cnt["dG" + str(b.rot["G"])], b.rot["G"]

        @block.vector
        def _(v):
            b.wait_allq(v, "G")
            v.drain()
            b.inc(v.reduce_max(out=st[0:1, 64:65], in_=st[0:1, 65:193],
                               axis=AX.X), "sV", 1)
        vc2 = b.cnt["sV"]

        @block.gpsimd
        def _(g):
            g.wait_ge(b.sems["sV"], vc2)
            tok = b.dma(g, dsc["ccin"].ap(), st[0:1, 64:65], "G")
            b.wtok(g, tok)
            b.inc(g.collective_compute(
                "AllGather", OP.bypass, replica_groups=[list(range(NCORES))],
                ins=[dsc["ccin"].ap()], outs=[dsc["ccout"].ap()]), "cc", 1)
            b.wait(g, "cc")
            tok = b.dma(g, st[0:1, 65:65 + NCORES],
                        dsc["ccout"].ap().rearrange("a b -> b a"), "G")
            b.wtok(g, tok)

        @block.vector
        def _(v):
            b.wait_allq(v, "G")
            v.drain()
            b.inc(v.reduce_max(out=st[0:1, 64:65], in_=st[0:1, 65:65 + NCORES],
                               axis=AX.X), "sV", 1)

    def emit_act_quant_scalars(self, gmax_cell, clip_cell, inv_col, fac_col, fac_scale):
        b, block = self.b, self.block
        st = self.stat

        @block.scalar
        def _(sc):
            b.wait(sc, "sV")
            b.inc(sc.activation(out=st[0:1, 70:71], in_=clip_cell, func=AF.Sigmoid,
                                bias=self.consts[0:1, 0:1], scale=1.0), "sS", 1)

        @block.vector
        def _(v):
            b.wait(v, "sS")
            v.drain()
            v.tensor_tensor(out=st[0:1, 71:72], in0=gmax_cell, in1=st[0:1, 70:71],
                            op=OP.mult)
            v.drain()
            v.tensor_scalar(out=st[0:1, 72:73], in0=st[0:1, 71:72], scalar1=1.0 / 127.0,
                            scalar2=None, op0=OP.mult)
            v.drain()
            v.reciprocal(out=st[0:1, 73:74], in_=st[0:1, 72:73])
            v.drain()
            b.inc(v.tensor_scalar(out=st[0:1, 74:75], in0=st[0:1, 72:73],
                                  scalar1=fac_scale, scalar2=None, op0=OP.mult), "sV", 1)
        vc = b.cnt["sV"]

        @block.gpsimd
        def _(g):
            g.wait_ge(b.sems["sV"], vc)
            tok = b.dma(g, self.dsc["vec"].ap()[0:1, 0:2], st[0:1, 73:75], "G")
            b.wtok(g, tok)
            tok = b.dma(g, self.stat2col[:, 0:2],
                        self.dsc["vec"].ap()[0:1, 0:2].partition_broadcast(128), "G")
            b.wtok(g, tok)

        @block.vector
        def _(v):
            b.wait_allq(v, "G")
            v.drain()
            v.tensor_copy(out=inv_col, in_=self.stat2col[:, 0:1])
            v.drain()
            b.inc(v.tensor_copy(out=fac_col, in_=self.stat2col[:, 1:2]), "sV", 1)

    def ph_init(self):
        b, block, din = self.b, self.block, self.din

        @block.sync
        def _(q):
            b.gbegin("S", q)
            b.dinc(q.dma_start(out=self.X[:].rearrange("p (c t) -> p c t", c=CC),
                               in_=din["x_t"].ap().rearrange("(c p) t -> p c t", p=128)), "S")
            b.dinc(q.dma_start(out=self.ident[:], in_=din["ident"].ap()), "S")
            b.dinc(q.dma_start(out=self.tri[:], in_=din["tri"].ap()), "S")
            for i, nm in enumerate(["bq", "bk", "bv", "bo", "ln_g", "ln_b"]):
                b.dinc(q.dma_start(out=self.biases[:, i * CC:(i + 1) * CC],
                                   in_=din[nm].ap()), "S")
            b.dinc(q.dma_start(out=self.clip_sb[0:1, 0:1], in_=din["clip_x"].ap()), "S")
            b.dinc(q.dma_start(out=self.clip_sb[0:1, 1:2], in_=din["clip_down"].ap()), "S")

        @block.vector
        def _(v):
            v.memset(self.onecol_bf[:], 1.0)
            v.memset(self.ones_bf[:], 1.0)
            v.memset(self.consts[:], 0.0)
            v.drain()
            v.memset(self.consts[:, 1:2], 1.0)
            v.drain()
            v.memset(self.consts[:, 2:3], EPS_LN)
            v.drain()
            v.memset(self.consts[:, 3:4], 128.5)
            v.drain()
            v.memset(self.consts[:, 4:5], -10.0)
            v.drain()
            b.inc(v.memset(self.consts[:, 5:6], -50.0), "sV", 1)

    def ph_wquant(self):
        b, block, din, dsc = self.b, self.block, self.din, self.dsc
        b.push()
        jobs = [("w_gate", "qwg", 0, 22, 1024), ("w_up", "qwu", 22, 44, 1024),
                ("w_down", "qwd", 44, 52, 2816)]
        wt = [b.scr(f"wt{i}", [128, 2816], F32) for i in range(2)]
        u8t = b.scr("wu8", [128, 2816], U8)
        qbt = [b.scr(f"wqb{i}", [128, 2816], BF16) for i in range(2)]
        ld_t, v_t, st_t = {}, {}, {}
        it = 0
        for (src, dst, c0, c1, W) in jobs:
            for r in range(c1 - c0):
                i = it; it += 1
                bi = i % 2

                @block.sync
                def _(q, src=src, r=r, W=W, bi=bi, i=i):
                    if i >= 2:
                        q.wait_ge(b.sems["sV"], v_t[i - 2])
                    ld_t[i] = b.dma(q, wt[bi][:, 0:W],
                                    din[src].ap()[128 * r:128 * (r + 1), :], "S")

                @block.vector
                def _(v, c0=c0, r=r, W=W, bi=bi, i=i):
                    b.wtok(v, ld_t[i])
                    if i >= 2:
                        b.wtok(v, st_t[i - 2])
                    v.drain()
                    v.reduce_max(out=self.wsc[:, c0 + r:c0 + r + 1], in_=wt[bi][:, 0:W],
                                 axis=AX.X, apply_absolute_value=True)
                    v.drain()
                    v.reciprocal(out=self.stat[:, 2:3], in_=self.wsc[:, c0 + r:c0 + r + 1])
                    v.drain()
                    v.tensor_scalar(out=self.stat[:, 3:4], in0=self.stat[:, 2:3],
                                    scalar1=127.0, scalar2=None, op0=OP.mult)
                    v.drain()
                    v.tensor_scalar(out=u8t[:, 0:W], in0=wt[bi][:, 0:W],
                                    scalar1=self.stat[:, 3:4], scalar2=128.0,
                                    op0=OP.mult, op1=OP.add)
                    v.drain()
                    b.inc(v.tensor_scalar(out=qbt[bi][:, 0:W], in0=u8t[:, 0:W],
                                          scalar1=128.0, scalar2=None, op0=OP.subtract),
                          "sV", 1)
                    v_t[i] = b.cnt["sV"]

                @block.gpsimd
                def _(g, dst=dst, r=r, W=W, bi=bi, i=i):
                    g.wait_ge(b.sems["sV"], v_t[i])
                    st_t[i] = b.dma(g, dsc[dst].ap()[128 * r:128 * (r + 1), :],
                                    qbt[bi][:, 0:W], "G")
        self.wq_store_done = [st_t[it - 1 - k] for k in range(min(NROT, it))]
        self.tap_now("qwg", dsc["qwg"].ap())
        b.pop()

    def ph_l0ffn(self, call):
        b, block, din, dsc, pb = self.b, self.block, self.din, self.dsc, self.pb
        X, st = self.X, self.stat
        b.push()
        rb = b.scr("rb", [128, T], F32)
        cb = b.scr("cb", [128, T], F32)
        b.push()
        xbf = b.scr("xbf", [128, CC * T], BF16)
        x2bf = b.scr("x2bf", [128, CC * T], BF16)
        self.emit_xbf_copies(xbf, x2bf)
        self.emit_ln_stats(xbf, x2bf, rb, cb)
        b.pop()
        xq = b.scr("xq", [128, CC * T], BF16)
        xtmp = b.scr("xtmp", [128, T], F32)
        u8t = b.scr("u8t", [128, T], U8)

        @block.vector
        def _(v):
            v.drain()
            v.memset(st[:, 0:1], 0.0)
            for cc in range(CC):
                v.drain()
                v.tensor_tensor(out=xtmp[:], in0=X[:, T * cc:T * (cc + 1)], in1=rb[:],
                                op=OP.mult)
                v.drain()
                v.tensor_tensor(out=xtmp[:], in0=xtmp[:], in1=cb[:], op=OP.subtract)
                v.drain()
                v.reduce_max(out=st[:, 1:2], in_=xtmp[:], axis=AX.X,
                             apply_absolute_value=True)
                v.drain()
                inst = v.tensor_tensor(out=st[:, 0:1], in0=st[:, 0:1], in1=st[:, 1:2],
                                       op=OP.max)
            b.inc(inst, "sV", 1)
        self.emit_flat_allreduce_max(st[:, 0:1], st)
        self.emit_act_quant_scalars(st[0:1, 64:65], self.clip_sb[0:1, 0:1],
                                    st[:, 5:6], st[:, 6:7], 1.0 / 127.0)

        @block.vector
        def _(v):
            b.wait(v, "sV")
            v.drain()
            v.tensor_scalar(out=st[:, 8:30], in0=self.wsc[:, 0:22],
                            scalar1=st[:, 6:7], scalar2=None, op0=OP.mult)
            v.drain()
            b.inc(v.tensor_scalar(out=st[:, 30:52], in0=self.wsc[:, 22:44],
                                  scalar1=st[:, 6:7], scalar2=None, op0=OP.mult), "sV", 1)

        xq_cc_t = {}

        @block.vector
        def _(v):
            for cc in range(CC):
                v.drain()
                v.tensor_tensor(out=xtmp[:], in0=X[:, T * cc:T * (cc + 1)], in1=rb[:],
                                op=OP.mult)
                v.drain()
                v.tensor_tensor(out=xtmp[:], in0=xtmp[:], in1=cb[:], op=OP.subtract)
                v.drain()
                v.tensor_scalar(out=u8t[:], in0=xtmp[:], scalar1=st[:, 5:6],
                                scalar2=128.0, op0=OP.mult, op1=OP.add)
                v.drain()
                b.inc(v.tensor_scalar(out=xq[:, T * cc:T * (cc + 1)], in0=u8t[:],
                                      scalar1=128.0, scalar2=None, op0=OP.subtract),
                      "sV", 1)
                xq_cc_t[cc] = b.cnt["sV"]
        xq_ready = b.cnt["sV"]
        self.tap_now("xq", xq[:], wait=("sV",))
        self.tap_now("rb", rb[:], wait=("sV",))
        self.tap_now("cb", cb[:], wait=("sV",))
        self.tap_now("gmaxrow", st[0:1, 64:80], wait=("sV",))

        gblk = [b.scr(f"gblk{c}", [128, 1408], BF16) for c in range(CC)]
        ublk = [b.scr(f"ublk{c}", [128, 1408], BF16) for c in range(CC)]
        sg4 = b.scr("sg4", [128, 4 * 512], BF16)
        htl = [b.scr(f"htl{i}", [128, T], F32) for i in range(2)]
        ICG, ICN = 2, 11
        half_mm_t = {}
        half_ev_t = {}
        h_st_t = {}
        wld_t = {}

        @block.vector
        def _(v):
            v.drain()
            b.inc(v.memset(st[:, 2:3], 0.0), "sV", 1)

        for icg in range(ICG):
            @block.sync
            def _(q, icg=icg):
                if icg >= 1:
                    q.wait_ge(b.sems["sT"], half_mm_t[(icg - 1, ICN - 1, 3)])
                else:
                    b.wait(q, "sT")
                for tk in self.wq_store_done:
                    b.wtok(q, tk)
                r0 = icg * 1408
                b.gbegin("S", q)
                for cc in range(CC):
                    b.dinc(q.dma_start_transpose(
                        gblk[cc][:], dsc["qwg"].ap()[r0:r0 + 1408,
                                                     128 * cc:128 * (cc + 1)]), "S")
                    b.dinc(q.dma_start_transpose(
                        ublk[cc][:], dsc["qwu"].ap()[r0:r0 + 1408,
                                                     128 * cc:128 * (cc + 1)]), "S")
                wld_t[icg] = b.gend("S")

            for ic in range(ICN):
                icga = icg * ICN + ic
                for half in range(4):
                    bG = pb[2 * (half % 4)]
                    bU = pb[2 * (half % 4) + 1]

                    @block.tensor
                    def _(t, icg=icg, ic=ic, half=half, bG=bG, bU=bU, icga=icga):
                        b.wtok(t, wld_t[icg])
                        prev = ((icg, ic, half - 4) if half >= 4 else
                                ((icg, ic - 1, half) if ic > 0
                                 else ((icg - 1, ICN - 1, half) if icg > 0 else None)))
                        if prev is not None and prev in half_ev_t:
                            t.wait_ge(b.sems["sV"], half_ev_t[prev])
                        tc = half
                        for wblk, bank in ((gblk, bG), (ublk, bU)):
                            for cc in range(CC):
                                if icg == 0 and ic == 0 and wblk is gblk:
                                    t.wait_ge(b.sems["sV"], xq_cc_t[cc])
                                lhsT = wblk[cc][:, 128 * ic:128 * (ic + 1)]
                                inst = t.matmul(
                                    bank[:, :], lhsT,
                                    xq[:, T * cc + 512 * tc:T * cc + 512 * tc + 512],
                                    start=(cc == 0), stop=(cc == CC - 1))
                        b.inc(inst, "sT", 1)
                        half_mm_t[(icg, ic, half)] = b.cnt["sT"]

                    @block.scalar
                    def _(sc, icg=icg, ic=ic, half=half, bG=bG, icga=icga):
                        sc.wait_ge(b.sems["sT"], half_mm_t[(icg, ic, half)])
                        tc = half
                        inst = sc.activation(out=sg4[:, 512 * tc:512 * (tc + 1)],
                                             in_=bG[:, :], func=AF.Silu,
                                             scale=st[:, 8 + icga:9 + icga])
                        b.inc(inst, "sS", 1)

                    @block.vector
                    def _(v, icg=icg, ic=ic, half=half, bU=bU, icga=icga):
                        b.wait(v, "sS")
                        if icga >= 2 and half == 0:
                            b.wtok(v, h_st_t[icga - 2])
                        tc = half
                        v.drain()
                        v.scalar_tensor_tensor(
                            out=htl[icga % 2][:, 512 * tc:512 * (tc + 1)],
                            in0=bU[:, :], scalar=st[:, 30 + icga:31 + icga],
                            in1=sg4[:, 512 * tc:512 * (tc + 1)],
                            op0=OP.mult, op1=OP.mult)
                        v.drain()
                        v.reduce_max(out=st[:, 1:2],
                                     in_=htl[icga % 2][:, 512 * tc:512 * (tc + 1)],
                                     axis=AX.X, apply_absolute_value=True)
                        v.drain()
                        inst = v.tensor_tensor(out=st[:, 2:3], in0=st[:, 2:3],
                                               in1=st[:, 1:2], op=OP.max)
                        b.inc(inst, "sV", 1)
                        half_ev_t[(icg, ic, half)] = b.cnt["sV"]

                @block.gpsimd
                def _(g, icg=icg, ic=ic, icga=icga):
                    g.wait_ge(b.sems["sV"], half_ev_t[(icg, ic, 3)])
                    h_st_t[icga] = b.dma(
                        g, dsc["hsp"].ap()[128 * icga:128 * (icga + 1), :],
                        htl[icga % 2][:], "G")

        @block.gpsimd
        def _(g):
            if "hrows" in self.taps:
                for tk in h_st_t.values():
                    b.wtok(g, tk)
        self.tap_now("hrows", dsc["hsp"].ap()[0:256, :], wait=())

        @block.vector
        def _(v):
            v.drain()
            b.inc(v.tensor_copy(out=st[:, 0:1], in_=st[:, 2:3]), "sV", 1)
        self.emit_flat_allreduce_max(st[:, 0:1], st)
        self.emit_act_quant_scalars(st[0:1, 64:65], self.clip_sb[0:1, 1:2],
                                    st[:, 5:6], st[:, 6:7], 0.5 / 127.0)

        @block.vector
        def _(v):
            b.wait(v, "sV")
            v.drain()
            b.inc(v.tensor_scalar(out=st[:, 8:16], in0=self.wsc[:, 44:52],
                                  scalar1=st[:, 6:7], scalar2=None, op0=OP.mult), "sV", 1)
        dsc_ready = b.cnt["sV"]
        b.pop()

        b.push()
        ICN2 = 11
        hq = b.scr("hq", [128, ICN2 * T], BF16)
        wdb = [b.scr(f"wdb{i}", [128, 1024], BF16) for i in range(ICN2)]
        ht2 = [b.scr(f"ht2{i}", [128, T], F32) for i in range(2)]
        u8b = b.scr("u8b", [128, T], U8)
        hq_q_t = {}
        hl_t = {}
        dn_mm_t = {}
        dn_ev_t = {}
        for rnd in range(2):
            for ic in range(ICN2):
                icga = rnd * ICN2 + ic

                @block.gpsimd
                def _(g, rnd=rnd, ic=ic, icga=icga):
                    b.wtok(g, h_st_t[icga])
                    if icga >= 2:
                        g.wait_ge(b.sems["sV"], hq_q_t[icga - 2])
                    if icga < 2:
                        g.wait_ge(b.sems["sT"], half_mm_t[(ICG - 1, ICN - 1, 1)])
                    if rnd == 1 and ic == 0:
                        g.wait_ge(b.sems["sT"], dn_mm_t[(0, 3, 1)])
                    hl_t[icga] = b.dma(
                        g, ht2[icga % 2][:],
                        dsc["hsp"].ap()[128 * icga:128 * (icga + 1), :], "G")

                @block.vector
                def _(v, ic=ic, icga=icga):
                    b.wtok(v, hl_t[icga])
                    v.wait_ge(b.sems["sV"], dsc_ready)
                    v.drain()
                    v.tensor_scalar(out=u8b[:], in0=ht2[icga % 2][:], scalar1=st[:, 5:6],
                                    scalar2=128.0, op0=OP.mult, op1=OP.add)
                    v.drain()
                    b.inc(v.tensor_scalar(out=hq[:, T * ic:T * (ic + 1)], in0=u8b[:],
                                          scalar1=128.0, scalar2=None, op0=OP.subtract),
                          "sV", 1)
                    hq_q_t[icga] = b.cnt["sV"]

            @block.sync
            def _(q, rnd=rnd):
                if rnd == 1:
                    q.wait_ge(b.sems["sT"], dn_mm_t[(0, 3, 1)])
                else:
                    q.wait_ge(b.sems["sT"], half_mm_t[(ICG - 1, ICN - 1, 3)])
                b.gbegin("S", q)
                for ic in range(ICN2):
                    icga = rnd * ICN2 + ic
                    b.dinc(q.dma_start_transpose(
                        wdb[ic][:, :],
                        dsc["qwd"].ap()[:, 128 * icga:128 * (icga + 1)]), "S")
            wdl = b.gend("S")

            for sw in range(4):
                hcg, tcg = sw // 2, sw % 2
                for sb in range(2):
                    k = (rnd * 4 + sw) * 2 + sb

                    @block.tensor
                    def _(t, rnd=rnd, sw=sw, sb=sb, hcg=hcg, tcg=tcg, wdl=wdl, k=k):
                        b.wtok(t, wdl)
                        t.wait_ge(b.sems["sV"], hq_q_t[rnd * ICN2 + ICN2 - 1])
                        if k >= 2:
                            t.wait_ge(b.sems["sV"], dn_ev_t[k - 2])
                        for ic in range(ICN2):
                            for hjl in range(2):
                                hj = 2 * sb + hjl
                                hc = 4 * hcg + hj
                                for tj in range(2):
                                    tc = 2 * tcg + tj
                                    inst = t.matmul(
                                        pb[4 * sb + 2 * hjl + tj][:, :],
                                        wdb[ic][:, 128 * hc:128 * (hc + 1)],
                                        hq[:, T * ic + 512 * tc:T * ic + 512 * tc + 512],
                                        start=(ic == 0), stop=(ic == ICN2 - 1))
                        b.inc(inst, "sT", 1)
                        dn_mm_t[(rnd, sw, sb)] = b.cnt["sT"]

                    @block.vector
                    def _(v, rnd=rnd, sw=sw, sb=sb, hcg=hcg, tcg=tcg, k=k):
                        v.wait_ge(b.sems["sT"], dn_mm_t[(rnd, sw, sb)])
                        v.drain()
                        for hjl in range(2):
                            hj = 2 * sb + hjl
                            hc = 4 * hcg + hj
                            for tj in range(2):
                                tc = 2 * tcg + tj
                                inst = v.scalar_tensor_tensor(
                                    out=X[:, T * hc + 512 * tc:T * hc + 512 * tc + 512],
                                    in0=pb[4 * sb + 2 * hjl + tj][:, :],
                                    scalar=st[:, 8 + hc:9 + hc],
                                    in1=X[:, T * hc + 512 * tc:T * hc + 512 * tc + 512],
                                    op0=OP.mult, op1=OP.add)
                        b.inc(inst, "sV", 1)
                        dn_ev_t[k] = b.cnt["sV"]
        b.pop()

    def emit_ln_bf16_apply(self, xbf, rb, cb, rbbf, cbbf):
        b, block = self.b, self.block

        @block.vector
        def _(v):
            b.wait(v, "sS")
            v.drain()
            v.tensor_copy(out=rbbf[:], in_=rb[:])
            v.drain()
            v.tensor_copy(out=cbbf[:], in_=cb[:])
            v.drain()
            for cc in range(CC):
                sl = slice(T * cc, T * (cc + 1))
                v.tensor_tensor(out=xbf[:, sl], in0=xbf[:, sl], in1=rbbf[:], op=OP.mult)
                v.drain()
                inst = v.tensor_tensor(out=xbf[:, sl], in0=xbf[:, sl], in1=cbbf[:],
                                       op=OP.subtract)
                v.drain()
            b.inc(inst, "sV", 1)

    def ph_l0att(self):
        b, block, din, dsc, pb, PBt = self.b, self.block, self.din, self.dsc, self.pb, self.PBt
        X, st = self.X, self.stat
        NB = 32
        b.push()
        xbf = b.scr("xbf2", [128, CC * T], BF16)
        b.push()
        x2bf = b.scr("x2bf2", [128, CC * T], BF16)
        rb = b.scr("rb2", [128, T], F32)
        cb = b.scr("cb2", [128, T], F32)
        rbbf = b.scr("rbbf", [128, T], BF16)
        cbbf = b.scr("cbbf", [128, T], BF16)
        self.emit_xbf_copies(xbf, x2bf)
        self.emit_ln_stats(xbf, x2bf, rb, cb)
        self.emit_ln_bf16_apply(xbf, rb, cb, rbbf, cbbf)
        b.pop()
        xn = xbf
        xn_ready = b.cnt["sV"]

        acc = b.scr("acc", [128, 8 * 66], F32)
        accbf = b.scr("accbf", [128, 8 * 66], BF16)
        sst = b.scr("sst", [128, 512], F32)
        qw = [b.scr_at(f"qw{i}", [128, 128], BF16, "acc") if i < 8 else
              b.scr_at(f"qw{i}", [128, 128], BF16, "accbf") for i in range(0)]
        ktm = b.scr("ktm", [128, NT * 1024], BF16)
        vtm = b.scr("vtm", [128, NT * 1024], BF16)
        kv_done = {}

        b.push()
        kvw = [b.scr(f"kvw{c}", [128, 1024], BF16) for c in range(CC)]
        for which, dst in (("k", ktm), ("v", vtm)):
            col0 = 1024 if which == "k" else 2048
            wtoks = []

            @block.gpsimd
            def _(g, which=which, col0=col0, wtoks=wtoks):
                b.wait(g, "sT")
                if which == "k":
                    b.wait(g, "sV")
                for c in range(CC):
                    wtoks.append(b.dma(
                        g, kvw[c][:],
                        din["w_qkv_t"].ap()[128 * c:128 * (c + 1), col0:col0 + 1024], "G"))

            mm_t = {}
            ev_t = {}
            for tt in range(NT):
                bk0 = pb[4 + 2 * (tt % 2)]
                bk1 = pb[5 + 2 * (tt % 2)]

                @block.tensor
                def _(t, which=which, tt=tt, bk0=bk0, bk1=bk1, wtoks=wtoks):
                    if tt == 0:
                        t.wait_ge(b.sems["sV"], xn_ready)
                        for tk in wtoks:
                            b.wtok(t, tk)
                    if tt < 2 and which == "v":
                        t.wait_ge(b.sems["sS"], kv_done["k"][1])
                    if tt >= 2:
                        t.wait_ge(b.sems["sS"], ev_t[tt - 2])
                    for c in range(CC):
                        lhsT = xn[:, T * c + 128 * tt:T * c + 128 * (tt + 1)]
                        t.matmul(bk0[:, :], lhsT, kvw[c][:, 0:512],
                                 start=(c == 0), stop=(c == CC - 1))
                        inst = t.matmul(bk1[:, :], lhsT, kvw[c][:, 512:1024],
                                        start=(c == 0), stop=(c == CC - 1))
                    b.inc(inst, "sT", 1)
                    mm_t[tt] = b.cnt["sT"]

                @block.scalar
                def _(sc, which=which, tt=tt, bk0=bk0, bk1=bk1):
                    sc.wait_ge(b.sems["sT"], mm_t[tt])
                    if which == "k":
                        o = ktm[:, 1024 * tt:1024 * (tt + 1)]
                        sc.activation(out=o[:, 0:512], in_=bk0[:, :], func=AF.Exp,
                                      bias=self.consts[:, 0:1], scale=1.0)
                        inst = sc.activation(out=o[:, 512:1024], in_=bk1[:, :], func=AF.Exp,
                                             bias=self.consts[:, 0:1], scale=1.0)
                    else:
                        o = vtm[:, 1024 * tt:1024 * (tt + 1)]
                        sc.copy(out=o[:, 0:512], in_=bk0[:, :])
                        inst = sc.copy(out=o[:, 512:1024], in_=bk1[:, :])
                    b.inc(inst, "sS", 1)
                    ev_t[tt] = b.cnt["sS"]
            kv_done[which] = (mm_t[NT - 1], b.cnt["sS"])
        b.pop()

        b.push()
        expq = b.scr("expq", [128, CC * T], BF16)
        qwbase = self.nc.alloc_sbuf_tensor_at(
            "scr_qwall", [128, 16 * 128], BF16,
            offset=b.scratch_addr + b.offs["acc"][0])
        qw = [qwbase[:, 128 * i:128 * (i + 1)] for i in range(16)]
        q_mm_t = {}
        q_ev_t = {}
        qw_tok = {}
        for hp in range(CC):
            @block.gpsimd
            def _(g, hp=hp):
                if hp >= 2:
                    g.wait_ge(b.sems["sT"], q_mm_t[hp - 2])
                else:
                    b.wait(g, "sT")
                toks = []
                for c in range(CC):
                    toks.append(b.dma(
                        g, qw[8 * (hp % 2) + c][:],
                        din["w_qkv_t"].ap()[128 * c:128 * (c + 1),
                                            128 * hp:128 * (hp + 1)], "G"))
                qw_tok[hp] = toks

            bks = [pb[4 * (hp % 2) + j] for j in range(4)]

            @block.tensor
            def _(t, hp=hp, bks=bks):
                for tk in qw_tok[hp]:
                    b.wtok(t, tk)
                if hp < 2:
                    t.wait_ge(b.sems["sS"], kv_done["v"][1])
                if hp >= 2:
                    t.wait_ge(b.sems["sS"], q_ev_t[hp - 2])
                for c in range(CC):
                    lhsT = qw[8 * (hp % 2) + c][:]
                    for tc in range(TC):
                        inst = t.matmul(bks[tc][:, :], lhsT,
                                        xn[:, T * c + 512 * tc:T * c + 512 * (tc + 1)],
                                        start=(c == 0), stop=(c == CC - 1))
                b.inc(inst, "sT", 1)
                q_mm_t[hp] = b.cnt["sT"]

            @block.scalar
            def _(sc, hp=hp, bks=bks):
                sc.wait_ge(b.sems["sT"], q_mm_t[hp])
                for tc in range(TC):
                    inst = sc.activation(out=expq[:, T * hp + 512 * tc:T * hp + 512 * (tc + 1)],
                                         in_=bks[tc][:, :], func=AF.Exp,
                                         bias=self.consts[:, 0:1], scale=1.0)
                b.inc(inst, "sS", 1)
                q_ev_t[hp] = b.cnt["sS"]

        attnstore = xn

        @block.vector
        def _(v):
            v.wait_ge(b.sems["sS"], q_ev_t[CC - 1])
            v.wait_ge(b.sems["sS"], kv_done["v"][1])
            v.drain()
            v.memset(acc[:], 0.0)
            v.drain()
            v.memset(accbf[:], 0.0)
            v.drain()
            v.memset(acc.rearrange("p (h z) -> p h z", h=8)[:, :, 65:66], 1.0)
            v.drain()
            b.inc(v.memset(accbf.rearrange("p (h z) -> p h z", h=8)[:, :, 65:66], 1.0),
                  "sV", 1)
        acc_ready = b.cnt["sV"]

        att_mm_t = {}
        ctx_mm_t = {}
        v1_t = {}
        v2_t = {}
        sst_dma_t = {}
        for n in range(NB):
            r = n % 2
            pctx = PBt[:, 1024 * r:1024 * r + 512]
            patt = PBt[:, 2048 + 1024 * r:2048 + 1024 * r + 512]
            psts = PBt[:, 2560 + 1024 * r:2560 + 1024 * r + 512]
            psum_k = PBt[:, 1024 * r + 512:1024 * r + 512 + 8]
            tt, half = n // 2, n % 2
            krows = slice(64 * half, 64 * half + 64)

            @block.tensor
            def _(t, n=n, pctx=pctx, patt=patt, psts=psts, psum_k=psum_k,
                  tt=tt, krows=krows):
                if n == 0:
                    t.wait_ge(b.sems["sV"], acc_ready)
                if n >= 1:
                    t.wait_ge(b.sems["sV"], v1_t[n - 1])
                if n >= 2:
                    t.wait_ge(b.sems["sV"], v2_t[n - 2])
                for h in range(16):
                    hp, par = h // 2, h % 2
                    qrows = slice(64 * par, 64 * par + 64)
                    rhs = expq[qrows, T * hp + 64 * n:T * hp + 64 * (n + 1)]
                    acol = accbf[qrows, 66 * hp:66 * hp + 66]
                    rbase = 32 * (h // 4)
                    cbase = 128 * (h % 4)
                    t.matmul(patt[slice(64 * par, 64 * par + 64),
                                  64 * hp:64 * (hp + 1)],
                             acol[:, 0:64], rhs, start=True, stop=True)
                    t.matmul(psts[rbase:rbase + 1, cbase:cbase + 64],
                             acol[:, 64:65], rhs, start=True, stop=True,
                             tile_position=(64 * par, rbase))
                    inst = t.matmul(psts[rbase:rbase + 1, cbase + 64:cbase + 128],
                                    acol[:, 65:66], rhs, start=True, stop=True,
                                    tile_position=(64 * par, rbase))
                b.inc(inst, "sT", 1)
                att_mm_t[n] = b.cnt["sT"]
                for h in range(16):
                    hp, par = h // 2, h % 2
                    kl = ktm[krows, 1024 * tt + 64 * h:1024 * tt + 64 * (h + 1)]
                    vl = vtm[krows, 1024 * tt + 64 * h:1024 * tt + 64 * (h + 1)]
                    ol = pctx[slice(64 * par, 64 * par + 64), 64 * hp:64 * (hp + 1)]
                    t.matmul(ol, kl, vl, start=True, stop=True)
                    inst = t.matmul(psum_k[slice(64 * par, 64 * par + 64), hp:hp + 1],
                                    kl, self.onecol_bf[krows, 0:1],
                                    start=True, stop=True)
                b.inc(inst, "sT", 1)
                ctx_mm_t[n] = b.cnt["sT"]

            @block.vector
            def _(v, n=n, pctx=pctx, psum_k=psum_k):
                v.wait_ge(b.sems["sT"], ctx_mm_t[n])
                v.drain()
                acc3 = acc.rearrange("p (h z) -> p h z", h=8)
                v.tensor_tensor(out=acc3[:, :, 0:64], in0=acc3[:, :, 0:64],
                                in1=pctx.rearrange("p (h z) -> p h z", h=8), op=OP.add)
                v.drain()
                v.tensor_tensor(out=acc3[:, :, 64:65], in0=acc3[:, :, 64:65],
                                in1=psum_k.rearrange("p (h z) -> p h z", h=8), op=OP.add)
                v.drain()
                inst = v.tensor_copy(
                    out=accbf.rearrange("p (h z) -> p h z", h=8)[:, :, 0:65],
                    in_=acc3[:, :, 0:65])
                v.drain()
                b.inc(inst, "sV", 1)
                v1_t[n] = b.cnt["sV"]

            @block.vector
            def _(v, n=n, patt=patt, psts=psts):
                if n >= 1:
                    b.wtok(v, sst_dma_t[n - 1])
                v.drain()
                a3 = attnstore.rearrange("p (hp tcol) -> p hp tcol", hp=8)
                v.tensor_copy(out=a3[:, :, 64 * n:64 * (n + 1)],
                              in_=patt.rearrange("p (hp z) -> p hp z", hp=8))
                v.drain()
                inst = v.tensor_copy(out=sst[:], in_=psts[:, :])
                v.drain()
                b.inc(inst, "sV", 1)
                v2_t[n] = b.cnt["sV"]

            @block.gpsimd
            def _(g, n=n):
                g.wait_ge(b.sems["sV"], v2_t[n])
                sst_dma_t[n] = b.dma(
                    g, dsc["statsd"].ap()[:, 512 * n:512 * (n + 1)], sst[:], "G")

        s2 = b.scr_at("s2", [128, 512], F32, "sst")
        minvt = b.scr_at("minvt", [128, 256], F32, "accbf")

        @block.gpsimd
        def _(g):
            for n in range(NB):
                b.wtok(g, sst_dma_t[n])
            src = dsc["statsd"].ap().rearrange("(rg s) c -> rg s c", rg=4)[:, 0:1, :] \
                .squeeze(1)
            src = src.rearrange("rg (n c) -> rg n c", n=NB)
            self._l0_sst_rd = b.dma(g, s2[:], src, "G")

        @block.vector
        def _(v):
            b.wtok(v, self._l0_sst_rd)
            v.drain()
            s3 = s2[:].rearrange("p (hs w t) -> p hs w t", hs=4, w=2)
            m3 = minvt[:].rearrange("p (hs t) -> p hs t", hs=4)
            v.scalar_tensor_tensor(out=m3, in0=s3[:, :, 1, :], scalar=8e-6,
                                   in1=s3[:, :, 0, :], op0=OP.mult, op1=OP.max)
            v.drain()
            b.inc(v.reciprocal(out=minvt[:], in_=minvt[:]), "sV", 1)
        minv_ready = b.cnt["sV"]

        @block.gpsimd
        def _(g):
            g.wait_ge(b.sems["sV"], minv_ready)
            toks = []
            for rg in range(4):
                dst = dsc["minvd"].ap()[4 * rg:4 * rg + 4, :] \
                    .rearrange("hs (n t) -> n hs t", n=NB)
                toks.append(b.dma(
                    g, dst,
                    minvt[32 * rg:32 * (rg + 1), :].rearrange(
                        "n (hs t) -> n hs t", hs=4), "G"))
            self._l0_minv_wr = toks

        b.pop()
        gsc = [b.scr(f"gsc{i}", [128, T], BF16) for i in range(2)]
        gsc_tok = {}
        scale_t = {}
        for hp in range(CC):
            @block.gpsimd
            def _(g, hp=hp):
                if hp == 0:
                    for tk in self._l0_minv_wr:
                        b.wtok(g, tk)
                    b.wait(g, "sT")
                if hp >= 2:
                    g.wait_ge(b.sems["sV"], scale_t[hp - 2])
                src = dsc["minvd"].ap()[2 * hp:2 * hp + 2, :]
                gsc_tok[hp] = b.dma(
                    g, gsc[hp % 2][:],
                    src.unsqueeze(1).broadcast_to([2, 64, T]), "G")

            @block.vector
            def _(v, hp=hp):
                b.wtok(v, gsc_tok[hp])
                v.drain()
                sl = slice(T * hp, T * (hp + 1))
                b.inc(v.tensor_tensor(out=attnstore[:, sl], in0=attnstore[:, sl],
                                      in1=gsc[hp % 2][:], op=OP.mult), "sV", 1)
                scale_t[hp] = b.cnt["sV"]

        wob = [b.scr(f"wob{i}", [128, 1024], BF16) for i in range(CC)]

        @block.gpsimd
        def _(g):
            b.wait(g, "sT")
            toks = []
            for c in range(CC):
                toks.append(b.dma(g, wob[c][:],
                                  din["w_out_t"].ap()[128 * c:128 * (c + 1), :], "G"))
            self._l0_wo_tok = toks

        wo_mm_t = {}
        wo_ev_t = {}
        for dc in range(CC):
            bks = [pb[4 * (dc % 2) + j] for j in range(4)]

            @block.tensor
            def _(t, dc=dc, bks=bks):
                if dc == 0:
                    for tk in self._l0_wo_tok:
                        b.wtok(t, tk)
                    t.wait_ge(b.sems["sV"], scale_t[CC - 1])
                if dc >= 2:
                    t.wait_ge(b.sems["sV"], wo_ev_t[dc - 2])
                for c in range(CC):
                    lhsT = wob[c][:, 128 * dc:128 * (dc + 1)]
                    for tc in range(TC):
                        inst = t.matmul(bks[tc][:, :], lhsT,
                                        attnstore[:, T * c + 512 * tc:T * c + 512 * (tc + 1)],
                                        start=(c == 0), stop=(c == CC - 1))
                b.inc(inst, "sT", 1)
                wo_mm_t[dc] = b.cnt["sT"]

            @block.vector
            def _(v, dc=dc, bks=bks):
                v.wait_ge(b.sems["sT"], wo_mm_t[dc])
                for tc in range(TC):
                    v.drain()
                    inst = v.scalar_tensor_tensor(
                        out=X[:, T * dc + 512 * tc:T * dc + 512 * (tc + 1)],
                        in0=bks[tc][:, :], scalar=1.0,
                        in1=X[:, T * dc + 512 * tc:T * dc + 512 * (tc + 1)],
                        op0=OP.mult, op1=OP.add)
                b.inc(inst, "sV", 1)
                wo_ev_t[dc] = b.cnt["sV"]
        b.pop()

    def ph_l1stash(self):
        b, block, din, dsc = self.b, self.block, self.din, self.dsc

        @block.gpsimd
        def _(g):
            toks = []
            for e in range(8):
                toks.append(b.dma(g, dsc["ved"].ap()[e], din["ve_r"].ap()[e], "G"))
            for j in range(8):
                toks.append(b.dma(g, dsc["oed"].ap()[512 * j:512 * (j + 1), :],
                                  din["oe_flat"].ap()[512 * j:512 * (j + 1), :], "G"))
            self.l1stash_tok = toks

    def ph_l1ffn(self, call):
        b, block, din, dsc, pb, PBt = self.b, self.block, self.din, self.dsc, self.pb, self.PBt
        X, st = self.X, self.stat
        b.push()
        xbf = b.scr("xbf3", [128, CC * T], BF16)
        b.push()
        x2bf = b.scr("x2bf3", [128, CC * T], BF16)
        rb = b.scr("rb3", [128, T], F32)
        cb = b.scr("cb3", [128, T], F32)
        rbbf = b.scr("rbbf3", [128, T], BF16)
        cbbf = b.scr("cbbf3", [128, T], BF16)
        self.emit_xbf_copies(xbf, x2bf)
        self.emit_ln_stats(xbf, x2bf, rb, cb)
        self.emit_ln_bf16_apply(xbf, rb, cb, rbbf, cbbf)
        b.pop()
        xn = xbf
        xn_ready = b.cnt["sV"]

        gates = b.scr("gates", [128, NT * 64], F32)
        maskbf = b.scr("maskbf", [128, NT * 64], BF16)
        qkT = b.scr("qkT", [128, 8 * T], BF16)
        values = b.scr("valuesl1", [128, NT * 520], BF16)

        b.push()
        wg1 = b.scr("wg1", [128, 8 * 64], BF16)
        wq1 = b.scr("wq1", [128, 8 * 512], BF16)
        wk1 = b.scr("wk1", [128, 8 * 512], BF16)

        @block.gpsimd
        def _(g):
            b.wait(g, "sT")
            b.wait(g, "sV")
            toks = []
            for c in range(CC):
                toks.append(b.dma(g, wg1[:, 64 * c:64 * (c + 1)],
                                  din["w_g1"].ap()[128 * c:128 * (c + 1), :], "G"))
                toks.append(b.dma(g, wq1[:, 512 * c:512 * (c + 1)],
                                  din["w_q1"].ap()[128 * c:128 * (c + 1), :], "G"))
                toks.append(b.dma(g, wk1[:, 512 * c:512 * (c + 1)],
                                  din["w_k1"].ap()[128 * c:128 * (c + 1), :], "G"))
            self._l1_w_tok = toks

        g_mm_t = {}
        g_ev_t = {}
        for tt in range(NT):
            bkg = pb[6 + (tt % 2)]

            @block.tensor
            def _(t, tt=tt, bkg=bkg):
                if tt == 0:
                    t.wait_ge(b.sems["sV"], xn_ready)
                    for tk in self._l1_w_tok:
                        b.wtok(t, tk)
                if tt >= 2:
                    t.wait_ge(b.sems["sS"], g_ev_t[tt - 2])
                for c in range(CC):
                    inst = t.matmul(bkg[:, 0:64],
                                    xn[:, T * c + 128 * tt:T * c + 128 * (tt + 1)],
                                    wg1[:, 64 * c:64 * (c + 1)],
                                    start=(c == 0), stop=(c == CC - 1))
                b.inc(inst, "sT", 1)
                g_mm_t[tt] = b.cnt["sT"]

            @block.scalar
            def _(sc, tt=tt, bkg=bkg):
                sc.wait_ge(b.sems["sT"], g_mm_t[tt])
                b.inc(sc.activation(out=gates[:, 64 * tt:64 * (tt + 1)], in_=bkg[:, 0:64],
                                    func=AF.Sigmoid, bias=self.consts[:, 0:1],
                                    scale=1.0), "sS", 1)
                g_ev_t[tt] = b.cnt["sS"]
        gates_done = b.cnt["sS"]

        @block.vector
        def _(v):
            v.wait_ge(b.sems["sS"], gates_done)
            v.drain()
            g4 = gates[:].rearrange("p (tt h e) -> p tt h e", tt=NT, h=8)
            v.reduce_max(out=st[:, 16:16 + 128].rearrange("p (tt h) -> p tt h", tt=NT),
                         in_=g4, axis=AX.X)
            v.drain()
            mxb = st[:, 16:16 + 128].rearrange("p (tt h) -> p tt h", tt=NT) \
                .unsqueeze(3).broadcast_to([128, NT, 8, 8])
            b.inc(v.tensor_tensor(out=maskbf[:].rearrange("p (tt h e) -> p tt h e",
                                                          tt=NT, h=8),
                                  in0=g4, in1=mxb, op=OP.is_ge), "sV", 1)
        mask_ready = b.cnt["sV"]

        qk_mm_t = {}
        qk_ev_t = {}
        for i, (which, mc) in enumerate([(w, m) for w in ("q", "k") for m in range(4)]):
            wsb = wq1 if which == "q" else wk1
            qoff = (0 if which == "q" else 4 * T)
            bks = [pb[4 * (i % 2) + j] for j in range(4)]

            @block.tensor
            def _(t, which=which, mc=mc, i=i, wsb=wsb, bks=bks):
                if i < 2:
                    t.wait_ge(b.sems["sS"], gates_done)
                if i >= 2:
                    t.wait_ge(b.sems["sV"], qk_ev_t[i - 2])
                for c in range(CC):
                    lhsT = wsb[:, 512 * c + 128 * mc:512 * c + 128 * (mc + 1)]
                    for tc in range(TC):
                        inst = t.matmul(bks[tc][:, :], lhsT,
                                        xn[:, T * c + 512 * tc:T * c + 512 * (tc + 1)],
                                        start=(c == 0), stop=(c == CC - 1))
                b.inc(inst, "sT", 1)
                qk_mm_t[i] = b.cnt["sT"]

            @block.vector
            def _(v, which=which, mc=mc, i=i, qoff=qoff, bks=bks):
                v.wait_ge(b.sems["sT"], qk_mm_t[i])
                scale = 0.125 if which == "q" else 1.0
                for tc in range(TC):
                    v.drain()
                    inst = v.tensor_scalar(
                        out=qkT[:, qoff + T * mc + 512 * tc:qoff + T * mc + 512 * (tc + 1)],
                        in0=bks[tc][:, :], scalar1=scale, scalar2=None, op0=OP.mult)
                b.inc(inst, "sV", 1)
                qk_ev_t[i] = b.cnt["sV"]
        qk_done = b.cnt["sV"]
        b.pop()

        b.push()
        vew = [b.scr(f"vew{j}", [128, 512], BF16) for j in range(32)]
        vtmp = b.scr("vtmp", [128, 512], F32)
        val_mm_t = {}
        val_ev_t = {}
        ve_tok = {}
        for eg in range(2):
            @block.gpsimd
            def _(g, eg=eg):
                for tk in getattr(self, "l1stash_tok", []):
                    b.wtok(g, tk)
                if eg == 0:
                    b.wait(g, "sT")
                    b.wait(g, "sV")
                if eg == 1:
                    t_last = val_mm_t[(NT - 1, 0)]
                    g.wait_ge(b.sems["sT"], t_last)
                for j in range(4):
                    e = 4 * eg + j
                    for c in range(CC):
                        ve_tok[(eg, j, c)] = b.dma(
                            g, vew[8 * j + c][:],
                            dsc["ved"].ap()[e, 128 * c:128 * (c + 1), :], "G")

            for tt in range(NT):
                bks = [pb[4 * (tt % 2) + j] for j in range(4)]

                @block.tensor
                def _(t, tt=tt, eg=eg, bks=bks):
                    if tt == 0:
                        if eg == 0:
                            t.wait_ge(b.sems["sV"], qk_done)
                        for k in list(ve_tok):
                            if k[0] == eg:
                                b.wtok(t, ve_tok[k])
                    prev = (tt - 2, eg) if tt >= 2 else ((NT - 2 + tt, eg - 1) if eg > 0 else None)
                    if prev in val_ev_t:
                        t.wait_ge(b.sems["sV"], val_ev_t[prev])
                    for c in range(CC):
                        lhsT = xn[:, T * c + 128 * tt:T * c + 128 * (tt + 1)]
                        for j in range(4):
                            inst = t.matmul(bks[j][:, :], lhsT, vew[8 * j + c][:],
                                            start=(c == 0), stop=(c == CC - 1))
                    b.inc(inst, "sT", 1)
                    val_mm_t[(tt, eg)] = b.cnt["sT"]

                @block.vector
                def _(v, tt=tt, eg=eg, bks=bks):
                    v.wait_ge(b.sems["sT"], val_mm_t[(tt, eg)])
                    if tt == 0 and eg == 0:
                        v.wait_ge(b.sems["sV"], mask_ready)
                    v3 = values[:, 520 * tt:520 * (tt + 1)].rearrange(
                        "p (h z) -> p h z", h=8)
                    for j in range(4):
                        e = 4 * eg + j
                        mask_ap = maskbf[:].rearrange(
                            "p (tt h e) -> p tt h e", tt=NT, h=8) \
                            [:, tt:tt + 1, :, e:e + 1].squeeze(1) \
                            .broadcast_to([128, 8, 64])
                        dat = bks[j][:, :].rearrange("p (h z) -> p h z", h=8)
                        v.drain()
                        if eg == 0 and j == 0:
                            v.tensor_tensor(out=v3[:, :, 0:64], in0=dat,
                                            in1=mask_ap, op=OP.mult)
                        else:
                            v.tensor_tensor(out=vtmp[:].rearrange(
                                "p (h z) -> p h z", h=8), in0=dat,
                                in1=mask_ap, op=OP.mult)
                            v.drain()
                            v.tensor_tensor(out=v3[:, :, 0:64], in0=v3[:, :, 0:64],
                                            in1=vtmp[:].rearrange(
                                                "p (h z) -> p h z", h=8),
                                            op=OP.add)
                    v.drain()
                    if eg == 1:
                        gsap = st[:, 16 + 8 * tt:16 + 8 * (tt + 1)] \
                            .unsqueeze(2).broadcast_to([128, 8, 64])
                        inst = v.tensor_tensor(out=v3[:, :, 0:64], in0=v3[:, :, 0:64],
                                               in1=gsap, op=OP.mult)
                        v.drain()
                    else:
                        inst = v.nop()
                    b.inc(inst, "sV", 1)
                    val_ev_t[(tt, eg)] = b.cnt["sV"]

        @block.vector
        def _(v):
            v.drain()
            b.inc(v.memset(values[:].rearrange("p (tt h z) -> p tt h z", tt=NT, h=8)
                           [:, :, :, 64:65], 1.0), "sV", 1)
        values_done = b.cnt["sV"]
        b.pop()

        pttm = [b.scr(f"pttm{i}", [128, 1024], BF16) for i in range(2)]
        araw = b.scr("araw", [128, 4 * T], BF16)
        lbuf = b.scr("lbuf", [128, 512], F32)
        s_mm_t = {}
        p_ev_t = {}
        msk_t = {}
        a_mm_t = {}
        a_ev_t = {}
        l_dma_t = {}
        QT = qkT
        pa = PBt[:, 0:512]
        pl = PBt[:, 512:1024]
        for hp in range(4):
            for qc in range(TC):
                jt_list = list(range(4 * qc + 4))
                nji = len(jt_list)
                for ji, jt in enumerate(jt_list):
                    psS = [PBt[:, 1024 + 1024 * (ji % 2) + 512 * par:
                               1024 + 1024 * (ji % 2) + 512 * (par + 1)]
                           for par in range(2)]
                    psSm = PBt[:, 1024 + 1024 * (ji % 2):2048 + 1024 * (ji % 2)]

                    @block.tensor
                    def _(t, hp=hp, qc=qc, jt=jt, ji=ji, nji=nji, psS=psS):
                        if (hp, qc) == (0, 0) and ji == 0:
                            t.wait_ge(b.sems["sV"], values_done)
                        if ji >= 2:
                            t.wait_ge(b.sems["sS"], p_ev_t[(hp, qc, ji - 2)])
                        if ji == 0:
                            prev = ((hp, qc - 1) if qc > 0 else
                                    ((hp - 1, TC - 1) if hp > 0 else None))
                            if prev is not None:
                                t.wait_ge(b.sems["sV"], a_ev_t[prev])
                        for par in range(2):
                            rows = slice(64 * par, 64 * par + 64)
                            inst = t.matmul(
                                psS[par][:, :],
                                QT[rows, 4 * T + T * hp + 128 * jt:
                                   4 * T + T * hp + 128 * (jt + 1)],
                                QT[rows, T * hp + 512 * qc:T * hp + 512 * (qc + 1)],
                                start=True, stop=True)
                        b.inc(inst, "sT", 1)
                        s_mm_t[(hp, qc, ji)] = b.cnt["sT"]

                    @block.scalar
                    def _(sc, hp=hp, qc=qc, jt=jt, ji=ji, psSm=psSm):
                        sc.wait_ge(b.sems["sT"], s_mm_t[(hp, qc, ji)])
                        if ji >= 2:
                            sc.wait_ge(b.sems["sT"], a_mm_t[(hp, qc, ji - 2)])
                        inst = sc.activation(out=pttm[ji % 2][:], in_=psSm[:, :],
                                             func=AF.Exp,
                                             bias=self.consts[:, 4:5], scale=1.0)
                        b.inc(inst, "sS", 1)
                        p_ev_t[(hp, qc, ji)] = b.cnt["sS"]

                    if jt >= 4 * qc:
                        @block.vector
                        def _(v, hp=hp, qc=qc, jt=jt, ji=ji):
                            v.wait_ge(b.sems["sS"], p_ev_t[(hp, qc, ji)])
                            bidx = jt - 4 * qc
                            v.drain()
                            for par in range(2):
                                inst = v.tensor_tensor(
                                    out=pttm[ji % 2][:, 512 * par:512 * (par + 1)],
                                    in0=pttm[ji % 2][:, 512 * par:512 * (par + 1)],
                                    in1=self.tri[:, 512 * bidx:512 * (bidx + 1)],
                                    op=OP.mult)
                                v.drain()
                            b.inc(inst, "sV", 1)
                            msk_t[(hp, qc, ji)] = b.cnt["sV"]

                    @block.tensor
                    def _(t, hp=hp, qc=qc, jt=jt, ji=ji, nji=nji):
                        t.wait_ge(b.sems["sS"], p_ev_t[(hp, qc, ji)])
                        if (hp, qc, ji) in msk_t:
                            t.wait_ge(b.sems["sV"], msk_t[(hp, qc, ji)])
                        for par in range(2):
                            h = 2 * hp + par
                            vl = values[:, 520 * jt + 65 * h:520 * jt + 65 * h + 64]
                            ol = values[:, 520 * jt + 65 * h + 64:520 * jt + 65 * h + 65]
                            pt = pttm[ji % 2][:, 512 * par:512 * (par + 1)]
                            t.matmul(pa[slice(64 * par, 64 * par + 64), :], vl, pt,
                                     start=(ji == 0), stop=(ji == nji - 1))
                            inst = t.matmul(pl[32 * par:32 * par + 1, :], ol, pt,
                                            start=(ji == 0), stop=(ji == nji - 1),
                                            tile_position=(0, 32 * par))
                        b.inc(inst, "sT", 1)
                        a_mm_t[(hp, qc, ji)] = b.cnt["sT"]

                @block.vector
                def _(v, hp=hp, qc=qc, nji=nji):
                    v.wait_ge(b.sems["sT"], a_mm_t[(hp, qc, nji - 1)])
                    if (hp, qc) != (0, 0):
                        prev = ((hp, qc - 1) if qc > 0 else (hp - 1, TC - 1))
                        b.wtok(v, l_dma_t[prev])
                    v.drain()
                    v.tensor_copy(out=araw[:, T * hp + 512 * qc:T * hp + 512 * (qc + 1)],
                                  in_=pa[:, :])
                    v.drain()
                    inst = v.tensor_copy(out=lbuf[:], in_=pl[:, :])
                    v.drain()
                    b.inc(inst, "sV", 1)
                    a_ev_t[(hp, qc)] = b.cnt["sV"]

                @block.gpsimd
                def _(g, hp=hp, qc=qc):
                    g.wait_ge(b.sems["sV"], a_ev_t[(hp, qc)])
                    l_dma_t[(hp, qc)] = b.dma(
                        g, dsc["lst"].ap()[:, 512 * (4 * hp + qc):512 * (4 * hp + qc + 1)],
                        lbuf[:], "G")

        lsb2 = b.scr("lsb2", [128, 128], F32)

        @block.gpsimd
        def _(g):
            for k in l_dma_t.values():
                b.wtok(g, k)
            src = dsc["lst"].ap()[:, 0:16 * 512] \
                .rearrange("(ra rb) c -> ra rb c", ra=4)[0:2, 0:1, :].squeeze(1)
            src = src.rearrange("par (hp qc q) -> par hp qc q", hp=4, qc=4)
            dst = dsc["ld8"].ap().rearrange("(hp par) (qc q) -> par hp qc q",
                                            hp=4, qc=4)
            tk = b.dma(g, dst, src, "G")
            b.wtok(g, tk)
            tk = b.dma(g, lsb2[:],
                       dsc["ld8"].ap().rearrange("h (qa qb) -> h qa qb", qa=16), "G")
            self._l1_l_rd = tk

        @block.vector
        def _(v):
            b.wtok(v, self._l1_l_rd)
            v.drain()
            v.reciprocal(out=lsb2[:], in_=lsb2[:])
            v.drain()
            b.inc(v.tensor_scalar(out=lsb2[:], in0=lsb2[:], scalar1=0.5,
                                  scalar2=None, op0=OP.mult), "sV", 1)
        minv_ready = b.cnt["sV"]

        @block.gpsimd
        def _(g):
            g.wait_ge(b.sems["sV"], minv_ready)
            self._l1_minv_wr = b.dma(
                g, dsc["minvd1"].ap().rearrange("h (qa qb) -> h qa qb", qa=16),
                lsb2[:], "G")

        gmT = b.scr("gmT", [64, T], F32)
        minvB = b.scr("minvB", [64, T], F32)
        gmtm = gates
        gm_tr_t = {}

        @block.vector
        def _(v):
            b.wait(v, "sV")
            v.drain()
            b.inc(v.tensor_tensor(out=gmtm[:], in0=gates[:], in1=maskbf[:],
                                  op=OP.mult), "sV", 1)
        gmtm_ready = b.cnt["sV"]

        for tt in range(NT):
            ptr = PBt[:, 3072 + 512 * (tt % 2):3072 + 512 * (tt % 2) + 128]

            @block.tensor
            def _(t, tt=tt, ptr=ptr):
                t.wait_ge(b.sems["sV"], gmtm_ready)
                if tt >= 2:
                    t.wait_ge(b.sems["sV"], gm_tr_t[tt - 2])
                b.inc(t.transpose(ptr[0:64, 0:128], gmtm[:, 64 * tt:64 * (tt + 1)],
                                  self.ident[:]), "sT", 1)
                gm_tr_t[(tt, "mm")] = b.cnt["sT"]

            @block.vector
            def _(v, tt=tt, ptr=ptr):
                v.wait_ge(b.sems["sT"], gm_tr_t[(tt, "mm")])
                v.drain()
                b.inc(v.tensor_copy(out=gmT[0:64, 128 * tt:128 * (tt + 1)],
                                    in_=ptr[0:64, :]), "sV", 1)
                gm_tr_t[tt] = b.cnt["sV"]

        @block.gpsimd
        def _(g):
            b.wtok(g, self._l1_minv_wr)
            self._l1_minvB = b.dma(
                g, minvB[0:64, :],
                dsc["minvd1"].ap().unsqueeze(1).broadcast_to([8, 8, T]), "G")

        @block.vector
        def _(v):
            b.wait(v, "sV")
            b.wtok(v, self._l1_minvB)
            v.drain()
            b.inc(v.tensor_tensor(out=gmT[0:64, :], in0=gmT[0:64, :],
                                  in1=minvB[0:64, :], op=OP.mult), "sV", 1)
        sT_ready = b.cnt["sV"]

        @block.gpsimd
        def _(g):
            g.wait_ge(b.sems["sV"], sT_ready)
            self._l1_sd_wr = b.dma(g, dsc["sd"].ap(), gmT[0:64, :], "G")

        oew = [b.scr_at(f"oew{i}", [128, 1024], BF16, "qkT", delta=2048 * i)
               for i in range(4)]
        sBt = [b.scr_at(f"sBt{i}", [128, 512], BF16, "qkT", delta=8192 + 1024 * i)
               for i in range(4)]
        Gt = [b.scr_at(f"Gt{i}", [128, 512], BF16, "qkT", delta=12288 + 1024 * i)
              for i in range(4)]
        oe_tok = {}
        sB_tok = {}
        g_bld_t = {}
        oe_mm_t = {}
        oe_ev_t = {}
        for qc in range(TC):
            for ch in range(32):
                e, hp = ch // 4, ch % 4
                ci = 32 * qc + ch
                rot = ci % 4

                @block.gpsimd
                def _(g, qc=qc, ch=ch, e=e, hp=hp, ci=ci, rot=rot):
                    if ci == 0:
                        b.wtok(g, self._l1_sd_wr)
                    if ci >= 4:
                        g.wait_ge(b.sems["sT"], oe_mm_t[ci - 4])
                    oe_tok[ci] = b.dma(
                        g, oew[rot][:],
                        dsc["oed"].ap()[512 * e + 128 * hp:512 * e + 128 * (hp + 1), :],
                        "G")
                    src = dsc["sd"].ap().rearrange("(h e2) t -> h e2 t", e2=8) \
                        [2 * hp:2 * hp + 2, e, 512 * qc:512 * (qc + 1)] \
                        .unsqueeze(1).broadcast_to([2, 64, 512])
                    sB_tok[ci] = b.dma(g, sBt[rot][:], src, "G")

                @block.vector
                def _(v, qc=qc, hp=hp, ci=ci, rot=rot):
                    b.wtok(v, sB_tok[ci])
                    if ci >= 4:
                        v.wait_ge(b.sems["sT"], oe_mm_t[ci - 4])
                    v.drain()
                    b.inc(v.tensor_tensor(
                        out=Gt[rot][:],
                        in0=araw[:, T * hp + 512 * qc:T * hp + 512 * (qc + 1)],
                        in1=sBt[rot][:], op=OP.mult), "sV", 1)
                    g_bld_t[ci] = b.cnt["sV"]

                @block.tensor
                def _(t, qc=qc, ch=ch, ci=ci, rot=rot):
                    b.wtok(t, oe_tok[ci])
                    t.wait_ge(b.sems["sV"], g_bld_t[ci])
                    if ch == 0 and qc > 0:
                        t.wait_ge(b.sems["sV"], oe_ev_t[qc - 1])
                    for dc in range(CC):
                        inst = t.matmul(pb[dc][:, :],
                                        oew[rot][:, 128 * dc:128 * (dc + 1)],
                                        Gt[rot][:], start=(ch == 0), stop=(ch == 31))
                    b.inc(inst, "sT", 1)
                    oe_mm_t[ci] = b.cnt["sT"]

            @block.vector
            def _(v, qc=qc):
                v.wait_ge(b.sems["sT"], oe_mm_t[32 * qc + 31])
                v.drain()
                for dc in range(CC):
                    inst = v.scalar_tensor_tensor(
                        out=X[:, T * dc + 512 * qc:T * dc + 512 * (qc + 1)],
                        in0=pb[dc][:, :], scalar=1.0,
                        in1=X[:, T * dc + 512 * qc:T * dc + 512 * (qc + 1)],
                        op0=OP.mult, op1=OP.add)
                b.inc(inst, "sV", 1)
                oe_ev_t[qc] = b.cnt["sV"]
        b.pop()

    def ph_l1att(self):
        b, block, din, dsc, pb, PBt = self.b, self.block, self.din, self.dsc, self.pb, self.PBt
        X, st = self.X, self.stat
        b.push()
        yt = b.scr("yt", [128, CC * T], F16)
        b.push()
        xbf = b.scr("xbf4", [128, CC * T], BF16)
        b.push()
        x2bf = b.scr("x2bf4", [128, CC * T], BF16)
        rb = b.scr("rb4", [128, T], F32)
        cb = b.scr("cb4", [128, T], F32)
        rbbf = b.scr("rbbf4", [128, T], BF16)
        cbbf = b.scr("cbbf4", [128, T], BF16)
        self.emit_xbf_copies(xbf, x2bf)
        self.emit_ln_stats(xbf, x2bf, rb, cb)
        self.emit_ln_bf16_apply(xbf, rb, cb, rbbf, cbbf)
        b.pop()
        xn = xbf
        xn_ready = b.cnt["sV"]

        araw = b.scr("araw2", [128, 8 * T], BF16)
        lbuf = b.scr("lbuf2", [128, 512], F32)
        b.push()
        qp = [b.scr(f"qp{i}", [128, T], BF16) for i in range(2)]
        kp = [b.scr(f"kp{i}", [128, T], BF16) for i in range(2)]
        vp = [b.scr(f"vp{i}", [128, NT * 128], BF16) for i in range(2)]
        ptt = [b.scr(f"pttA{i}", [128, 1024], BF16) for i in range(2)]
        qkvw = [b.scr(f"qkvw{i}", [128, 3 * 128], BF16) for i in range(8)]

        qkv_tok = {}
        qkv_mm_t = {}
        qkv_ev_t = {}
        s_mm_t = {}
        p_ev_t = {}
        a_mm_t = {}
        a_ev_t = {}
        l_dma_t = {}
        pa = PBt[:, 0:512]
        pl = PBt[:, 512:1024]
        for hp in range(CC):
            rot = hp % 2

            @block.gpsimd
            def _(g, hp=hp):
                if hp == 0:
                    b.wait(g, "sT")
                    b.wait(g, "sV")
                if hp >= 1:
                    g.wait_ge(b.sems["sT"], self._la_vmm_last[hp - 1])
                toks = []
                for c in range(CC):
                    for wi, wn in enumerate(("wq_t", "wk_t", "wv_t")):
                        toks.append(b.dma(
                            g, qkvw[c][:, 128 * wi:128 * (wi + 1)],
                            din[wn].ap()[128 * c:128 * (c + 1),
                                         128 * hp:128 * (hp + 1)], "G"))
                qkv_tok[hp] = toks

            qk_half_t = {}
            for which, wslice, bcol, scale2 in (("q", slice(0, 128), 0, 0.125),
                                                ("k", slice(128, 256), 1, None)):
                dstp = qp if which == "q" else kp
                for qh in range(2):
                    bks = [pb[6], pb[7]]

                    @block.tensor
                    def _(t, hp=hp, rot=rot, which=which, wslice=wslice, qh=qh, bks=bks):
                        if hp == 0 and which == "q" and qh == 0:
                            t.wait_ge(b.sems["sV"], xn_ready)
                        for tk in qkv_tok[hp]:
                            b.wtok(t, tk)
                        prev = qk_half_t.get((hp, which, qh - 1))
                        if prev is None and which == "k":
                            prev = qk_half_t.get((hp, "q", 1))
                        if prev is not None:
                            t.wait_ge(b.sems["sV"], prev)
                        elif hp >= 1:
                            t.wait_ge(b.sems["sV"], qkv_ev_t[(hp - 1, "v", 3)])
                        if hp >= 2:
                            t.wait_ge(b.sems["sT"], s_mm_t[(hp - 2, TC - 1, NT - 1)])
                        for c in range(CC):
                            lhsT = qkvw[c][:, wslice]
                            for j in range(2):
                                tc = 2 * qh + j
                                inst = t.matmul(bks[j][:, :], lhsT,
                                                xn[:, T * c + 512 * tc:T * c + 512 * (tc + 1)],
                                                start=(c == 0), stop=(c == CC - 1))
                        b.inc(inst, "sT", 1)
                        qkv_mm_t[(hp, which, qh)] = b.cnt["sT"]

                    @block.vector
                    def _(v, hp=hp, rot=rot, which=which, bcol=bcol, scale2=scale2,
                          qh=qh, bks=bks, dstp=dstp):
                        v.wait_ge(b.sems["sT"], qkv_mm_t[(hp, which, qh)])
                        bb = self.biases[:, bcol * CC + hp:bcol * CC + hp + 1]
                        for j in range(2):
                            tc = 2 * qh + j
                            v.drain()
                            if scale2 is None:
                                inst = v.tensor_scalar(
                                    out=dstp[rot][:, 512 * tc:512 * (tc + 1)],
                                    in0=bks[j][:, :], scalar1=bb, scalar2=None,
                                    op0=OP.add)
                            else:
                                inst = v.tensor_scalar(
                                    out=dstp[rot][:, 512 * tc:512 * (tc + 1)],
                                    in0=bks[j][:, :], scalar1=bb, scalar2=scale2,
                                    op0=OP.add, op1=OP.mult)
                        b.inc(inst, "sV", 1)
                        qk_half_t[(hp, which, qh)] = b.cnt["sV"]
                        qkv_ev_t[(hp, which)] = b.cnt["sV"]

            vv_mm = {}
            for tg in range(4):
                bkv = [pb[6 + (tg % 2)]]

                @block.tensor
                def _(t, hp=hp, rot=rot, tg=tg, bkv=bkv):
                    if tg == 0:
                        t.wait_ge(b.sems["sV"], qk_half_t[(hp, "k", 1)])
                    if tg == 1:
                        pass
                    if tg >= 2:
                        t.wait_ge(b.sems["sV"], qkv_ev_t[(hp, "v", tg - 2)])
                    if hp >= 2 and tg < 2:
                        t.wait_ge(b.sems["sT"], a_mm_t[(hp - 2, TC - 1, NT - 1)])
                    for tj in range(4):
                        tt = 4 * tg + tj
                        for c in range(CC):
                            inst = t.matmul(
                                bkv[0][:, 128 * tj:128 * (tj + 1)],
                                xn[:, T * c + 128 * tt:T * c + 128 * (tt + 1)],
                                qkvw[c][:, 256:384],
                                start=(c == 0), stop=(c == CC - 1))
                    b.inc(inst, "sT", 1)
                    vv_mm[tg] = b.cnt["sT"]

                @block.vector
                def _(v, hp=hp, rot=rot, tg=tg, bkv=bkv):
                    v.wait_ge(b.sems["sT"], vv_mm[tg])
                    if hp >= 2 and tg == 0:
                        t_l = a_mm_t[(hp - 2, TC - 1, NT - 1)]
                        v.wait_ge(b.sems["sT"], t_l)
                    v.drain()
                    inst = v.tensor_copy(out=vp[rot][:, 512 * tg:512 * (tg + 1)],
                                         in_=bkv[0][:, :])
                    v.drain()
                    b.inc(inst, "sV", 1)
                    qkv_ev_t[(hp, "v", tg)] = b.cnt["sV"]
            if not hasattr(self, "_la_vmm_last"):
                self._la_vmm_last = {}
            self._la_vmm_last[hp] = vv_mm[3]
            vready = b.cnt["sV"]

            for qc in range(TC):
                for jt in range(NT):
                    psS = PBt[:, 1024 + 1024 * (jt % 2):2048 + 1024 * (jt % 2)]

                    @block.tensor
                    def _(t, hp=hp, rot=rot, qc=qc, jt=jt, vready=vready, psS=psS):
                        if qc == 0 and jt == 0:
                            t.wait_ge(b.sems["sV"], vready)
                        if jt >= 2:
                            t.wait_ge(b.sems["sS"], p_ev_t[(hp, qc, jt - 2)])
                        if jt == 0:
                            prev = ((hp, qc - 1) if qc > 0 else
                                    ((hp - 1, TC - 1) if hp >= 1 else None))
                            if prev is not None and prev in a_ev_t:
                                t.wait_ge(b.sems["sV"], a_ev_t[prev])
                        for par in range(2):
                            rows = slice(64 * par, 64 * par + 64)
                            inst = t.matmul(
                                psS[:, 512 * par:512 * (par + 1)],
                                kp[rot][rows, 128 * jt:128 * (jt + 1)],
                                qp[rot][rows, 512 * qc:512 * (qc + 1)],
                                start=True, stop=True)
                        b.inc(inst, "sT", 1)
                        s_mm_t[(hp, qc, jt)] = b.cnt["sT"]

                    @block.scalar
                    def _(sc, hp=hp, qc=qc, jt=jt, psS=psS):
                        sc.wait_ge(b.sems["sT"], s_mm_t[(hp, qc, jt)])
                        if jt >= 2:
                            sc.wait_ge(b.sems["sT"], a_mm_t[(hp, qc, jt - 2)])
                        b.inc(sc.activation(out=ptt[jt % 2][:], in_=psS[:, :],
                                            func=AF.Exp, bias=self.consts[:, 5:6],
                                            scale=1.0), "sS", 1)
                        p_ev_t[(hp, qc, jt)] = b.cnt["sS"]

                    @block.tensor
                    def _(t, hp=hp, rot=rot, qc=qc, jt=jt, psS=psS):
                        t.wait_ge(b.sems["sS"], p_ev_t[(hp, qc, jt)])
                        for par in range(2):
                            pt = ptt[jt % 2][:, 512 * par:512 * (par + 1)]
                            vl = vp[rot][:, 128 * jt:128 * jt + 64] if False else \
                                vp[rot].rearrange("p (tt h) -> p tt h", tt=NT) \
                                [:, jt:jt + 1, :].squeeze(1)
                            t.matmul(pa[slice(64 * par, 64 * par + 64), :],
                                     vl[:, 64 * par:64 * (par + 1)] if False else
                                     vp[rot][:, 128 * jt + 64 * par:128 * jt + 64 * (par + 1)]
                                     if False else vp[rot][:, 128 * jt + 64 * 0:128 * jt + 64],
                                     pt, start=(jt == 0), stop=(jt == NT - 1))
                            inst = t.matmul(pl[32 * par:32 * par + 1, :],
                                            self.onecol_bf[:, 0:1], pt,
                                            start=(jt == 0), stop=(jt == NT - 1),
                                            tile_position=(0, 32 * par))
                        b.inc(inst, "sT", 1)
                        a_mm_t[(hp, qc, jt)] = b.cnt["sT"]

                @block.vector
                def _(v, hp=hp, qc=qc):
                    v.wait_ge(b.sems["sT"], a_mm_t[(hp, qc, NT - 1)])
                    if (hp, qc) != (0, 0):
                        prev = ((hp, qc - 1) if qc > 0 else (hp - 1, TC - 1))
                        b.wtok(v, l_dma_t[prev])
                    v.drain()
                    v.tensor_copy(out=araw[:, T * hp + 512 * qc:T * hp + 512 * (qc + 1)],
                                  in_=pa[:, :])
                    v.drain()
                    inst = v.tensor_copy(out=lbuf[:], in_=pl[:, :])
                    v.drain()
                    b.inc(inst, "sV", 1)
                    a_ev_t[(hp, qc)] = b.cnt["sV"]

                @block.gpsimd
                def _(g, hp=hp, qc=qc):
                    g.wait_ge(b.sems["sV"], a_ev_t[(hp, qc)])
                    l_dma_t[(hp, qc)] = b.dma(
                        g, dsc["lst"].ap()[:, 512 * (4 * hp + qc):512 * (4 * hp + qc + 1)],
                        lbuf[:], "G")
        b.pop()

        lsb2 = b.scr("lsb2A", [128, 256], F32)

        @block.gpsimd
        def _(g):
            for k in l_dma_t.values():
                b.wtok(g, k)
            src = dsc["lst"].ap().rearrange("(ra rb) c -> ra rb c", ra=4)[0:2, 0:1, :] \
                .squeeze(1)
            src = src.rearrange("par (hp qc q) -> par hp qc q", hp=8, qc=4)
            dst = dsc["ld16"].ap().rearrange("(hp par) (qc q) -> par hp qc q",
                                             hp=8, qc=4)
            tk = b.dma(g, dst, src, "G")
            b.wtok(g, tk)
            tk = b.dma(g, lsb2[:],
                       dsc["ld16"].ap().rearrange("h (qa qb) -> h qa qb", qa=8), "G")
            self._la_l_rd = tk

        @block.vector
        def _(v):
            b.wtok(v, self._la_l_rd)
            v.drain()
            b.inc(v.reciprocal(out=lsb2[:], in_=lsb2[:]), "sV", 1)
        minv_ready = b.cnt["sV"]

        @block.gpsimd
        def _(g):
            g.wait_ge(b.sems["sV"], minv_ready)
            self._la_minv_wr = b.dma(
                g, dsc["minvd"].ap().rearrange("h (qa qb) -> h qa qb", qa=8),
                lsb2[:], "G")

        gsc = [b.scr(f"gscA{i}", [128, T], BF16) for i in range(2)]
        gsc_tok = {}
        scale_t = {}
        for hp in range(CC):
            @block.gpsimd
            def _(g, hp=hp):
                if hp == 0:
                    b.wtok(g, self._la_minv_wr)
                if hp >= 2:
                    g.wait_ge(b.sems["sV"], scale_t[hp - 2])
                src = dsc["minvd"].ap()[2 * hp:2 * hp + 2, :]
                gsc_tok[hp] = b.dma(g, gsc[hp % 2][:],
                                    src.unsqueeze(1).broadcast_to([2, 64, T]), "G")

            @block.vector
            def _(v, hp=hp):
                b.wtok(v, gsc_tok[hp])
                v.drain()
                sl = slice(T * hp, T * (hp + 1))
                b.inc(v.tensor_tensor(out=araw[:, sl], in0=araw[:, sl],
                                      in1=gsc[hp % 2][:], op=OP.mult), "sV", 1)
                scale_t[hp] = b.cnt["sV"]

        wob = [b.scr(f"wobA{i}", [128, 1024], BF16) for i in range(CC)]
        bvb8 = b.scr("bvb8", [128, CC], BF16)

        @block.gpsimd
        def _(g):
            b.wait(g, "sT")
            toks = []
            for c in range(CC):
                toks.append(b.dma(g, wob[c][:],
                                  din["wo_t"].ap()[128 * c:128 * (c + 1), :], "G"))
            self._la_wo_tok = toks

        @block.vector
        def _(v):
            b.wait(v, "sV")
            v.drain()
            b.inc(v.tensor_copy(out=bvb8[:], in_=self.biases[:, 2 * CC:3 * CC]), "sV", 1)
        bv_ready = b.cnt["sV"]

        cv_t = {}

        @block.tensor
        def _(t):
            for tk in self._la_wo_tok:
                b.wtok(t, tk)
            t.wait_ge(b.sems["sV"], bv_ready)
            pc = PBt[:, 2048:2176]
            for dc in range(CC):
                for c in range(CC):
                    inst = t.matmul(pc[:, dc:dc + 1],
                                    wob[c][:, 128 * dc:128 * (dc + 1)],
                                    bvb8[:, c:c + 1],
                                    start=(c == 0), stop=(c == CC - 1))
            b.inc(inst, "sT", 1)
        cv_done = b.cnt["sT"]

        @block.vector
        def _(v):
            v.wait_ge(b.sems["sT"], cv_done)
            v.drain()
            v.tensor_copy(out=st[:, 32:40], in_=PBt[:, 2048:2056])
            v.drain()
            b.inc(v.tensor_tensor(out=st[:, 40:48], in0=st[:, 32:40],
                                  in1=self.biases[:, 3 * CC:4 * CC], op=OP.add), "sV", 1)
        boc_ready = b.cnt["sV"]

        wo_mm_t = {}
        wo_ev_t = {}
        for dc in range(CC):
            bks = [pb[4 * (dc % 2) + j] for j in range(4)]

            @block.tensor
            def _(t, dc=dc, bks=bks):
                if dc == 0:
                    t.wait_ge(b.sems["sV"], scale_t[CC - 1])
                    t.wait_ge(b.sems["sV"], boc_ready)
                if dc >= 2:
                    t.wait_ge(b.sems["sV"], wo_ev_t[dc - 2])
                for c in range(CC):
                    lhsT = wob[c][:, 128 * dc:128 * (dc + 1)]
                    for tc in range(TC):
                        inst = t.matmul(bks[tc][:, :], lhsT,
                                        araw[:, T * c + 512 * tc:T * c + 512 * (tc + 1)],
                                        start=(c == 0), stop=(c == CC - 1))
                b.inc(inst, "sT", 1)
                wo_mm_t[dc] = b.cnt["sT"]

            @block.vector
            def _(v, dc=dc, bks=bks):
                v.wait_ge(b.sems["sT"], wo_mm_t[dc])
                for tc in range(TC):
                    v.drain()
                    inst = v.scalar_tensor_tensor(
                        out=yt[:, T * dc + 512 * tc:T * dc + 512 * (tc + 1)],
                        in0=bks[tc][:, :], scalar=st[:, 40 + dc:41 + dc],
                        in1=xn[:, T * dc + 512 * tc:T * dc + 512 * (tc + 1)],
                        op0=OP.add, op1=OP.add)
                b.inc(inst, "sV", 1)
                wo_ev_t[dc] = b.cnt["sV"]
        y_done = b.cnt["sV"]
        b.pop()

        b.push()
        ybf = b.scr("ybf", [128, CC * T], BF16)
        y2bf = b.scr("y2bf", [128, CC * T], BF16)
        rb2 = b.scr("rb5", [128, T], F32)
        cb2 = b.scr("cb5", [128, T], F32)
        t1 = b.scr("t1", [128, T], F32)

        @block.vector
        def _(v):
            v.wait_ge(b.sems["sV"], y_done)
            v.drain()
            b.inc(v.tensor_copy(out=ybf[:], in_=yt[:]), "sV", 1)

        @block.scalar
        def _(s):
            b.wait(s, "sV")
            b.inc(s.activation(out=y2bf[:], in_=yt[:], func=AF.Square,
                               bias=self.consts[:, 0:1], scale=1.0), "sS", 1)
        self.emit_ln_stats(ybf, y2bf, rb2, cb2)

        @block.vector
        def _(v):
            b.wait(v, "sV")
            v.drain()
            v.reciprocal(out=st[:, 48:56], in_=self.biases[:, 4 * CC:5 * CC])
            v.drain()
            b.inc(v.tensor_tensor(out=st[:, 48:56], in0=st[:, 48:56],
                                  in1=self.biases[:, 5 * CC:6 * CC], op=OP.mult),
                  "sV", 1)

        xup_t = {}
        for dc in range(CC):
            @block.vector
            def _(v, dc=dc):
                b.wait(v, "sV")
                v.drain()
                v.tensor_tensor(out=t1[:], in0=yt[:, T * dc:T * (dc + 1)], in1=rb2[:],
                                op=OP.mult)
                v.drain()
                v.scalar_tensor_tensor(out=t1[:], in0=t1[:],
                                       scalar=st[:, 48 + dc:49 + dc],
                                       in1=cb2[:], op0=OP.add, op1=OP.subtract)
                v.drain()
                inst = v.scalar_tensor_tensor(
                    out=X[:, T * dc:T * (dc + 1)], in0=t1[:],
                    scalar=self.biases[:, 4 * CC + dc:4 * CC + dc + 1],
                    in1=X[:, T * dc:T * (dc + 1)], op0=OP.mult, op1=OP.add)
                v.drain()
                b.inc(inst, "sV", 1)
                xup_t[dc] = b.cnt["sV"]
        b.pop()
        b.pop()
    def ph_store(self):
        b, block = self.b, self.block
        self.stored = True

        @block.sync
        def _(q):
            b.wait(q, "sV")
            b.wait(q, "sT")
            b.wait(q, "sS")
            tok = b.dma(q, self.dout.ap().rearrange("(c p) t -> p c t", p=128),
                        self.X[:].rearrange("p (c t) -> p c t", c=CC), "S")
            b.wtok(q, tok)

    def ph_store_taps(self):
        b, block = self.b, self.block
        if not self.stored:
            self.ph_store()

        @block.sync
        def _(q):
            b.wait_allq(q, "S")

        @block.gpsimd
        def _(g):
            b.wait_allq(g, "G")


def kernel(**inputs):
    in_maps = _prep_inputs(inputs)
    nc = build_program()
    res = run_bass_kernel_spmd(nc, in_maps, core_ids=list(range(NCORES)))
    out = np.stack([np.ascontiguousarray(r["out_t"].T) for r in res.results])
    return out.astype(np.float32)
